# version 07
# baseline (speedup 1.0000x reference)
import numpy as np
from contextlib import ExitStack

import concourse.bass as bass
import concourse.mybir as mybir
from concourse.bass_utils import run_bass_kernel_spmd

F32 = mybir.dt.float32
BF16 = mybir.dt.bfloat16
AF = mybir.ActivationFunctionType
ALU = mybir.AluOpType
AX = mybir.AxisListType

D = 2048
HD = 128
SEG = 1024
NT = 8
GH = 8
EPS = 1e-5
QSC = 0.08838834764831845
RB = 192.0
RBF = 12582912.0
HID = 8192
MODW = 12288
MROW = MODW + 16
DEBUG = False
STOP_AFTER = None


class StopBuild(Exception):
    pass


_LAST = {}


def build_nc():
    nc = bass.Bass("TRN2", target_bir_lowering=False)
    _LAST["nc"] = nc
    di = lambda n_, s_, dt=F32: nc.dram_tensor(n_, s_, dt, kind="ExternalInput").ap()

    x_in = di("x_seg", [SEG, D])
    cbc = di("cbc", [128, 16])
    wmod_q = di("wmod_q", [D, 3072])
    wq_i, wk_i, wv_i, wg_i, wo_i = (di(n_, [D, D]) for n_ in ("wq", "wk", "wv", "wg", "wo"))
    wgk1_i = di("wgk1", [D, 16])
    wgk2_i = di("wgk2", [16, D])
    bgk2_i = di("bgk2c", [128, 16])
    gnw_i = di("gnw2", [1, 256])
    nw_i = di("nw_row", [1, D])
    wgvT_i = di("wgvT", [D, 2 * HID])
    woutT_i = di("woutT", [HID, D])
    wgvsh_i = di("wgv_sh", [D, D])
    woutsh_i = di("wout_sh", [256, HID])
    scanm_i = di("scanm", [128, SEG])
    umask_i = di("umask", [64, 64])
    ident_i = di("ident", [128, 128])
    smask_i = di("smask", [128, 8])
    bsel_i = di("bsel", [8, 8])
    y_out = nc.dram_tensor("y", [SEG, D], F32, kind="ExternalOutput").ap()

    cc1i = nc.dram_tensor("cc1i", [1, MROW], F32)
    cc1o = nc.dram_tensor("cc1o", [8, MROW], F32)
    cc2i = [nc.dram_tensor(f"cc2i{g}", [128, 1056], F32) for g in range(2)]
    cc2o = [nc.dram_tensor(f"cc2o{g}", [1024, 1056], F32) for g in range(2)]
    modscr = nc.dram_tensor("modscr", [1, MROW], F32)
    if DEBUG:
        dbg_h1 = nc.dram_tensor("dbg_h1", [128, 16 * SEG], BF16)
        dbg_og = nc.dram_tensor("dbg_og", [128, 16 * SEG], BF16)
        dbg_afm = nc.dram_tensor("dbg_afm", [128, 16 * SEG], BF16)
    x2scr = nc.dram_tensor("x2scr", [SEG, D], F32)
    midscr = nc.dram_tensor("midscr", [SEG, HID], BF16)

    x3 = x_in.rearrange("(t p) n -> t p n", p=128)
    y3 = y_out.rearrange("(t p) n -> t p n", p=128)
    x2s3 = x2scr.ap().rearrange("(t p) n -> t p n", p=128)
    mid3 = midscr.ap().rearrange("(t p) n -> t p n", p=128)

    es0 = ExitStack()
    sb = lambda n_, s_, dt: es0.enter_context(nc.sbuf_tensor(n_, s_, dt))
    pbank = [es0.enter_context(nc.psum_tensor(f"bk{i}", [128, 512], F32)) for i in range(8)]
    bk = lambda i: pbank[i].ap()
    bkb = lambda i: pbank[i].ap().bitcast(BF16)

    actq = sb("actq", [128, 16 * SEG], BF16)
    scanms = sb("scanms", [128, SEG], F32)
    umasks = sb("umasks", [64, 64], BF16)
    ident = sb("ident_s", [128, 128], BF16)
    smasks = sb("smasks", [128, 8], F32)
    bsels = sb("bsels", [8, 8], F32)
    bgk2s = sb("bgk2s", [128, 16], F32)
    cbs = sb("cbs", [128, 16], F32)
    gnwt = sb("gnwt", [128, 256], BF16)
    epsc = sb("epsc", [128, 1], F32)
    onesr = sb("onesr", [1, 128], F32)
    onescl = sb("onescl", [128, 1], F32)
    sbb16 = sb("sbb16", [128, 16], BF16)
    colb = sb("colb", [128, 32], F32)
    qcols = sb("qcols", [128, NT], F32)
    qcols2 = sb("qcols2", [128, NT], F32)
    icols2 = sb("icols2", [128, NT], F32)
    svcol = sb("svcol", [128, NT], F32)
    ambuf = sb("ambuf", [128, NT * 32], F32)
    scw = sb("scw", [128, 4], F32)
    gacol = sb("gacol", [128, 16], F32)
    rbneg = sb("rbneg", [128, 1], F32)

    espW = ExitStack()
    ogf = espW.enter_context(nc.sbuf_tensor("ogf", [128, 16 * SEG], BF16))
    wring = espW.enter_context(nc.sbuf_tensor("wring", [128, 2 * 4096], BF16))
    esp01 = ExitStack()
    cpA = esp01.enter_context(nc.sbuf_tensor("cpA", [128, 2 * D], BF16))

    sems = {k: es0.enter_context(nc.semaphore(k))
            for k in ["WL0", "WL1", "WF0", "WF1", "XL0", "XL1", "CN", "CS", "RNK",
                      "MD", "SP", "ST", "TT", "VV", "AA", "GG", "CC", "PB"]}
    n = {k: 0 for k in sems}

    with es0, nc.Block() as blk:
        T = lambda f: blk.tensor(f)
        V = lambda f: blk.vector(f)
        A = lambda f: blk.scalar(f)
        G = lambda f: blk.gpsimd(f)
        SY = lambda f: blk.sync(f)

        def inc(instr, nm, k=1):
            instr.then_inc(sems[nm], k)
            n[nm] += k
            return n[nm]

        def bar():
            dma_owned = {"gpsimd": ["WL0", "WL1", "WF0", "WF1", "CN"],
                         "sync": ["XL0", "XL1", "CS", "RNK", "MD", "ST"],
                         "scalar": ["SP"]}
            for attr in ("tensor", "vector", "scalar", "gpsimd", "sync"):
                def _f(e, attr=attr):
                    if attr in ("vector", "scalar", "gpsimd"):
                        e.drain()
                    for s_ in dma_owned.get(attr, []):
                        if n[s_]:
                            e.wait_ge(sems[s_], n[s_])
                    inc(e.nop(), "PB")
                getattr(blk, attr)(_f)
            tgt = n["PB"]
            for attr in ("tensor", "vector", "scalar", "gpsimd", "sync"):
                getattr(blk, attr)(lambda e, tgt=tgt: e.wait_ge(sems["PB"], tgt))

        pb_free = {}

        def ckstop(tag, zb=None):
            if STOP_AFTER is not None and tag == STOP_AFTER:
                def vz(v):
                    v.memset(zb[:, 0:1024], 0.0)
                    v.drain()
                    inc(v.nop(), "VV")
                V(vz)
                vzn = n["VV"]

                def fin(sy):
                    sy.wait_ge(sems["VV"], vzn)
                    for ti in range(NT):
                        inc(sy.dma_start(out=y3[ti][:, 0:1024], in_=zb[:, 0:1024]), "ST", 16)
                        inc(sy.dma_start(out=y3[ti][:, 1024:2048], in_=zb[:, 0:1024]), "ST", 16)
                    sy.wait_ge(sems["ST"], n["ST"])
                SY(fin)
                raise StopBuild()

        def tr_pipe(src_tile, dst_fn, nblk, bank0=6):
            vv0 = n["VV"]
            tt0 = n["TT"]

            def tf(t):
                for j in range(nblk):
                    if j >= 2:
                        t.wait_ge(sems["VV"], vv0 + (j - 2) + 1)
                    else:
                        bnk = bank0 + j % 2
                        if bnk in pb_free:
                            t.wait_ge(sems["VV"], pb_free[bnk])
                    tr = t.transpose(bkb(bank0 + j % 2)[:, 0:128],
                                     src_tile[:, j * 128:(j + 1) * 128], ident[:, :])
                    inc(tr, "TT")
            T(tf)

            def vf(v):
                for j in range(nblk):
                    v.wait_ge(sems["TT"], tt0 + j + 1)
                    cp = dst_fn(v, j, bkb(bank0 + j % 2)[:, 0:128])
                    cnt = inc(cp, "VV")
                    pb_free[bank0 + j % 2] = cnt
            V(vf)
            return n["VV"]

        esp0 = ExitStack()
        sb0 = lambda n_, s_, dt: esp0.enter_context(nc.sbuf_tensor(n_, s_, dt))
        modrow = sb0("modrow", [1, MROW], F32)
        g1s = sb0("g1s", [8, 3104], F32)
        xt0 = sb0("xt0", [128, 2 * D], F32)
        wsum = sb0("wsum", [128, 32], F32)
        wmslot = sb0("wmslot", [128, 2 * 3072], F32)

        def g0(g):
            for dst, src in [(scanms[:, :], scanm_i), (umasks[:, :], umask_i),
                             (ident[:, :], ident_i), (smasks[:, :], smask_i),
                             (bsels[:, :], bsel_i), (bgk2s[:, :], bgk2_i),
                             (cbs[:, :], cbc), (gnwt[0:1, :], gnw_i)]:
                inc(g.dma_start(out=dst, in_=src), "CN", 16)
        G(g0)
        cn_c = n["CN"]

        V(lambda v: (v.memset(epsc[:, :], EPS), v.memset(onesr[:, :], 1.0),
                     v.memset(onescl[:, :], 1.0), v.memset(rbneg[:, :], -RB),
                     v.drain(), inc(v.nop(), "VV")))

        def a0(a):
            a.wait_ge(sems["CN"], cn_c)
            a.activation(colb[:, 0:16], cbs[:, :], AF.Silu)
            a.drain()
            inc(a.nop(), "AA")
        A(a0)
        aa0 = n["AA"]
        V(lambda v: (v.wait_ge(sems["AA"], aa0), inc(v.nop(), "VV")))
        vv_scb = n["VV"]

        p0tt = {}
        wm = wmod_q.rearrange("(c p) j -> c p j", p=128)
        for c in range(16):
            sp_ = f"WL{c % 2}"

            def g1(g, c=c, sp_=sp_):
                if c >= 2:
                    g.wait_ge(sems["TT"], p0tt[c - 2])
                slot = wmslot[:, (c % 2) * 3072:(c % 2) * 3072 + 3072]
                inc(g.dma_start(out=slot, in_=wm[c]), sp_, 16)
            G(g1)
            wlc = n[sp_]

            def t0(t, c=c, sp_=sp_, wlc=wlc):
                if c == 0:
                    t.wait_ge(sems["VV"], vv_scb)
                t.wait_ge(sems[sp_], wlc)
                slot = wmslot[:, (c % 2) * 3072:(c % 2) * 3072 + 3072]
                for nn in range(6):
                    mm = t.matmul(bk(nn)[0:1, :], colb[:, c:c + 1],
                                  slot[:, nn * 512:(nn + 1) * 512],
                                  start=(c == 0), stop=(c == 15))
                p0tt[c] = inc(mm, "TT")
            T(t0)
        tt_mod = n["TT"]

        gv = wgvsh_i.rearrange("(c p) j -> c p j", p=128)
        wo4 = woutsh_i.rearrange("(c p) (d j) -> c d p j", p=128, j=D)
        wred_vv = {}
        for i in range(24):
            sp_ = f"XL{i % 2}"

            def sy0(sy, i=i, sp_=sp_):
                if i >= 2:
                    sy.wait_ge(sems["VV"], wred_vv[i - 2])
                src = gv[i] if i < 16 else wo4[(i - 16) // 4, (i - 16) % 4]
                inc(sy.dma_start(out=xt0[:, (i % 2) * D:(i % 2) * D + D], in_=src), sp_, 16)
            SY(sy0)
            xlc = n[sp_]

            def v0(v, i=i, sp_=sp_, xlc=xlc):
                v.wait_ge(sems[sp_], xlc)
                wred_vv[i] = inc(
                    v.tensor_reduce(wsum[:, i:i + 1], xt0[:, (i % 2) * D:(i % 2) * D + D],
                                    AX.X, ALU.add, apply_absolute_value=True), "VV")
            V(v0)

        def v0b(v):
            v.drain()
            v.tensor_reduce(wsum[:, 30:31], wsum[:, 0:16], AX.X, ALU.add)
            v.tensor_reduce(wsum[:, 31:32], wsum[:, 16:24], AX.X, ALU.add)
            v.drain()
            inc(v.nop(), "VV")
        V(v0b)
        vv_ws = n["VV"]

        def t1(t):
            t.wait_ge(sems["VV"], vv_ws)
            inc(t.matmul(bk(6)[0:1, 0:2], onescl[:, 0:1], wsum[:, 30:32],
                         start=True, stop=True), "TT")
        T(t1)
        tt_ws = n["TT"]

        def v1(v):
            v.memset(modrow[:, :], 0.0)
            v.drain()
            v.wait_ge(sems["TT"], tt_mod)
            for nn in range(6):
                v.tensor_copy(modrow[0:1, nn * 512:(nn + 1) * 512], bk(nn)[0:1, :])
            v.wait_ge(sems["TT"], tt_ws)
            v.tensor_copy(modrow[0:1, MODW:MODW + 2], bk(6)[0:1, 0:2])
            v.drain()
            inc(v.nop(), "VV")
        V(v1)
        vv_row = n["VV"]

        def g2(g):
            g.wait_ge(sems["VV"], vv_row)
            inc(g.dma_start(out=cc1i.ap(), in_=modrow[0:1, :]), "CN", 16)
            g.wait_ge(sems["CN"], n["CN"])
            inc(g.collective_compute(
                "AllGather", ALU.bypass, replica_groups=[list(range(8))],
                ins=[cc1i.ap().opt()], outs=[cc1o.ap().opt()]), "CC")
            g.wait_ge(sems["CC"], n["CC"])
            inc(g.dma_start(out=g1s[:, 0:3072], in_=cc1o.ap()[:, 0:3072]), "CN", 16)
            inc(g.dma_start(out=g1s[:, 3072:3074], in_=cc1o.ap()[:, MODW:MODW + 2]),
                "CN", 16)
        G(g2)
        wl_g1 = n["CN"]

        for q in range(4):
            def t2(t, q=q):
                if q == 0:
                    t.wait_ge(sems["CN"], wl_g1)
                else:
                    t.wait_ge(sems["VV"], n["VV"])
                for nn in range(6):
                    mm = t.matmul(bk(nn)[0:1, :], bsels[:, q:q + 1],
                                  g1s[:, nn * 512:(nn + 1) * 512], start=True, stop=True)
                inc(mm, "TT")
            T(t2)
            ttq = n["TT"]

            def v2(v, q=q, ttq=ttq):
                v.wait_ge(sems["TT"], ttq)
                for nn in range(6):
                    v.tensor_copy(modrow[0:1, q * 3072 + nn * 512:q * 3072 + (nn + 1) * 512],
                                  bk(nn)[0:1, :])
                v.drain()
                inc(v.nop(), "VV")
            V(v2)

        def t2b(t):
            t.wait_ge(sems["VV"], n["VV"])
            inc(t.matmul(bk(6)[0:1, 0:2], bsels[:, 4:5], g1s[:, 3072:3074],
                         start=True, stop=True), "TT")
        T(t2b)
        tt_tot = n["TT"]

        def v3(v):
            v.wait_ge(sems["TT"], tt_tot)
            v.tensor_copy(modrow[0:1, MODW:MODW + 2], bk(6)[0:1, 0:2])
            v.drain()
            v.tensor_scalar(modrow[0:1, MODW + 2:MODW + 3], modrow[0:1, MODW:MODW + 1],
                            1.0 / (2.0 * HID * D), 0.0, ALU.mult, ALU.add)
            v.tensor_scalar(modrow[0:1, MODW + 3:MODW + 4], modrow[0:1, MODW + 1:MODW + 2],
                            1.0 / (1.0 * HID * D), 0.0, ALU.mult, ALU.add)
            v.drain()
            for base in (2 * D, 5 * D):
                r0 = modrow[0:1, base:base + D]
                v.tensor_scalar(xt0[0:1, 0:D], r0, -1.0, 0.25, ALU.mult, ALU.max)
                v.drain()
                v.tensor_tensor(xt0[0:1, 0:D], r0, xt0[0:1, 0:D], ALU.max)
                v.tensor_scalar(xt0[0:1, D:2 * D], r0, 0.0, 2.0, ALU.is_ge, ALU.mult)
                v.drain()
                v.scalar_tensor_tensor(r0, xt0[0:1, D:2 * D], 1.0, xt0[0:1, 0:D],
                                       ALU.subtract, ALU.mult)
                v.drain()
            for base in (1 * D, 4 * D):
                v.tensor_scalar(modrow[0:1, base:base + D], modrow[0:1, base:base + D],
                                1.0, 0.0, ALU.add, ALU.add)
            v.drain()
            inc(v.nop(), "VV")
        V(v3)
        vv_mod = n["VV"]

        def sy1(a):
            a.wait_ge(sems["VV"], vv_mod)
            inc(a.dma_start(out=modscr.ap(), in_=modrow[0:1, :]), "SP", 16)
        A(sy1)
        sp_modscr = n["SP"]

        def bcast_row(row_ap, dst_tile, nparts=4):
            ttb = n["TT"]

            def tb(t):
                t.wait_ge(sems["VV"], n["VV"])
                for nn in range(nparts):
                    mm = t.matmul(bk(nn)[:, 0:512], onesr[0:1, :],
                                  row_ap[0:1, nn * 512:(nn + 1) * 512], start=True, stop=True)
                inc(mm, "TT")
            T(tb)
            ttb = n["TT"]

            def vb(v):
                v.wait_ge(sems["TT"], ttb)
                for nn in range(nparts):
                    v.tensor_copy(dst_tile[:, nn * 512:(nn + 1) * 512], bk(nn)[:, 0:512])
                v.drain()
                inc(v.nop(), "VV")
            V(vb)

        def tg(t):
            t.wait_ge(sems["VV"], vv_mod)
            inc(t.matmul(bk(7)[:, 0:2], onesr[0:1, :], modrow[0:1, MODW + 2:MODW + 4],
                         start=True, stop=True), "TT")
        T(tg)
        tt_s = n["TT"]

        def vg(v):
            v.wait_ge(sems["TT"], tt_s)
            v.tensor_copy(scw[:, 0:2], bk(7)[:, 0:2])
            v.drain()
            v.reciprocal(scw[:, 2:3], scw[:, 0:1])
            v.drain()
            v.reciprocal(scw[:, 3:4], scw[:, 1:2])
            v.drain()
            inc(v.nop(), "VV")
        V(vg)

        bcast_row(modrow[0:1, 0:D], cpA[:, 0:D])
        bcast_row(modrow[0:1, D:2 * D], cpA[:, D:2 * D])

        def sy2(sy):
            sy.wait_ge(sems["VV"], n["VV"])
            inc(sy.dma_start(out=xt0[0:1, 0:256], in_=gnw_i), "CS", 16)
        SY(sy2)
        xl_g = n["CS"]

        def tgn(t):
            t.wait_ge(sems["CS"], xl_g)
            t.wait_ge(sems["VV"], n["VV"])
            inc(t.matmul(bk(7)[:, 0:256], onesr[0:1, :], xt0[0:1, 0:256],
                         start=True, stop=True), "TT")
        T(tgn)
        tt_gn = n["TT"]

        def vgn(v):
            v.wait_ge(sems["TT"], tt_gn)
            v.tensor_copy(gnwt[:, :], bk(7)[:, 0:256])
            v.drain()
            inc(v.nop(), "VV")
        V(vgn)
        bar()
        ckstop("P0", xt0)
        esp0.close()

        esp1 = ExitStack()
        sb1 = lambda n_, s_, dt: esp1.enter_context(nc.sbuf_tensor(n_, s_, dt))
        xt = sb1("xt", [128, 2 * D], F32)
        lnb = sb1("lnb", [128, D], F32)
        lnb2 = sb1("lnb2", [128, D], F32)
        sqj = [sb1(f"sqj{i}", [128, D], F32) for i in range(2)]
        hstage = sb1("hstage", [128, D], BF16)

        sa_t, ca1_t = cpA[:, 0:D], cpA[:, D:2 * D]

        def ln_tile(ti, xv, xln, cmul_t, cadd_t, done_d, var_d):
            def a1(a, ti=ti, xv=xv, xln=xln):
                a.wait_ge(sems[xln[0]], xln[1])
                if ti >= 1:
                    a.wait_ge(sems["VV"], var_d[ti - 1])
                a.activation(sqj[ti % 2][:, :], xv, AF.Square, accum_out=colb[:, 2:3])
                a.drain()
                inc(a.nop(), "AA")
            A(a1)
            aan = n["AA"]

            def v5(v, xv=xv, xln=xln, aan=aan, ti=ti):
                v.wait_ge(sems[xln[0]], xln[1])
                v.tensor_reduce(colb[:, 0:1], xv, AX.X, ALU.add)
                v.drain()
                v.tensor_scalar(colb[:, 1:2], colb[:, 0:1], 1.0 / D, 0.0, ALU.mult, ALU.add)
                v.drain()
                v.tensor_tensor(colb[:, 4:5], colb[:, 1:2], colb[:, 1:2], ALU.mult)
                v.wait_ge(sems["AA"], aan)
                v.drain()
                v.scalar_tensor_tensor(colb[:, 5:6], colb[:, 2:3], 1.0 / D, colb[:, 4:5],
                                       ALU.mult, ALU.subtract)
                v.drain()
                var_d[ti] = inc(v.nop(), "VV")
            V(v5)
            vvn = n["VV"]

            def a2(a, vvn=vvn):
                a.wait_ge(sems["VV"], vvn)
                a.activation(colb[:, 6:7], colb[:, 5:6], AF.Ln, bias=epsc[:, 0:1])
                a.drain()
                a.activation(colb[:, 7:8], colb[:, 6:7], AF.Exp, scale=-0.5)
                a.drain()
                inc(a.nop(), "AA")
            A(a2)
            aan2 = n["AA"]

            def v6(v, xv=xv, aan2=aan2):
                v.wait_ge(sems["AA"], aan2)
                v.scalar_tensor_tensor(colb[:, 8:9], colb[:, 1:2], -1.0, colb[:, 7:8],
                                       ALU.mult, ALU.mult)
                v.drain()
                v.tensor_scalar(lnb[:, :], xv, colb[:, 7:8], colb[:, 8:9], ALU.mult, ALU.add)
                v.drain()
                v.tensor_tensor(lnb2[:, :], lnb[:, :], cmul_t, ALU.mult)
                v.drain()
                v.tensor_tensor(hstage[:, :], lnb2[:, :], cadd_t, ALU.add)
                v.drain()
                inc(v.nop(), "VV")
            V(v6)
            return n["VV"]

        p1_done, p1_var = {}, {}
        for ti in range(NT):
            xv = xt[:, (ti % 2) * D:(ti % 2) * D + D]

            sp_ = f"XL{ti % 2}"

            def syl(sy, ti=ti, xv=xv, sp_=sp_):
                if ti >= 2:
                    sy.wait_ge(sems["VV"], p1_done[ti - 2])
                inc(sy.dma_start(out=xv, in_=x3[ti]), sp_, 16)
            SY(syl)
            xln = (sp_, n[sp_])
            h1n = ln_tile(ti, xv, xln, ca1_t, sa_t, p1_done, p1_var)

            T(lambda t, h1n=h1n: t.wait_ge(sems["VV"], h1n))
            p1_done[ti] = tr_pipe(
                hstage,
                lambda v, j, ps, ti=ti: v.tensor_copy(
                    actq[:, j * SEG + ti * 128:j * SEG + (ti + 1) * 128], ps), 16)
        bar()
        ckstop("P1", xt)
        if DEBUG:
            def dmp1(sy):
                inc(sy.dma_start(out=dbg_h1.ap(), in_=actq[:, :]), "ST", 16)
                sy.wait_ge(sems["ST"], n["ST"])
            SY(dmp1)
            bar()
        esp1.close()
        esp01.close()

        espA = ExitStack()
        sbA = lambda n_, s_, dt: espA.enter_context(nc.sbuf_tensor(n_, s_, dt))
        qf = sbA("qf", [128, GH * SEG], BF16)
        kf = sbA("kf", [128, GH * SEG], BF16)
        Gf = sbA("Gf", [128, GH * SEG], BF16)
        vtm = sbA("vtm", [128, GH * SEG], BF16)
        sgtm = sbA("sgtm", [128, GH * SEG], BF16)
        EG = sbA("EG", [128, SEG], F32)
        EGi = sbA("EGi", [128, SEG], F32)
        tmpb = None
        hrow = sbA("hrow", [128, SEG], F32)
        hrow2 = sbA("hrow2", [128, SEG], F32)
        qhb = sbA("qhb", [128, SEG], BF16)
        khb = sbA("khb", [128, SEG], BF16)
        kwt = sbA("kwt", [128, 1280], BF16)
        Sst = sbA("Sst", [128, GH * HD], F32)
        Scur = sbA("Scur", [128, HD], F32)
        sgbuf = sbA("sgbuf", [128, HD], BF16)
        s2buf = sbA("s2buf", [128, HD], F32)
        otm = hrow2
        atb = sbA("atb", [128, 64], BF16)
        zbA = sbA("zbA", [128, 16], F32)
        zbB = sbA("zbB", [128, 16], F32)
        glc = sbA("glc", [128, 16], F32)
        ssq = sbA("ssq", [128, 8], F32)
        ctile = sbA("ctile", [128, 1056], F32)
        ogstage = sbA("ogstage", [128, SEG], BF16)
        kwb = ogstage
        tmpb = EG[:, :].bitcast(BF16)[0:16, 0:1024]
        state = {"tmpb_vv": 0, "ttog_prev": 0}
        V(lambda v: (v.memset(ctile[:, 1032:1056], 0.0), v.drain(), inc(v.nop(), "VV")))

        def proj(grp):
            co = grp * 1024
            if True:
                def gld(g):
                    wg1 = wgk1_i.rearrange("(c p) r -> p c r", p=128)
                    dst = kwt[:, 0:256].rearrange("p (c r) -> p c r", r=16)
                    inc(g.dma_start(out=dst, in_=wg1), "CN", 16)
                G(gld)
                wl0 = n["CN"]

                def t6(t):
                    t.wait_ge(sems["CN"], wl0)
                    for th in range(2):
                        for c in range(16):
                            mm = t.matmul(bk(6 + th)[0:16, :], kwt[:, c * 16:(c + 1) * 16],
                                          actq[:, c * SEG + th * 512:c * SEG + th * 512 + 512],
                                          start=(c == 0), stop=(c == 15))
                        inc(mm, "TT")
                T(t6)
                tt6 = n["TT"]

                def v8(v):
                    v.wait_ge(sems["TT"], tt6)
                    v.tensor_copy(tmpb[0:16, 0:512], bk(6)[0:16, :])
                    v.tensor_copy(tmpb[0:16, 512:1024], bk(7)[0:16, :])
                    v.drain()
                    state["tmpb_vv"] = inc(v.nop(), "VV")
                V(v8)

            def gld2(g, grp=grp):
                g.wait_ge(sems["VV"], state["tmpb_vv"])
                inc(g.dma_start(out=kwt[0:16, 256:1280], in_=wgk2_i[:, co:co + 1024]), "CN", 16)
            G(gld2)
            wlgk = n["CN"]
            wgk2b = kwt[0:16, 256:1280]

            gk_aa, gk_vv = {}, {}
            for hh in range(GH):
                def t7(t, hh=hh):
                    t.wait_ge(sems["CN"], wlgk)
                    if hh >= 1:
                        t.wait_ge(sems["VV"], gk_vv[hh - 1])
                    for th in range(2):
                        mm = t.matmul(bk(6 + th)[:, :], wgk2b[:, hh * 128:(hh + 1) * 128],
                                      tmpb[0:16, th * 512:(th + 1) * 512], start=True, stop=True)
                    inc(mm, "TT")
                T(t7)
                ttg = n["TT"]

                def a3(a, hh=hh, ttg=ttg, grp=grp):
                    a.wait_ge(sems["TT"], ttg)
                    if hh >= 1:
                        a.wait_ge(sems["VV"], gk_vv[hh - 1])
                    col = bgk2s[:, grp * 8 + hh:grp * 8 + hh + 1]
                    for th in range(2):
                        sl = hrow[:, th * 512:(th + 1) * 512]
                        a.activation(sl, bk(6 + th)[:, :], AF.Abs, bias=col)
                        a.drain()
                        a.activation(sl, sl, AF.Exp, scale=-1.0)
                        a.drain()
                        a.activation(sl, sl, AF.Ln, bias=1.0)
                        a.drain()
                    gk_aa[hh] = inc(a.nop(), "AA")
                A(a3)

                def v9(v, hh=hh, grp=grp):
                    v.wait_ge(sems["AA"], gk_aa[hh])
                    col = bgk2s[:, grp * 8 + hh:grp * 8 + hh + 1]
                    for th in range(2):
                        v.tensor_scalar(hrow2[:, th * 512:(th + 1) * 512], bk(6 + th)[:, :],
                                        col, 0.0, ALU.add, ALU.min)
                    v.drain()
                    v.tensor_tensor(hrow2[:, :], hrow2[:, :], hrow[:, :], ALU.subtract)
                    v.drain()
                    v.tensor_tensor_scan(Gf[:, hh * SEG:(hh + 1) * SEG], scanms[:, :],
                                         hrow2[:, :], 0.0, ALU.mult, ALU.add)
                    v.drain()
                    gk_vv[hh] = inc(v.nop(), "VV")
                V(v9)

            strip_tt = {}
            for mi, (mname, wsrc) in enumerate(
                    [("q", wq_i), ("k", wk_i), ("v", wv_i), ("g", wg_i)]):
                for s4 in range(4):
                    sidx = mi * 4 + s4
                    slot = wring[:, (sidx % 2) * 4096:(sidx % 2) * 4096 + 4096]
                    c0 = co + s4 * 256

                    spn = f"WL{sidx % 2}"

                    def gld3(g, wsrc=wsrc, slot=slot, c0=c0, sidx=sidx, spn=spn):
                        if sidx >= 2:
                            g.wait_ge(sems["TT"], strip_tt[sidx - 2])
                        src = wsrc.rearrange("(c p) j -> p c j", p=128)[:, :, c0:c0 + 256]
                        dst = slot.rearrange("p (c j) -> p c j", j=256)
                        for q4 in range(4):
                            inc(g.dma_start(out=dst[:, q4 * 4:(q4 + 1) * 4, :],
                                            in_=src[:, q4 * 4:(q4 + 1) * 4, :]), spn, 16)
                    G(gld3)
                    wln = (spn, n[spn])

                    if mname in ("q", "k"):
                        dstb = qf if mname == "q" else kf

                        def t8(t, slot=slot, wln=wln):
                            t.wait_ge(sems[wln[0]], wln[1])
                            for u in range(4):
                                m2, th = u // 2, u % 2
                                if u in pb_free:
                                    t.wait_ge(sems["VV"], pb_free[u])
                                for kc in range(16):
                                    mm = t.matmul(
                                        bk(u)[:, :],
                                        slot[:, kc * 256 + m2 * 128:kc * 256 + m2 * 128 + 128],
                                        actq[:, kc * SEG + th * 512:kc * SEG + th * 512 + 512],
                                        start=(kc == 0), stop=(kc == 15))
                                inc(mm, "TT")
                        T(t8)
                        ttn = n["TT"]
                        strip_tt[sidx] = ttn

                        def v10(v, s4=s4, ttn=ttn, dstb=dstb, mname=mname):
                            for u in range(4):
                                m2, th = u // 2, u % 2
                                v.wait_ge(sems["TT"], ttn - 4 + u + 1)
                                hcol = (s4 * 2 + m2) * SEG + th * 512
                                if mname == "q":
                                    cp = v.tensor_scalar(dstb[:, hcol:hcol + 512], bk(u)[:, :],
                                                         QSC, 0.0, ALU.mult, ALU.add)
                                else:
                                    cp = v.tensor_copy(dstb[:, hcol:hcol + 512], bk(u)[:, :])
                                pb_free[u] = inc(cp, "VV")
                        V(v10)
                    else:
                        vvb9 = n["VV"]
                        aab9 = n["AA"]
                        is_v = (mname == "v")

                        def t9(t, slot=slot, wln=wln, vvb9=vvb9, aab9=aab9, is_v=is_v):
                            t.wait_ge(sems[wln[0]], wln[1])
                            for ti in range(NT):
                                bnk = ti % 4
                                if ti < 4:
                                    if bnk in pb_free:
                                        t.wait_ge(sems["VV"], pb_free[bnk])
                                    if (bnk, "a") in pb_free:
                                        t.wait_ge(sems["AA"], pb_free[(bnk, "a")])
                                elif is_v:
                                    t.wait_ge(sems["VV"], vvb9 + (ti - 4) + 1)
                                else:
                                    t.wait_ge(sems["AA"], aab9 + (ti - 4) + 1)
                                for kc in range(16):
                                    mm = t.matmul(
                                        bk(bnk)[:, 0:256],
                                        actq[:, kc * SEG + ti * 128:kc * SEG + (ti + 1) * 128],
                                        slot[:, kc * 256:(kc + 1) * 256],
                                        start=(kc == 0), stop=(kc == 15))
                                inc(mm, "TT")
                        T(t9)
                        ttn = n["TT"]
                        strip_tt[sidx] = ttn

                        if mname == "v":
                            def v11(v, s4=s4, ttn=ttn):
                                for ti in range(NT):
                                    bnk = ti % 4
                                    v.wait_ge(sems["TT"], ttn - 8 + ti + 1)
                                    dst = vtm.rearrange("p (h t) -> p h t", t=SEG)[
                                        :, 2 * s4:2 * s4 + 2, ti * 128:(ti + 1) * 128]
                                    src = bk(bnk)[:, 0:256].rearrange("p (h d) -> p h d", d=128)
                                    pb_free[bnk] = inc(v.tensor_copy(dst, src), "VV")
                            V(v11)
                        else:
                            sg_aa, sg_vv = {}, {}
                            vvb4 = n["VV"]

                            def a4(a, ttn=ttn, vvb4=vvb4):
                                for ti in range(NT):
                                    bnk = ti % 4
                                    a.wait_ge(sems["TT"], ttn - 8 + ti + 1)
                                    if ti >= 2:
                                        a.wait_ge(sems["VV"], vvb4 + (ti - 2) + 1)
                                    else:
                                        a.wait_ge(sems["VV"], vvb4)
                                    a.activation(ogstage[:, (ti % 2) * 256:(ti % 2) * 256 + 256],
                                                 bk(bnk)[:, 0:256], AF.Silu)
                                    a.drain()
                                    cnt = inc(a.nop(), "AA")
                                    sg_aa[ti] = cnt
                                    pb_free[(bnk, "a")] = cnt
                            A(a4)

                            def v12(v, s4=s4):
                                for ti in range(NT):
                                    v.wait_ge(sems["AA"], sg_aa[ti])
                                    dst = sgtm.rearrange("p (h t) -> p h t", t=SEG)[
                                        :, 2 * s4:2 * s4 + 2, ti * 128:(ti + 1) * 128]
                                    src = ogstage[:, (ti % 2) * 256:(ti % 2) * 256 + 256].rearrange(
                                        "p (h d) -> p h d", d=128)
                                    gsrc = gnwt[:, :].rearrange("p (h d) -> p h d", d=128)
                                    sg_vv[ti] = inc(v.tensor_tensor(dst, src, gsrc, ALU.mult), "VV")
                            V(v12)
            bar()

        def sweep1(grp):
            ds_vv = {}
            for hh in range(GH):
                Gh = Gf[:, hh * SEG:(hh + 1) * SEG]
                g3 = Gh.rearrange("p (c t) -> p c t", t=64)

                def a5(a, Gh=Gh, hh=hh):
                    if hh >= 1:
                        a.wait_ge(sems["VV"], ds_vv[hh - 1])
                    a.activation(EGi[:, :], Gh, AF.Exp, scale=-1.0 / 16.0)
                    a.drain()
                    inc(a.nop(), "AA")
                A(a5)
                aan = n["AA"]

                def v13(v, g3=g3, aan=aan):
                    v.wait_ge(sems["AA"], aan)
                    v.tensor_copy(glc[:, 0:16], g3[:, :, 63])
                    v.drain()
                    v.tensor_tensor_scan(zbA[:, 0:16], scanms[:, 1:17], glc[:, 0:16],
                                         0.0, ALU.mult, ALU.add)
                    v.drain()
                    v.tensor_tensor(zbB[:, 0:16], glc[:, 0:16], zbA[:, 0:16], ALU.subtract)
                    v.drain()
                    v.tensor_scalar(zbB[:, 0:16], zbB[:, 0:16], zbA[:, 15:16], 0.0,
                                    ALU.add, ALU.add)
                    v.drain()
                    inc(v.nop(), "VV")
                V(v13)
                if hh == 0:
                    ckstop("s1v13", hrow)
                vv2 = n["VV"]

                def a6(a, vv2=vv2, hh=hh):
                    a.wait_ge(sems["VV"], vv2)
                    a.activation(zbB[:, 0:16], zbB[:, 0:16], AF.Exp, scale=1.0 / 16.0)
                    a.drain()
                    a.activation(ctile[:, 1024 + hh:1025 + hh], zbA[:, 15:16], AF.Exp,
                                 scale=1.0 / 16.0)
                    a.drain()
                    inc(a.nop(), "AA")
                A(a6)
                aan2 = n["AA"]

                def v14(v, hh=hh, aan2=aan2):
                    v.wait_ge(sems["AA"], aan2)
                    ez3 = zbB[:, 0:16].unsqueeze(2).broadcast_to([128, 16, 64])
                    egi3 = EGi[:, :].rearrange("p (c t) -> p c t", t=64)
                    h3 = hrow[:, :].rearrange("p (c t) -> p c t", t=64)
                    v.tensor_tensor(h3, egi3, ez3, ALU.mult)
                    v.drain()
                    v.tensor_tensor(kwb[:, :], kf[:, hh * SEG:(hh + 1) * SEG], hrow[:, :],
                                    ALU.mult)
                    v.drain()
                    inc(v.nop(), "VV")
                V(v14)
                if hh == 0:
                    ckstop("s1v14", hrow)
                kwn = n["VV"]

                T(lambda t, kwn=kwn: t.wait_ge(sems["VV"], kwn))
                vkw = tr_pipe(kwb, lambda v, j, ps: v.tensor_copy(
                    kwt[:, j * 128:(j + 1) * 128], ps), 8)
                if hh == 0:
                    ckstop("s1tr", hrow)

                def t11(t, hh=hh, vkw=vkw):
                    t.wait_ge(sems["VV"], vkw)
                    if hh >= 1:
                        t.wait_ge(sems["VV"], ds_vv[hh - 1])
                    for b2 in range(8):
                        lhsT = kwt[:, b2 * 128:(b2 + 1) * 128]
                        rhs = vtm.rearrange("p (h b t) -> p h b t", h=GH, t=128)[
                            :, hh, b2, :]
                        mm = t.matmul(bk(4)[:, 0:128], lhsT, rhs,
                                      start=(b2 == 0), stop=(b2 == 7))
                    inc(mm, "TT")
                T(t11)
                ttds = n["TT"]

                def v16(v, hh=hh, ttds=ttds):
                    v.wait_ge(sems["TT"], ttds)
                    ds_vv[hh] = inc(v.tensor_copy(ctile[:, hh * 128:(hh + 1) * 128],
                                                  bk(4)[:, 0:128]), "VV")
                V(v16)
                if hh == 0:
                    ckstop("s1mm", hrow)
            bar()

        def gather(grp):
            def g4(g, grp=grp):
                inc(g.dma_start(out=cc2i[grp].ap(), in_=ctile[:, :]), "CN", 16)
                g.wait_ge(sems["CN"], n["CN"])
                inc(g.collective_compute(
                    "AllGather", ALU.bypass, replica_groups=[list(range(8))],
                    ins=[cc2i[grp].ap().opt()], outs=[cc2o[grp].ap().opt()]), "CC")
            G(g4)
            ccn = n["CC"]

            V(lambda v: (v.memset(Sst[:, :], 0.0), v.drain(), inc(v.nop(), "VV")))
            ranks = cc2o[grp].ap().rearrange("(r p) n -> r p n", p=128)
            p4vv = {}
            for r in range(8):
                def syr(sy, r=r, ccn=ccn):
                    sy.wait_ge(sems["CC"], ccn)
                    if r >= 1:
                        sy.wait_ge(sems["VV"], p4vv[r - 1])
                    inc(sy.dma_start(out=ctile[:, :], in_=ranks[r]), "RNK", 16)
                SY(syr)
                xln = n["RNK"]

                def v17(v, r=r, xln=xln):
                    v.wait_ge(sems["RNK"], xln)
                    m = smasks[:, r:r + 1]
                    v.tensor_scalar(zbB[:, 0:1], m, -1.0, 1.0, ALU.mult, ALU.add)
                    v.drain()
                    v.tensor_scalar(zbA[:, 0:8], ctile[:, 1024:1032], m, zbB[:, 0:1],
                                    ALU.mult, ALU.add)
                    v.drain()
                    a3_ = zbA[:, 0:8].unsqueeze(2).broadcast_to([128, 8, 128])
                    s3_ = Sst[:, :].rearrange("p (h d) -> p h d", d=128)
                    v.tensor_tensor(s3_, s3_, a3_, ALU.mult)
                    v.drain()
                    v.scalar_tensor_tensor(Sst[:, :], ctile[:, 0:1024], m, Sst[:, :],
                                           ALU.mult, ALU.add)
                    v.drain()
                    p4vv[r] = inc(v.nop(), "VV")
                V(v17)
            bar()

        def sweep2(grp):
            for hh in range(GH):
                hg = grp * GH + hh
                Gh = Gf[:, hh * SEG:(hh + 1) * SEG]
                qh = qf[:, hh * SEG:(hh + 1) * SEG]
                kh = kf[:, hh * SEG:(hh + 1) * SEG]
                sgh = sgtm[:, hh * SEG:(hh + 1) * SEG]

                def a7(a, Gh=Gh):
                    a.wait_ge(sems["VV"], n["VV"])
                    a.wait_ge(sems["GG"], n["GG"])
                    a.activation(EG[:, :], Gh, AF.Exp, scale=1.0 / 16.0)
                    a.activation(EGi[:, :], Gh, AF.Exp, scale=-1.0 / 16.0)
                    a.drain()
                    inc(a.nop(), "AA")
                A(a7)
                aan = n["AA"]
                eg3 = EG[:, :].rearrange("p (c t) -> p c t", t=64)
                egi3 = EGi[:, :].rearrange("p (c t) -> p c t", t=64)

                def gp(g, aan=aan):
                    g.wait_ge(sems["AA"], aan)
                    egb = eg3[:, :, 0].unsqueeze(2).broadcast_to([128, 16, 64])
                    egbi = egi3[:, :, 0].unsqueeze(2).broadcast_to([128, 16, 64])
                    h3 = hrow[:, :].rearrange("p (c t) -> p c t", t=64)
                    h23 = hrow2[:, :].rearrange("p (c t) -> p c t", t=64)
                    g.tensor_tensor(h3, eg3, egbi, ALU.mult)
                    g.drain()
                    g.tensor_tensor(h23, egi3, egb, ALU.mult)
                    g.drain()
                    inc(g.nop(), "GG")
                G(gp)
                ggn = n["GG"]

                def v18(v, qh=qh, kh=kh, ggn=ggn):
                    v.wait_ge(sems["GG"], ggn)
                    v.tensor_tensor(qhb[:, :], qh, hrow[:, :], ALU.mult)
                    v.tensor_tensor(khb[:, :], kh, hrow2[:, :], ALU.mult)
                    v.drain()
                    inc(v.nop(), "VV")
                V(v18)
                qkn = n["VV"]

                def gp2(g, qkn=qkn):
                    g.wait_ge(sems["VV"], qkn)
                    egl = eg3[:, :, 63].unsqueeze(2).broadcast_to([128, 16, 64])
                    h3 = hrow[:, :].rearrange("p (c t) -> p c t", t=64)
                    g.tensor_tensor(h3, egi3, egl, ALU.mult)
                    g.drain()
                    inc(g.nop(), "GG")
                G(gp2)
                gg2 = n["GG"]

                def v19(v, kh=kh, gg2=gg2, hh=hh):
                    v.wait_ge(sems["GG"], gg2)
                    v.tensor_tensor(kwb[:, :], kh, hrow[:, :], ALU.mult)
                    v.drain()
                    v.tensor_copy(Scur[:, :], Sst[:, hh * HD:(hh + 1) * HD])
                    v.drain()
                    inc(v.nop(), "VV")
                V(v19)
                kwn = n["VV"]

                T(lambda t, kwn=kwn: t.wait_ge(sems["VV"], kwn))
                sready = tr_pipe(kwb, lambda v, j, ps: v.tensor_copy(
                    kwt[:, j * 128:(j + 1) * 128], ps), 8)

                ch_aa, ch_tt, ch_vv = {}, {}, {}
                for c in range(16):
                    p0 = (c % 2) * 64
                    b2 = c // 2
                    vslice = vtm.rearrange("p (h b t) -> p h b t", h=GH, t=128)[
                        p0:p0 + 64, hh, b2, :]
                    kwts = kwt[p0:p0 + 64, b2 * 128:(b2 + 1) * 128]

                    def a8(a, c=c, sready=sready):
                        if c == 0:
                            a.wait_ge(sems["VV"], sready)
                        else:
                            a.wait_ge(sems["VV"], ch_vv[(c - 1, "oev")])
                        a.activation(s2buf[:, :], Scur[:, :], AF.Copy,
                                     scale=eg3[:, c, 63].unsqueeze(1))
                        a.activation(sgbuf[:, :], Scur[:, :], AF.Copy,
                                     scale=eg3[:, c, 0].unsqueeze(1))
                        a.drain()
                        ch_aa[c] = inc(a.nop(), "AA")
                    A(a8)

                    def t13(t, c=c, sready=sready):
                        if c == 0:
                            t.wait_ge(sems["VV"], sready)
                        else:
                            t.wait_ge(sems["VV"], ch_vv[(c - 1, "oev")])
                        mm = t.matmul(bk(5)[0:64, 0:64], khb[:, c * 64:(c + 1) * 64],
                                      qhb[:, c * 64:(c + 1) * 64], start=True, stop=True)
                        ch_tt[(c, "at")] = inc(mm, "TT")
                    T(t13)

                    def v21(v, c=c, p0=p0):
                        v.wait_ge(sems["TT"], ch_tt[(c, "at")])
                        ch_vv[(c, "atm")] = inc(
                            v.tensor_tensor(atb[p0:p0 + 64, :], bk(5)[0:64, 0:64],
                                            umasks[:, :], ALU.mult), "VV")
                    V(v21)

                    def t14(t, c=c, vslice=vslice, kwts=kwts, p0=p0):
                        t.wait_ge(sems["VV"], ch_vv[(c, "atm")])
                        t.wait_ge(sems["AA"], ch_aa[c])
                        t.matmul(bk(6)[0:64, 0:128], qhb[:, c * 64:(c + 1) * 64],
                                 sgbuf[:, :], start=True, stop=True)
                        mm2 = t.matmul(bk(3)[0:64, 0:128], atb[p0:p0 + 64, :], vslice,
                                       start=True, stop=True)
                        ch_tt[(c, "o")] = inc(mm2, "TT")
                        mm3 = t.matmul(bk(2)[:, 0:128], kwts, vslice, start=True, stop=True)
                        ch_tt[(c, "sd")] = inc(mm3, "TT")
                    T(t14)

                    def a22(a, c=c, b2=b2):
                        a.wait_ge(sems["TT"], ch_tt[(c, "o")])
                        dst = otm[(c % 2) * 64:(c % 2) * 64 + 64, b2 * 128:(b2 + 1) * 128]
                        a.activation(dst, bk(3)[0:64, 0:128], AF.Copy)
                        a.drain()
                        ch_aa[(c, "oc")] = inc(a.nop(), "AA")
                    A(a22)

                    def v22(v, c=c, b2=b2):
                        v.wait_ge(sems["TT"], ch_tt[(c, "sd")])
                        v.wait_ge(sems["AA"], ch_aa[(c, "oc")])
                        dst = otm[(c % 2) * 64:(c % 2) * 64 + 64, b2 * 128:(b2 + 1) * 128]
                        ch_vv[c] = inc(v.tensor_tensor(Scur[:, :], s2buf[:, :],
                                                       bk(2)[:, 0:128], ALU.add), "VV")
                        ch_vv[(c, "oev")] = inc(v.tensor_tensor(dst, dst, bk(6)[0:64, 0:128],
                                                                ALU.add), "VV")
                    V(v22)

                def a9(a):
                    for b2 in range(8):
                        a.wait_ge(sems["VV"], ch_vv[(b2 * 2 + 1, "oev")])
                        a.activation(qhb[:, b2 * 128:(b2 + 1) * 128],
                                     otm[:, b2 * 128:(b2 + 1) * 128],
                                     AF.Square, accum_out=ssq[:, b2:b2 + 1])
                    a.drain()
                    a.activation(zbA[:, 0:8], ssq[:, 0:8], AF.Ln, scale=1.0 / 128.0,
                                 bias=epsc[:, 0:1])
                    a.drain()
                    a.activation(zbA[:, 0:8], zbA[:, 0:8], AF.Exp, scale=-0.5)
                    a.drain()
                    inc(a.nop(), "AA")
                A(a9)
                arst = n["AA"]

                def v23(v, sgh=sgh, arst=arst):
                    v.wait_ge(sems["AA"], arst)
                    v.wait_ge(sems["TT"], state["ttog_prev"])
                    for b2 in range(8):
                        v.tensor_scalar(hrow[:, b2 * 128:(b2 + 1) * 128],
                                        otm[:, b2 * 128:(b2 + 1) * 128],
                                        zbA[:, b2:b2 + 1], 0.0, ALU.mult, ALU.add)
                    v.drain()
                    v.tensor_tensor(ogstage[:, :], hrow[:, :], sgh, ALU.mult)
                    v.drain()
                    inc(v.nop(), "VV")
                V(v23)
                ogn = n["VV"]

                T(lambda t, ogn=ogn: t.wait_ge(sems["VV"], ogn))
                tr_pipe(ogstage, lambda v, j, ps, hg=hg: v.tensor_copy(
                    ogf[:, hg * SEG + j * 128:hg * SEG + (j + 1) * 128], ps), 8)
                state["ttog_prev"] = n["TT"]
            bar()

        for grp in range(2):
            proj(grp)
            ckstop(f"proj{grp}", hrow)
            sweep1(grp)
            ckstop(f"sw1_{grp}", hrow)
            gather(grp)
            ckstop(f"gth{grp}", hrow)
            sweep2(grp)
            ckstop(f"sw2_{grp}", hrow)
        espA.close()

        esp6 = ExitStack()
        sb6 = lambda n_, s_, dt: esp6.enter_context(nc.sbuf_tensor(n_, s_, dt))
        cpB = sb6("cpB", [128, 5 * D], BF16)
        xt6 = sb6("xt6", [128, 2 * D], F32)
        lnb = sb6("lnb6", [128, D], F32)
        lnb2 = sb6("lnb26", [128, D], F32)
        sqj = [sb6(f"sqj6{i}", [128, D], F32) for i in range(2)]
        hstage = sb6("hstage6", [128, D], BF16)
        hrow6 = sb6("hrow6", [128, D], F32)
        mrow6 = sb6("mrow6", [1, D], F32)
        qstage = sb6("qstage", [128, D], F32)
        grow = sb6("grow", [1, D], BF16)

        for sl, off in [(0, 2 * D), (1, 3 * D), (2, 4 * D), (3, 5 * D)]:
            def sy3(sy, sl=sl, off=off):
                if sl == 0:
                    sy.wait_ge(sems["SP"], sp_modscr)
                sy.wait_ge(sems["VV"], n["VV"])
                inc(sy.dma_start(out=mrow6[0:1, :], in_=modscr.ap()[0:1, off:off + D]),
                    "CS", 16)
            SY(sy3)
            xr = n["CS"]

            def t16(t, xr=xr):
                t.wait_ge(sems["CS"], xr)
                t.wait_ge(sems["VV"], n["VV"])
                for nn in range(4):
                    mm = t.matmul(bk(nn)[:, 0:512], onesr[0:1, :],
                                  mrow6[0:1, nn * 512:(nn + 1) * 512], start=True, stop=True)
                inc(mm, "TT")
            T(t16)
            ttb = n["TT"]

            def v25(v, sl=sl, ttb=ttb):
                v.wait_ge(sems["TT"], ttb)
                for nn in range(4):
                    v.tensor_copy(cpB[:, sl * D + nn * 512:sl * D + (nn + 1) * 512],
                                  bk(nn)[:, 0:512])
                v.drain()
                inc(v.nop(), "VV")
            V(v25)

        def sy4(sy):
            sy.wait_ge(sems["VV"], n["VV"])
            inc(sy.dma_start(out=mrow6[0:1, :], in_=nw_i), "CS", 16)
        SY(sy4)
        xr2 = n["CS"]

        def t17(t, xr2=xr2):
            t.wait_ge(sems["CS"], xr2)
            t.wait_ge(sems["VV"], n["VV"])
            for nn in range(4):
                mm = t.matmul(bk(nn)[:, 0:512], onesr[0:1, :],
                              mrow6[0:1, nn * 512:(nn + 1) * 512], start=True, stop=True)
            inc(mm, "TT")
        T(t17)
        ttnw = n["TT"]

        def v26(v, ttnw=ttnw):
            v.wait_ge(sems["TT"], ttnw)
            for nn in range(4):
                v.tensor_copy(cpB[:, 4 * D + nn * 512:4 * D + (nn + 1) * 512], bk(nn)[:, 0:512])
            v.drain()
            inc(v.nop(), "VV")
        V(v26)

        def g5(g):
            g.wait_ge(sems["VV"], n["VV"])
            inc(g.dma_start(out=grow[0:1, :], in_=modscr.ap()[0:1, 2 * D:3 * D]), "CN", 16)
        G(g5)
        wlga = n["CN"]

        def t18(t, wlga=wlga):
            t.wait_ge(sems["CN"], wlga)
            t.wait_ge(sems["VV"], n["VV"])
            for c in range(16):
                tr = t.transpose(bkb(5)[:, 2 * c:2 * c + 1], grow[0:1, c * 128:(c + 1) * 128],
                                 ident[0:1, 0:1])
            inc(tr, "TT")
        T(t18)
        ttga = n["TT"]

        def v27(v, ttga=ttga):
            v.wait_ge(sems["TT"], ttga)
            src = bkb(5)[:, 0:32].rearrange("p (c u) -> p c u", u=2)[:, :, 0]
            v.tensor_copy(gacol[:, 0:16], src)
            v.drain()
            inc(v.nop(), "VV")
        V(v27)
        ga_done = n["VV"]
        bar()

        afm = actq
        wo_tt = {}
        for s8 in range(8):
            slot = wring[:, (s8 % 2) * 4096:(s8 % 2) * 4096 + 4096]

            spn = f"WL{s8 % 2}"

            def g6(g, s8=s8, slot=slot, spn=spn):
                if s8 >= 2:
                    g.wait_ge(sems["TT"], wo_tt[s8 - 2])
                src = wo_i.rearrange("(c p) j -> p c j", p=128)[:, :, s8 * 256:(s8 + 1) * 256]
                dst = slot.rearrange("p (c j) -> p c j", j=256)
                for q4 in range(4):
                    inc(g.dma_start(out=dst[:, q4 * 4:(q4 + 1) * 4, :],
                                    in_=src[:, q4 * 4:(q4 + 1) * 4, :]), spn, 16)
            G(g6)
            wln = (spn, n[spn])

            def t19(t, slot=slot, wln=wln):
                t.wait_ge(sems[wln[0]], wln[1])
                t.wait_ge(sems["VV"], ga_done)
                for u in range(4):
                    m2, th = u // 2, u % 2
                    if u in pb_free:
                        t.wait_ge(sems["VV"], pb_free[u])
                    for kc in range(16):
                        mm = t.matmul(bk(u)[:, :],
                                      slot[:, kc * 256 + m2 * 128:kc * 256 + m2 * 128 + 128],
                                      ogf[:, kc * SEG + th * 512:kc * SEG + th * 512 + 512],
                                      start=(kc == 0), stop=(kc == 15))
                    inc(mm, "TT")
            T(t19)
            ttn = n["TT"]
            wo_tt[s8] = ttn

            def v28(v, s8=s8, ttn=ttn):
                for u in range(4):
                    m2, th = u // 2, u % 2
                    v.wait_ge(sems["TT"], ttn - 4 + u + 1)
                    oc = s8 * 2 + m2
                    cp = v.tensor_scalar(afm[:, oc * SEG + th * 512:oc * SEG + th * 512 + 512],
                                         bk(u)[:, :], gacol[:, oc:oc + 1], 0.0,
                                         ALU.mult, ALU.add)
                    pb_free[u] = inc(cp, "VV")
            V(v28)
        bar()
        ckstop("wo", lnb)
        if DEBUG:
            def dmp2(sy):
                inc(sy.dma_start(out=dbg_og.ap(), in_=ogf[:, :]), "ST", 16)
                inc(sy.dma_start(out=dbg_afm.ap(), in_=afm[:, :]), "ST", 16)
                sy.wait_ge(sems["ST"], n["ST"])
            SY(dmp2)
            bar()

        sb_t = cpB[:, 1 * D:2 * D]
        cb1_t = cpB[:, 2 * D:3 * D]
        gb_t = cpB[:, 3 * D:4 * D]
        nw_t = cpB[:, 4 * D:5 * D]
        p6_done, p6_var, p6_sp = {}, {}, {}
        for ti in range(NT):
            xv = xt6[:, (ti % 2) * D:(ti % 2) * D + D]

            spn = f"XL{ti % 2}"

            def sy5(sy, ti=ti, xv=xv, spn=spn):
                if ti >= 2:
                    sy.wait_ge(sems["VV"], p6_done[ti - 2])
                inc(sy.dma_start(out=xv, in_=x3[ti]), spn, 16)
            SY(sy5)
            xln = (spn, n[spn])

            vv0 = n["VV"]
            tt0b = n["TT"]

            def t20(t, ti=ti, vv0=vv0):
                for j in range(16):
                    if j >= 2:
                        t.wait_ge(sems["VV"], vv0 + (j - 2) + 1)
                    else:
                        bnk = 6 + j % 2
                        if bnk in pb_free:
                            t.wait_ge(sems["VV"], pb_free[bnk])
                    tr = t.transpose(bkb(6 + j % 2)[:, 0:128],
                                     afm[:, j * SEG + ti * 128:j * SEG + (ti + 1) * 128],
                                     ident[:, :])
                    inc(tr, "TT")
            T(t20)

            def v29(v, ti=ti, xln=xln, tt0b=tt0b):
                v.wait_ge(sems[xln[0]], xln[1])
                if ti >= 1:
                    v.wait_ge(sems["SP"], p6_sp[ti - 1])
                for j in range(16):
                    v.wait_ge(sems["TT"], tt0b + j + 1)
                    xs_ = xt6[:, (ti % 2) * D + j * 128:(ti % 2) * D + (j + 1) * 128]
                    cnt = inc(v.tensor_tensor(lnb[:, j * 128:(j + 1) * 128],
                                              bkb(6 + j % 2)[:, 0:128], xs_, ALU.add), "VV")
                    pb_free[6 + j % 2] = cnt
                v.drain()
                inc(v.nop(), "VV")
            V(v29)
            x2n = n["VV"]

            def a10(a, ti=ti, x2n=x2n):
                a.wait_ge(sems["VV"], x2n)
                if ti >= 1:
                    a.wait_ge(sems["VV"], p6_var[ti - 1])
                a.activation(sqj[ti % 2][:, :], lnb[:, :], AF.Square, accum_out=colb[:, 2:3])
                a.drain()
                inc(a.nop(), "AA")
            A(a10)
            aan = n["AA"]

            def v30(v, x2n=x2n, aan=aan, ti=ti):
                v.wait_ge(sems["VV"], x2n)
                v.tensor_reduce(colb[:, 0:1], lnb[:, :], AX.X, ALU.add)
                v.drain()
                v.tensor_scalar(colb[:, 1:2], colb[:, 0:1], 1.0 / D, 0.0, ALU.mult, ALU.add)
                v.drain()
                v.tensor_tensor(colb[:, 4:5], colb[:, 1:2], colb[:, 1:2], ALU.mult)
                v.wait_ge(sems["AA"], aan)
                v.drain()
                v.scalar_tensor_tensor(colb[:, 5:6], colb[:, 2:3], 1.0 / D, colb[:, 4:5],
                                       ALU.mult, ALU.subtract)
                v.drain()
                p6_var[ti] = inc(v.nop(), "VV")
            V(v30)
            vvn = n["VV"]

            def a11(a, vvn=vvn):
                a.wait_ge(sems["VV"], vvn)
                a.activation(colb[:, 6:7], colb[:, 5:6], AF.Ln, bias=epsc[:, 0:1])
                a.drain()
                a.activation(colb[:, 7:8], colb[:, 6:7], AF.Exp, scale=-0.5)
                a.drain()
                inc(a.nop(), "AA")
            A(a11)
            aan2 = n["AA"]

            def v31(v, aan2=aan2):
                v.wait_ge(sems["AA"], aan2)
                v.scalar_tensor_tensor(colb[:, 8:9], colb[:, 1:2], -1.0, colb[:, 7:8],
                                       ALU.mult, ALU.mult)
                v.drain()
                v.tensor_scalar(lnb2[:, :], lnb[:, :], colb[:, 7:8], colb[:, 8:9],
                                ALU.mult, ALU.add)
                v.drain()
                v.tensor_tensor(lnb2[:, :], lnb2[:, :], cb1_t, ALU.mult)
                v.drain()
                v.tensor_tensor(lnb2[:, :], lnb2[:, :], sb_t, ALU.add)
                v.drain()
                inc(v.nop(), "VV")
            V(v31)
            h2n = n["VV"]

            def g7(g, h2n=h2n, ti=ti):
                g.wait_ge(sems["VV"], h2n)
                g.tensor_tensor(sqj[ti % 2][:, :], lnb2[:, :], gb_t, ALU.mult)
                g.drain()
                inc(g.nop(), "GG")
            G(g7)
            ggn = n["GG"]

            def v32(v, ti=ti, ggn=ggn):
                v.wait_ge(sems["GG"], ggn)
                v.tensor_tensor(lnb[:, :], lnb[:, :], sqj[ti % 2][:, :], ALU.add)
                v.drain()
                inc(v.nop(), "VV")
            V(v32)
            x2pn = n["VV"]

            def a12(a, ti=ti, x2pn=x2pn):
                a.wait_ge(sems["VV"], x2pn)
                p6_sp[ti] = inc(a.dma_start(out=x2s3[ti], in_=lnb[:, :]), "SP", 16)
                a.activation(sqj[ti % 2][:, :], lnb2[:, :], AF.Square,
                             accum_out=colb[:, 10:11])
                a.drain()
                inc(a.nop(), "AA")
            A(a12)
            aan3 = n["AA"]

            def v33(v, aan3=aan3):
                v.tensor_tensor(hrow6[:, :], lnb2[:, :], nw_t, ALU.mult)
                v.drain()
                v.tensor_reduce(colb[:, 12:13], hrow6[:, :], AX.X, ALU.max,
                                apply_absolute_value=True)
                v.wait_ge(sems["AA"], aan3)
                v.drain()
                inc(v.nop(), "VV")
            V(v33)
            vvz = n["VV"]

            def a13(a, vvz=vvz):
                a.wait_ge(sems["VV"], vvz)
                a.activation(colb[:, 13:14], colb[:, 10:11], AF.Ln, scale=1.0 / D,
                             bias=epsc[:, 0:1])
                a.drain()
                a.activation(colb[:, 13:14], colb[:, 13:14], AF.Exp, scale=-0.5)
                a.drain()
                inc(a.nop(), "AA")
            A(a13)
            aan4 = n["AA"]

            def v34(v, ti=ti, aan4=aan4):
                v.wait_ge(sems["AA"], aan4)
                v.tensor_tensor(colb[:, 14:15], colb[:, 12:13], colb[:, 13:14], ALU.mult)
                v.drain()
                v.tensor_scalar(colb[:, 14:15], colb[:, 14:15], 1e-5, 0.0, ALU.max, ALU.add)
                v.drain()
                v.reciprocal(colb[:, 15:16], colb[:, 14:15])
                v.drain()
                v.tensor_scalar(colb[:, 16:17], colb[:, 15:16], 127.0, 0.0, ALU.mult, ALU.add)
                v.tensor_scalar(qcols[:, ti:ti + 1], colb[:, 14:15], 1.0 / 127.0, 0.0,
                                ALU.mult, ALU.add)
                v.drain()
                v.tensor_tensor(colb[:, 17:18], colb[:, 13:14], colb[:, 16:17], ALU.mult)
                v.drain()
                v.tensor_scalar(qstage[:, :], hrow6[:, :], colb[:, 17:18], RBF,
                                ALU.mult, ALU.add)
                v.drain()
                inc(v.nop(), "VV")
            V(v34)
            vq = n["VV"]

            def g8(g, vq=vq):
                g.wait_ge(sems["VV"], vq)
                g.tensor_scalar(qstage[:, :], qstage[:, :], RBF + 127.0, RBF - 127.0,
                                ALU.min, ALU.max)
                g.drain()
                inc(g.nop(), "GG")
            G(g8)
            gq = n["GG"]

            def v35(v, gq=gq):
                v.wait_ge(sems["GG"], gq)
                v.tensor_scalar(hstage[:, :], qstage[:, :], RBF, 0.0, ALU.subtract, ALU.add)
                v.drain()
                inc(v.nop(), "VV")
            V(v35)
            hqn = n["VV"]

            T(lambda t, hqn=hqn: t.wait_ge(sems["VV"], hqn))
            p6_done[ti] = tr_pipe(
                hstage,
                lambda v, j, ps, ti=ti: v.tensor_copy(
                    actq[:, j * SEG + ti * 128:j * SEG + (ti + 1) * 128], ps), 16)
        bar()
        ckstop("P6", lnb)
        esp6.close()
        espW.close()

        esp7 = ExitStack()
        sb7 = lambda n_, s_, dt: esp7.enter_context(nc.sbuf_tensor(n_, s_, dt))
        wffn = sb7("wffn", [128, 16384], BF16)
        acc = sb7("acc", [128, 16 * SEG], F32)
        midst = sb7("midst", [128, 1024], BF16)
        qmidt = sb7("qmidt", [128, 1024], BF16)
        qmidf = sb7("qmidf", [128, 1024], BF16)
        qmstage = sb7("qmstage", [128, 1024], F32)
        sgst = sb7("sgst", [128, 256], BF16)
        valst = sb7("valst", [128, 256], F32)
        xf = sb7("xf", [128, D], F32)
        mrow7 = sb7("mrow7", [1, D], F32)
        gbt = sb7("gbt", [128, D], BF16)

        def sy9(sy):
            inc(sy.dma_start(out=mrow7[0:1, :], in_=modscr.ap()[0:1, 5 * D:6 * D]),
                "CS", 16)
        SY(sy9)
        xgb = n["CS"]

        def t26(t, xgb=xgb):
            t.wait_ge(sems["CS"], xgb)
            for nn in range(4):
                mm = t.matmul(bk(4 + nn)[:, 0:512], onesr[0:1, :],
                              mrow7[0:1, nn * 512:(nn + 1) * 512], start=True, stop=True)
                inc(mm, "TT")
        T(t26)
        ttgb = n["TT"]

        def vgb(v, ttgb=ttgb):
            for nn in range(4):
                v.wait_ge(sems["TT"], ttgb - 4 + nn + 1)
                v.tensor_copy(gbt[:, nn * 512:(nn + 1) * 512], bk(4 + nn)[:, 0:512])
            v.drain()
            inc(v.nop(), "VV")
        V(vgb)

        def v37(v):
            for ti in range(NT):
                v.tensor_scalar(svcol[:, ti:ti + 1], qcols[:, ti:ti + 1], scw[:, 0:1],
                                0.0, ALU.mult, ALU.add)
            v.memset(acc[:, :], 0.0)
            v.drain()
            inc(v.nop(), "VV")
        V(v37)
        bar()

        gv_tt = {}
        last_sp = {}
        for B in range(32):
            wslot = wffn[:, (B % 2) * 8192:(B % 2) * 8192 + 8192]
            ws3 = wslot.rearrange("p (c j) -> p c j", j=512)

            spn = f"WF{B % 2}"

            def g9(g, B=B, ws3=ws3, spn=spn):
                if B >= 2:
                    g.wait_ge(sems["TT"], gv_tt[B - 2])
                srcv = wgvT_i.rearrange("(c p) j -> p c j", p=128)[:, :, B * 256:(B + 1) * 256]
                srcg = wgvT_i.rearrange("(c p) j -> p c j", p=128)[
                    :, :, HID + B * 256:HID + (B + 1) * 256]
                for q4 in range(4):
                    sl = slice(q4 * 4, (q4 + 1) * 4)
                    inc(g.dma_start(out=ws3[:, sl, 0:256], in_=srcv[:, sl, :]), spn, 16)
                    inc(g.dma_start(out=ws3[:, sl, 256:512], in_=srcg[:, sl, :]), spn, 16)
            G(g9)
            wln = (spn, n[spn])

            def wq1(v, ws3=ws3, wln=wln):
                v.wait_ge(sems[wln[0]], wln[1])
                v.tensor_scalar(ws3[:, :, :], ws3[:, :, :], scw[:, 2:3], RB,
                                ALU.mult, ALU.add)
                v.drain()
                inc(v.nop(), "VV")
            V(wq1)
            q1n = n["VV"]

            def wq2(g, ws3=ws3, q1n=q1n):
                g.wait_ge(sems["VV"], q1n)
                g.tensor_scalar(ws3[:, :, :], ws3[:, :, :], RB + 1.0, RB - 1.0,
                                ALU.min, ALU.max)
                g.drain()
                inc(g.nop(), "GG")
            G(wq2)
            q2n = n["GG"]

            def wq3(v, ws3=ws3, q2n=q2n):
                v.wait_ge(sems["GG"], q2n)
                v.tensor_scalar(ws3[:, :, :], ws3[:, :, :], RB, 0.0, ALU.subtract, ALU.add)
                v.drain()
                inc(v.nop(), "VV")
            V(wq3)
            wqn = n["VV"]

            for ti in range(NT):
                bv, bg = ti % 2, 2 + ti % 2

                def t22(t, ti=ti, bv=bv, bg=bg, ws3=ws3, wln=wln, wqn=wqn):
                    t.wait_ge(sems["VV"], wqn)
                    for bnk in (bv, bg):
                        if bnk in pb_free:
                            t.wait_ge(sems["VV"], pb_free[bnk])
                        if (bnk, "a") in pb_free:
                            t.wait_ge(sems["AA"], pb_free[(bnk, "a")])
                    for kc in range(16):
                        t.matmul(bk(bv)[:, 0:256],
                                 actq[:, kc * SEG + ti * 128:kc * SEG + (ti + 1) * 128],
                                 ws3[:, kc, 0:256], start=(kc == 0), stop=(kc == 15))
                    for kc in range(16):
                        mm = t.matmul(bk(bg)[:, 0:256],
                                      actq[:, kc * SEG + ti * 128:kc * SEG + (ti + 1) * 128],
                                      ws3[:, kc, 256:512], start=(kc == 0), stop=(kc == 15))
                    inc(mm, "TT")
                T(t22)
                ttn = n["TT"]
                gv_tt[B] = ttn

                def a14(a, ti=ti, bv=bv, bg=bg, ttn=ttn):
                    a.wait_ge(sems["TT"], ttn)
                    a.activation(valst[:, :], bk(bv)[:, 0:256], AF.Copy,
                                 scale=svcol[:, ti:ti + 1])
                    a.activation(sgst[:, :], bk(bg)[:, 0:256], AF.Silu,
                                 scale=svcol[:, ti:ti + 1])
                    a.drain()
                    cnt = inc(a.nop(), "AA")
                    pb_free[(bv, "a")] = cnt
                    pb_free[(bg, "a")] = cnt
                A(a14)
                aan = n["AA"]

                def v38(v, ti=ti, B=B, aan=aan, bv=bv, bg=bg):
                    v.wait_ge(sems["AA"], aan)
                    if last_sp:
                        v.wait_ge(sems["SP"], last_sp["c"])
                    v.tensor_tensor(midst[:, 0:256], valst[:, :], sgst[:, :], ALU.mult)
                    v.drain()
                    r = v.tensor_reduce(ambuf[:, ti * 32 + B:ti * 32 + B + 1],
                                        midst[:, 0:256], AX.X, ALU.max,
                                        apply_absolute_value=True)
                    cnt = inc(r, "VV")
                    pb_free[bv] = cnt
                    pb_free[bg] = cnt
                V(v38)
                vvm = n["VV"]

                def a15(a, ti=ti, B=B, vvm=vvm):
                    a.wait_ge(sems["VV"], vvm)
                    last_sp["c"] = inc(a.dma_start(out=mid3[ti][:, B * 256:(B + 1) * 256],
                                                   in_=midst[:, 0:256]), "SP", 16)
                A(a15)
        sp_mid = n["SP"]
        bar()
        ckstop("gv", acc)

        def v39(v):
            for ti in range(NT):
                v.tensor_reduce(colb[:, 20:21], ambuf[:, ti * 32:(ti + 1) * 32], AX.X, ALU.max)
                v.drain()
                v.tensor_scalar(colb[:, 20:21], colb[:, 20:21], 1e-5, 0.0, ALU.max, ALU.add)
                v.drain()
                v.reciprocal(colb[:, 21:22], colb[:, 20:21])
                v.drain()
                v.tensor_scalar(qcols2[:, ti:ti + 1], colb[:, 21:22], 127.0, 0.0,
                                ALU.mult, ALU.add)
                v.tensor_scalar(icols2[:, ti:ti + 1], colb[:, 20:21], 1.0 / 127.0, 0.0,
                                ALU.mult, ALU.add)
                v.drain()
                v.tensor_scalar(icols2[:, ti:ti + 1], icols2[:, ti:ti + 1], scw[:, 1:2],
                                0.0, ALU.mult, ALU.add)
                v.drain()
            inc(v.nop(), "VV")
        V(v39)
        bar()

        wo2_tt = {}
        for SB in range(8):
            ws3 = wffn[:, 0:16384].rearrange("p (c j) -> p c j", j=2048)

            def g10(g, SB=SB, ws3=ws3):
                if SB >= 1:
                    g.wait_ge(sems["TT"], wo2_tt[SB - 1])
                src = woutT_i.rearrange("(c p) j -> p c j", p=128)[:, SB * 8:(SB + 1) * 8, :]
                inc(g.dma_start(out=ws3[:, 0:4, :], in_=src[:, 0:4, :]), "WF0", 16)
                inc(g.dma_start(out=ws3[:, 4:8, :], in_=src[:, 4:8, :]), "WF0", 16)
            G(g10)
            wln = n["WF0"]

            def wq1b(v, ws3=ws3, wln=wln):
                v.wait_ge(sems["WF0"], wln)
                v.tensor_scalar(ws3[:, :, :], ws3[:, :, :], scw[:, 3:4], RB,
                                ALU.mult, ALU.add)
                v.drain()
                inc(v.nop(), "VV")
            V(wq1b)
            q1n = n["VV"]

            def wq2b(g, ws3=ws3, q1n=q1n):
                g.wait_ge(sems["VV"], q1n)
                g.tensor_scalar(ws3[:, :, :], ws3[:, :, :], RB + 1.0, RB - 1.0,
                                ALU.min, ALU.max)
                g.drain()
                inc(g.nop(), "GG")
            G(wq2b)
            q2n = n["GG"]

            def wq3b(v, ws3=ws3, q2n=q2n):
                v.wait_ge(sems["GG"], q2n)
                v.tensor_scalar(ws3[:, :, :], ws3[:, :, :], RB, 0.0, ALU.subtract, ALU.add)
                v.drain()
                inc(v.nop(), "VV")
            V(wq3b)
            wqnb = n["VV"]

            for ti in range(NT):
                def sy6(sy, SB=SB, ti=ti):
                    if (SB, ti) == (0, 0):
                        sy.wait_ge(sems["SP"], sp_mid)
                    sy.wait_ge(sems["VV"], n["VV"])
                    inc(sy.dma_start(out=midst[:, :],
                                     in_=mid3[ti][:, SB * 1024:(SB + 1) * 1024]), "MD", 16)
                SY(sy6)
                xlm = n["MD"]

                def v40(v, ti=ti, xlm=xlm):
                    v.wait_ge(sems["MD"], xlm)
                    v.tensor_scalar(qmstage[:, :], midst[:, :], qcols2[:, ti:ti + 1], RBF,
                                    ALU.mult, ALU.add)
                    v.drain()
                    inc(v.nop(), "VV")
                V(v40)
                vq1 = n["VV"]

                def g11(g, vq1=vq1):
                    g.wait_ge(sems["VV"], vq1)
                    g.tensor_scalar(qmstage[:, :], qmstage[:, :], RBF + 127.0, RBF - 127.0,
                                    ALU.min, ALU.max)
                    g.drain()
                    inc(g.nop(), "GG")
                G(g11)
                gq1 = n["GG"]

                def v41(v, gq1=gq1):
                    v.wait_ge(sems["GG"], gq1)
                    v.tensor_scalar(qmidt[:, :], qmstage[:, :], RBF, 0.0, ALU.subtract, ALU.add)
                    v.drain()
                    inc(v.nop(), "VV")
                V(v41)
                qn = n["VV"]

                T(lambda t, qn=qn: t.wait_ge(sems["VV"], qn))
                qfn = tr_pipe(qmidt, lambda v, j, ps: v.tensor_copy(
                    qmidf[:, j * 128:(j + 1) * 128], ps), 8)

                def t24(t, ws3=ws3, wln=wln, qfn=qfn, wqnb=wqnb):
                    t.wait_ge(sems["VV"], wqnb)
                    t.wait_ge(sems["VV"], qfn)
                    for nn in range(4):
                        if nn in pb_free:
                            t.wait_ge(sems["VV"], pb_free[nn])
                        for hc in range(8):
                            mm = t.matmul(bk(nn)[:, :], qmidf[:, hc * 128:(hc + 1) * 128],
                                          ws3[:, hc, nn * 512:(nn + 1) * 512],
                                          start=(hc == 0), stop=(hc == 7))
                        inc(mm, "TT")
                T(t24)
                ttn = n["TT"]
                wo2_tt[SB] = ttn

                def v43(v, ti=ti, ttn=ttn):
                    for nn in range(4):
                        v.wait_ge(sems["TT"], ttn - 4 + nn + 1)
                        ac = acc[:, ti * D + nn * 512:ti * D + (nn + 1) * 512]
                        pb_free[nn] = inc(v.tensor_tensor(ac, ac, bk(nn)[:, :], ALU.add), "VV")
                V(v43)
        bar()

        xl_prev = {}
        for ti in range(NT):
            at = acc[:, ti * D:(ti + 1) * D]

            def sy7(sy, ti=ti):
                if ti >= 1:
                    sy.wait_ge(sems["VV"], xl_prev[ti - 1])
                inc(sy.dma_start(out=xf[:, :], in_=x2s3[ti]), "XL0", 16)
            SY(sy7)
            xln = n["XL0"]

            def v44(v, ti=ti, at=at):
                v.tensor_scalar(at, at, icols2[:, ti:ti + 1], 0.0, ALU.mult, ALU.add)
                v.drain()
                inc(v.nop(), "VV")
            V(v44)
            vo1 = n["VV"]

            def g12(g, vo1=vo1, at=at):
                g.wait_ge(sems["VV"], vo1)
                g.tensor_tensor(at, at, gbt[:, :], ALU.mult)
                g.drain()
                inc(g.nop(), "GG")
            G(g12)
            go2 = n["GG"]

            def v45(v, ti=ti, xln=xln, go2=go2, at=at):
                v.wait_ge(sems["XL0"], xln)
                v.wait_ge(sems["GG"], go2)
                v.tensor_tensor(at, at, xf[:, :], ALU.add)
                v.drain()
                xl_prev[ti] = inc(v.nop(), "VV")
            V(v45)
            vfin = n["VV"]

            def sy8(sy, ti=ti, vfin=vfin, at=at):
                sy.wait_ge(sems["VV"], vfin)
                inc(sy.dma_start(out=y3[ti], in_=at), "ST", 16)
            SY(sy8)
        stf = n["ST"]
        SY(lambda sy: sy.wait_ge(sems["ST"], stf))
        esp7.close()

    return nc


_NC_CACHE = {}


def kernel(L0_attn_x, L0_attn_Wq, L0_attn_Wk, L0_attn_Wv, L0_attn_Wgk1,
           L0_attn_Wgk2, L0_attn_bgk2, L0_attn_Wg, L0_attn_gnorm_w, L0_attn_Wo,
           L0_ffn_norm_w, L0_ffn_w_gv, L0_ffn_w_out, L0_c, L0_wmod):
    f32 = np.float32
    ca = lambda a: np.ascontiguousarray(np.asarray(a, f32))
    x = ca(L0_attn_x)
    wq, wk, wv, wg, wo = ca(L0_attn_Wq), ca(L0_attn_Wk), ca(L0_attn_Wv), \
        ca(L0_attn_Wg), ca(L0_attn_Wo)
    wgk1, wgk2, bgk2 = ca(L0_attn_Wgk1), ca(L0_attn_Wgk2), ca(L0_attn_bgk2)
    gnw, nw = ca(L0_attn_gnorm_w), ca(L0_ffn_norm_w)
    wgv, wout = ca(L0_ffn_w_gv), ca(L0_ffn_w_out)
    cc, wmod = ca(L0_c), ca(L0_wmod)

    wgvT = np.ascontiguousarray(wgv.T)
    woutT = np.ascontiguousarray(wout.T)
    bgk2c = np.ascontiguousarray(bgk2.reshape(16, 128).T)
    gnw2 = np.ascontiguousarray(np.tile(gnw, 2)[None, :])
    nwr = np.ascontiguousarray(nw[None, :])
    scanm = np.ones((128, SEG), f32)
    scanm[:, 0::64] = 0.0
    umask = np.ascontiguousarray(np.triu(np.ones((64, 64), f32)))
    ident = np.eye(128, dtype=f32)

    if "nc" not in _NC_CACHE:
        try:
            _NC_CACHE["nc"] = build_nc()
        except StopBuild:
            _NC_CACHE["nc"] = _LAST["nc"]
    nc = _NC_CACHE["nc"]

    in_maps = []
    for c in range(8):
        b, s = c // 4, c % 4
        smask = np.zeros((128, 8), f32)
        for j in range(8):
            if j // 4 == b and j % 4 < s:
                smask[:, j] = 1.0
        bsel = np.zeros((8, 8), f32)
        for r in range(8):
            if r // 4 == b:
                bsel[r, r % 4] = 1.0
            bsel[r, 4] = 1.0
        in_maps.append({
            "x_seg": np.ascontiguousarray(x[b, s * SEG:(s + 1) * SEG, :]),
            "cbc": np.ascontiguousarray(cc[b].reshape(16, 128).T),
            "wmod_q": np.ascontiguousarray(wmod[:, s * 3072:(s + 1) * 3072]),
            "wq": wq, "wk": wk, "wv": wv, "wg": wg, "wo": wo,
            "wgk1": wgk1, "wgk2": wgk2, "bgk2c": bgk2c, "gnw2": gnw2,
            "nw_row": nwr, "wgvT": wgvT, "woutT": woutT,
            "wgv_sh": np.ascontiguousarray(wgv[c * 2048:(c + 1) * 2048, :]),
            "wout_sh": np.ascontiguousarray(wout[c * 256:(c + 1) * 256, :]),
            "scanm": scanm, "umask": umask, "ident": ident,
            "smask": smask, "bsel": bsel,
        })
    res = run_bass_kernel_spmd(nc, in_maps, core_ids=list(range(8)))
    out = np.empty((2, 4096, D), f32)
    for c in range(8):
        b, s = c // 4, c % 4
        out[b, s * SEG:(s + 1) * SEG, :] = res.results[c]["y"]
    return out


# version 08
# speedup vs baseline: 1.0221x; 1.0221x over previous
import numpy as np
from contextlib import ExitStack

import concourse.bass as bass
import concourse.mybir as mybir
from concourse.bass_utils import run_bass_kernel_spmd

F32 = mybir.dt.float32
BF16 = mybir.dt.bfloat16
AF = mybir.ActivationFunctionType
ALU = mybir.AluOpType
AX = mybir.AxisListType

D = 2048
HD = 128
SEG = 1024
NT = 8
GH = 8
EPS = 1e-5
QSC = 0.08838834764831845
RB = 192.0
RBF = 12582912.0
HID = 8192
MODW = 12288
MROW = MODW + 16
DEBUG = False
STOP_AFTER = None


class StopBuild(Exception):
    pass


_LAST = {}


def build_nc():
    nc = bass.Bass("TRN2", target_bir_lowering=False)
    _LAST["nc"] = nc
    di = lambda n_, s_, dt=F32: nc.dram_tensor(n_, s_, dt, kind="ExternalInput").ap()

    x_in = di("x_seg", [SEG, D])
    cbc = di("cbc", [128, 16])
    wmod_q = di("wmod_q", [D, 3072])
    wq_i, wk_i, wv_i, wg_i, wo_i = (di(n_, [D, D]) for n_ in ("wq", "wk", "wv", "wg", "wo"))
    wgk1_i = di("wgk1", [D, 16])
    wgk2_i = di("wgk2", [16, D])
    bgk2_i = di("bgk2c", [128, 16])
    gnw_i = di("gnw2", [1, 256])
    nw_i = di("nw_row", [1, D])
    wgvT_i = di("wgvT", [D, 2 * HID])
    woutT_i = di("woutT", [HID, D])
    wgvsh_i = di("wgv_sh", [D, D])
    woutsh_i = di("wout_sh", [256, HID])
    scanm_i = di("scanm", [128, SEG])
    umask_i = di("umask", [64, 64])
    ident_i = di("ident", [128, 128])
    smask_i = di("smask", [128, 8])
    bsel_i = di("bsel", [8, 8])
    y_out = nc.dram_tensor("y", [SEG, D], F32, kind="ExternalOutput").ap()

    cc1i = nc.dram_tensor("cc1i", [1, MROW], F32)
    cc1o = nc.dram_tensor("cc1o", [8, MROW], F32)
    cc2i = [nc.dram_tensor(f"cc2i{g}", [128, 1056], F32) for g in range(2)]
    cc2o = [nc.dram_tensor(f"cc2o{g}", [1024, 1056], F32) for g in range(2)]
    modscr = nc.dram_tensor("modscr", [1, MROW], F32)
    if DEBUG:
        dbg_h1 = nc.dram_tensor("dbg_h1", [128, 16 * SEG], BF16)
        dbg_og = nc.dram_tensor("dbg_og", [128, 16 * SEG], BF16)
        dbg_afm = nc.dram_tensor("dbg_afm", [128, 16 * SEG], BF16)
    x2scr = nc.dram_tensor("x2scr", [SEG, D], F32)
    midscr = nc.dram_tensor("midscr", [SEG, HID], BF16)

    x3 = x_in.rearrange("(t p) n -> t p n", p=128)
    y3 = y_out.rearrange("(t p) n -> t p n", p=128)
    x2s3 = x2scr.ap().rearrange("(t p) n -> t p n", p=128)
    mid3 = midscr.ap().rearrange("(t p) n -> t p n", p=128)

    es0 = ExitStack()
    sb = lambda n_, s_, dt: es0.enter_context(nc.sbuf_tensor(n_, s_, dt))
    pbank = [es0.enter_context(nc.psum_tensor(f"bk{i}", [128, 512], F32)) for i in range(8)]
    bk = lambda i: pbank[i].ap()
    bkb = lambda i: pbank[i].ap().bitcast(BF16)

    actq = sb("actq", [128, 16 * SEG], BF16)
    scanms = sb("scanms", [128, SEG], F32)
    umasks = sb("umasks", [64, 64], BF16)
    ident = sb("ident_s", [128, 128], BF16)
    smasks = sb("smasks", [128, 8], F32)
    bsels = sb("bsels", [8, 8], F32)
    bgk2s = sb("bgk2s", [128, 16], F32)
    cbs = sb("cbs", [128, 16], F32)
    gnwt = sb("gnwt", [128, 256], BF16)
    epsc = sb("epsc", [128, 1], F32)
    onesr = sb("onesr", [1, 128], F32)
    onescl = sb("onescl", [128, 1], F32)
    sbb16 = sb("sbb16", [128, 16], BF16)
    colb = sb("colb", [128, 32], F32)
    qcols = sb("qcols", [128, NT], F32)
    qcols2 = sb("qcols2", [128, NT], F32)
    icols2 = sb("icols2", [128, NT], F32)
    svcol = sb("svcol", [128, NT], F32)
    ambuf = sb("ambuf", [128, NT * 32], F32)
    scw = sb("scw", [128, 4], F32)
    gacol = sb("gacol", [128, 16], F32)
    rbneg = sb("rbneg", [128, 1], F32)

    espW = ExitStack()
    ogf = espW.enter_context(nc.sbuf_tensor("ogf", [128, 16 * SEG], BF16))
    wring = espW.enter_context(nc.sbuf_tensor("wring", [128, 2 * 4096], BF16))
    esp01 = ExitStack()
    cpA = esp01.enter_context(nc.sbuf_tensor("cpA", [128, 2 * D], BF16))

    sems = {k: es0.enter_context(nc.semaphore(k))
            for k in ["WL0", "WL1", "WF0", "WF1", "XL0", "XL1", "CN", "CS", "RNK",
                      "MD", "SP", "ST", "TT", "VV", "AA", "GG", "CC", "PB"]}
    n = {k: 0 for k in sems}

    with es0, nc.Block() as blk:
        T = lambda f: blk.tensor(f)
        V = lambda f: blk.vector(f)
        A = lambda f: blk.scalar(f)
        G = lambda f: blk.gpsimd(f)
        SY = lambda f: blk.sync(f)

        def inc(instr, nm, k=1):
            instr.then_inc(sems[nm], k)
            n[nm] += k
            return n[nm]

        def bar():
            dma_owned = {"gpsimd": ["WL0", "WL1", "WF0", "WF1", "CN"],
                         "sync": ["XL0", "XL1", "CS", "RNK", "MD", "ST"],
                         "scalar": ["SP"]}
            for attr in ("tensor", "vector", "scalar", "gpsimd", "sync"):
                def _f(e, attr=attr):
                    if attr in ("vector", "scalar", "gpsimd"):
                        e.drain()
                    for s_ in dma_owned.get(attr, []):
                        if n[s_]:
                            e.wait_ge(sems[s_], n[s_])
                    inc(e.nop(), "PB")
                getattr(blk, attr)(_f)
            tgt = n["PB"]
            for attr in ("tensor", "vector", "scalar", "gpsimd", "sync"):
                getattr(blk, attr)(lambda e, tgt=tgt: e.wait_ge(sems["PB"], tgt))

        pb_free = {}

        def ckstop(tag, zb=None):
            if STOP_AFTER is not None and tag == STOP_AFTER:
                def vz(v):
                    v.memset(zb[:, 0:1024], 0.0)
                    v.drain()
                    inc(v.nop(), "VV")
                V(vz)
                vzn = n["VV"]

                def fin(sy):
                    sy.wait_ge(sems["VV"], vzn)
                    for ti in range(NT):
                        inc(sy.dma_start(out=y3[ti][:, 0:1024], in_=zb[:, 0:1024]), "ST", 16)
                        inc(sy.dma_start(out=y3[ti][:, 1024:2048], in_=zb[:, 0:1024]), "ST", 16)
                    sy.wait_ge(sems["ST"], n["ST"])
                SY(fin)
                raise StopBuild()

        def tr_pipe(src_tile, dst_fn, nblk, bank0=6):
            vv0 = n["VV"]
            tt0 = n["TT"]

            def tf(t):
                for j in range(nblk):
                    if j >= 2:
                        t.wait_ge(sems["VV"], vv0 + (j - 2) + 1)
                    else:
                        bnk = bank0 + j % 2
                        if bnk in pb_free:
                            t.wait_ge(sems["VV"], pb_free[bnk])
                    tr = t.transpose(bkb(bank0 + j % 2)[:, 0:128],
                                     src_tile[:, j * 128:(j + 1) * 128], ident[:, :])
                    inc(tr, "TT")
            T(tf)

            def vf(v):
                for j in range(nblk):
                    v.wait_ge(sems["TT"], tt0 + j + 1)
                    cp = dst_fn(v, j, bkb(bank0 + j % 2)[:, 0:128])
                    cnt = inc(cp, "VV")
                    pb_free[bank0 + j % 2] = cnt
            V(vf)
            return n["VV"]

        esp0 = ExitStack()
        sb0 = lambda n_, s_, dt: esp0.enter_context(nc.sbuf_tensor(n_, s_, dt))
        modrow = sb0("modrow", [1, MROW], F32)
        g1s = sb0("g1s", [8, 3104], F32)
        xt0 = sb0("xt0", [128, 2 * D], F32)
        wsum = sb0("wsum", [128, 32], F32)
        wmslot = sb0("wmslot", [128, 2 * 3072], F32)

        def g0(g):
            for dst, src in [(scanms[:, :], scanm_i), (umasks[:, :], umask_i),
                             (ident[:, :], ident_i), (smasks[:, :], smask_i),
                             (bsels[:, :], bsel_i), (bgk2s[:, :], bgk2_i),
                             (cbs[:, :], cbc), (gnwt[0:1, :], gnw_i)]:
                inc(g.dma_start(out=dst, in_=src), "CN", 16)
        G(g0)
        cn_c = n["CN"]

        V(lambda v: (v.memset(epsc[:, :], EPS), v.memset(onesr[:, :], 1.0),
                     v.memset(onescl[:, :], 1.0), v.memset(rbneg[:, :], -RB),
                     v.drain(), inc(v.nop(), "VV")))

        def a0(a):
            a.wait_ge(sems["CN"], cn_c)
            a.activation(colb[:, 0:16], cbs[:, :], AF.Silu)
            a.drain()
            inc(a.nop(), "AA")
        A(a0)
        aa0 = n["AA"]
        V(lambda v: (v.wait_ge(sems["AA"], aa0), inc(v.nop(), "VV")))
        vv_scb = n["VV"]

        p0tt = {}
        wm = wmod_q.rearrange("(c p) j -> c p j", p=128)
        for c in range(16):
            sp_ = f"WL{c % 2}"

            def g1(g, c=c, sp_=sp_):
                if c >= 2:
                    g.wait_ge(sems["TT"], p0tt[c - 2])
                slot = wmslot[:, (c % 2) * 3072:(c % 2) * 3072 + 3072]
                inc(g.dma_start(out=slot, in_=wm[c]), sp_, 16)
            G(g1)
            wlc = n[sp_]

            def t0(t, c=c, sp_=sp_, wlc=wlc):
                if c == 0:
                    t.wait_ge(sems["VV"], vv_scb)
                t.wait_ge(sems[sp_], wlc)
                slot = wmslot[:, (c % 2) * 3072:(c % 2) * 3072 + 3072]
                for nn in range(6):
                    mm = t.matmul(bk(nn)[0:1, :], colb[:, c:c + 1],
                                  slot[:, nn * 512:(nn + 1) * 512],
                                  start=(c == 0), stop=(c == 15))
                p0tt[c] = inc(mm, "TT")
            T(t0)
        tt_mod = n["TT"]

        gv = wgvsh_i.rearrange("(c p) j -> c p j", p=128)
        wo4 = woutsh_i.rearrange("(c p) (d j) -> c d p j", p=128, j=D)
        wred_vv = {}
        for i in range(24):
            sp_ = f"XL{i % 2}"

            def sy0(sy, i=i, sp_=sp_):
                if i >= 2:
                    sy.wait_ge(sems["VV"], wred_vv[i - 2])
                src = gv[i] if i < 16 else wo4[(i - 16) // 4, (i - 16) % 4]
                inc(sy.dma_start(out=xt0[:, (i % 2) * D:(i % 2) * D + D], in_=src), sp_, 16)
            SY(sy0)
            xlc = n[sp_]

            def v0(v, i=i, sp_=sp_, xlc=xlc):
                v.wait_ge(sems[sp_], xlc)
                wred_vv[i] = inc(
                    v.tensor_reduce(wsum[:, i:i + 1], xt0[:, (i % 2) * D:(i % 2) * D + D],
                                    AX.X, ALU.add, apply_absolute_value=True), "VV")
            V(v0)

        def v0b(v):
            v.drain()
            v.tensor_reduce(wsum[:, 30:31], wsum[:, 0:16], AX.X, ALU.add)
            v.tensor_reduce(wsum[:, 31:32], wsum[:, 16:24], AX.X, ALU.add)
            v.drain()
            inc(v.nop(), "VV")
        V(v0b)
        vv_ws = n["VV"]

        def t1(t):
            t.wait_ge(sems["VV"], vv_ws)
            inc(t.matmul(bk(6)[0:1, 0:2], onescl[:, 0:1], wsum[:, 30:32],
                         start=True, stop=True), "TT")
        T(t1)
        tt_ws = n["TT"]

        def v1(v):
            v.memset(modrow[:, :], 0.0)
            v.drain()
            v.wait_ge(sems["TT"], tt_mod)
            for nn in range(6):
                v.tensor_copy(modrow[0:1, nn * 512:(nn + 1) * 512], bk(nn)[0:1, :])
            v.wait_ge(sems["TT"], tt_ws)
            v.tensor_copy(modrow[0:1, MODW:MODW + 2], bk(6)[0:1, 0:2])
            v.drain()
            inc(v.nop(), "VV")
        V(v1)
        vv_row = n["VV"]

        def g2(g):
            g.wait_ge(sems["VV"], vv_row)
            inc(g.dma_start(out=cc1i.ap(), in_=modrow[0:1, :]), "CN", 16)
            g.wait_ge(sems["CN"], n["CN"])
            inc(g.collective_compute(
                "AllGather", ALU.bypass, replica_groups=[list(range(8))],
                ins=[cc1i.ap().opt()], outs=[cc1o.ap().opt()]), "CC")
            g.wait_ge(sems["CC"], n["CC"])
            inc(g.dma_start(out=g1s[:, 0:3072], in_=cc1o.ap()[:, 0:3072]), "CN", 16)
            inc(g.dma_start(out=g1s[:, 3072:3074], in_=cc1o.ap()[:, MODW:MODW + 2]),
                "CN", 16)
        G(g2)
        wl_g1 = n["CN"]

        for q in range(4):
            def t2(t, q=q):
                if q == 0:
                    t.wait_ge(sems["CN"], wl_g1)
                else:
                    t.wait_ge(sems["VV"], n["VV"])
                for nn in range(6):
                    mm = t.matmul(bk(nn)[0:1, :], bsels[:, q:q + 1],
                                  g1s[:, nn * 512:(nn + 1) * 512], start=True, stop=True)
                inc(mm, "TT")
            T(t2)
            ttq = n["TT"]

            def v2(v, q=q, ttq=ttq):
                v.wait_ge(sems["TT"], ttq)
                for nn in range(6):
                    v.tensor_copy(modrow[0:1, q * 3072 + nn * 512:q * 3072 + (nn + 1) * 512],
                                  bk(nn)[0:1, :])
                v.drain()
                inc(v.nop(), "VV")
            V(v2)

        def t2b(t):
            t.wait_ge(sems["VV"], n["VV"])
            inc(t.matmul(bk(6)[0:1, 0:2], bsels[:, 4:5], g1s[:, 3072:3074],
                         start=True, stop=True), "TT")
        T(t2b)
        tt_tot = n["TT"]

        def v3(v):
            v.wait_ge(sems["TT"], tt_tot)
            v.tensor_copy(modrow[0:1, MODW:MODW + 2], bk(6)[0:1, 0:2])
            v.drain()
            v.tensor_scalar(modrow[0:1, MODW + 2:MODW + 3], modrow[0:1, MODW:MODW + 1],
                            1.0 / (2.0 * HID * D), 0.0, ALU.mult, ALU.add)
            v.tensor_scalar(modrow[0:1, MODW + 3:MODW + 4], modrow[0:1, MODW + 1:MODW + 2],
                            1.0 / (1.0 * HID * D), 0.0, ALU.mult, ALU.add)
            v.drain()
            for base in (2 * D, 5 * D):
                r0 = modrow[0:1, base:base + D]
                v.tensor_scalar(xt0[0:1, 0:D], r0, -1.0, 0.25, ALU.mult, ALU.max)
                v.drain()
                v.tensor_tensor(xt0[0:1, 0:D], r0, xt0[0:1, 0:D], ALU.max)
                v.tensor_scalar(xt0[0:1, D:2 * D], r0, 0.0, 2.0, ALU.is_ge, ALU.mult)
                v.drain()
                v.scalar_tensor_tensor(r0, xt0[0:1, D:2 * D], 1.0, xt0[0:1, 0:D],
                                       ALU.subtract, ALU.mult)
                v.drain()
            for base in (1 * D, 4 * D):
                v.tensor_scalar(modrow[0:1, base:base + D], modrow[0:1, base:base + D],
                                1.0, 0.0, ALU.add, ALU.add)
            v.drain()
            inc(v.nop(), "VV")
        V(v3)
        vv_mod = n["VV"]

        def sy1(a):
            a.wait_ge(sems["VV"], vv_mod)
            inc(a.dma_start(out=modscr.ap(), in_=modrow[0:1, :]), "SP", 16)
        A(sy1)
        sp_modscr = n["SP"]

        def bcast_row(row_ap, dst_tile, nparts=4):
            ttb = n["TT"]

            def tb(t):
                t.wait_ge(sems["VV"], n["VV"])
                for nn in range(nparts):
                    mm = t.matmul(bk(nn)[:, 0:512], onesr[0:1, :],
                                  row_ap[0:1, nn * 512:(nn + 1) * 512], start=True, stop=True)
                inc(mm, "TT")
            T(tb)
            ttb = n["TT"]

            def vb(v):
                v.wait_ge(sems["TT"], ttb)
                for nn in range(nparts):
                    v.tensor_copy(dst_tile[:, nn * 512:(nn + 1) * 512], bk(nn)[:, 0:512])
                v.drain()
                inc(v.nop(), "VV")
            V(vb)

        def tg(t):
            t.wait_ge(sems["VV"], vv_mod)
            inc(t.matmul(bk(7)[:, 0:2], onesr[0:1, :], modrow[0:1, MODW + 2:MODW + 4],
                         start=True, stop=True), "TT")
        T(tg)
        tt_s = n["TT"]

        def vg(v):
            v.wait_ge(sems["TT"], tt_s)
            v.tensor_copy(scw[:, 0:2], bk(7)[:, 0:2])
            v.drain()
            v.reciprocal(scw[:, 2:3], scw[:, 0:1])
            v.drain()
            v.reciprocal(scw[:, 3:4], scw[:, 1:2])
            v.drain()
            inc(v.nop(), "VV")
        V(vg)

        bcast_row(modrow[0:1, 0:D], cpA[:, 0:D])
        bcast_row(modrow[0:1, D:2 * D], cpA[:, D:2 * D])

        def sy2(sy):
            sy.wait_ge(sems["VV"], n["VV"])
            inc(sy.dma_start(out=xt0[0:1, 0:256], in_=gnw_i), "CS", 16)
        SY(sy2)
        xl_g = n["CS"]

        def tgn(t):
            t.wait_ge(sems["CS"], xl_g)
            t.wait_ge(sems["VV"], n["VV"])
            inc(t.matmul(bk(7)[:, 0:256], onesr[0:1, :], xt0[0:1, 0:256],
                         start=True, stop=True), "TT")
        T(tgn)
        tt_gn = n["TT"]

        def vgn(v):
            v.wait_ge(sems["TT"], tt_gn)
            v.tensor_copy(gnwt[:, :], bk(7)[:, 0:256])
            v.drain()
            inc(v.nop(), "VV")
        V(vgn)
        bar()
        ckstop("P0", xt0)
        esp0.close()

        esp1 = ExitStack()
        sb1 = lambda n_, s_, dt: esp1.enter_context(nc.sbuf_tensor(n_, s_, dt))
        xt = sb1("xt", [128, 2 * D], F32)
        lnb = sb1("lnb", [128, D], F32)
        lnb2 = sb1("lnb2", [128, D], F32)
        sqj = [sb1(f"sqj{i}", [128, D], F32) for i in range(2)]
        hstage = sb1("hstage", [128, D], BF16)

        sa_t, ca1_t = cpA[:, 0:D], cpA[:, D:2 * D]

        def ln_tile(ti, xv, xln, cmul_t, cadd_t, done_d, var_d):
            def a1(a, ti=ti, xv=xv, xln=xln):
                a.wait_ge(sems[xln[0]], xln[1])
                if ti >= 1:
                    a.wait_ge(sems["VV"], var_d[ti - 1])
                a.activation(sqj[ti % 2][:, :], xv, AF.Square, accum_out=colb[:, 2:3])
                a.drain()
                inc(a.nop(), "AA")
            A(a1)
            aan = n["AA"]

            def v5(v, xv=xv, xln=xln, aan=aan, ti=ti):
                v.wait_ge(sems[xln[0]], xln[1])
                v.tensor_reduce(colb[:, 0:1], xv, AX.X, ALU.add)
                v.drain()
                v.tensor_scalar(colb[:, 1:2], colb[:, 0:1], 1.0 / D, 0.0, ALU.mult, ALU.add)
                v.drain()
                v.tensor_tensor(colb[:, 4:5], colb[:, 1:2], colb[:, 1:2], ALU.mult)
                v.wait_ge(sems["AA"], aan)
                v.drain()
                v.scalar_tensor_tensor(colb[:, 5:6], colb[:, 2:3], 1.0 / D, colb[:, 4:5],
                                       ALU.mult, ALU.subtract)
                v.drain()
                var_d[ti] = inc(v.nop(), "VV")
            V(v5)
            vvn = n["VV"]

            def a2(a, vvn=vvn):
                a.wait_ge(sems["VV"], vvn)
                a.activation(colb[:, 6:7], colb[:, 5:6], AF.Ln, bias=epsc[:, 0:1])
                a.drain()
                a.activation(colb[:, 7:8], colb[:, 6:7], AF.Exp, scale=-0.5)
                a.drain()
                inc(a.nop(), "AA")
            A(a2)
            aan2 = n["AA"]

            def v6(v, xv=xv, aan2=aan2):
                v.wait_ge(sems["AA"], aan2)
                v.scalar_tensor_tensor(colb[:, 8:9], colb[:, 1:2], -1.0, colb[:, 7:8],
                                       ALU.mult, ALU.mult)
                v.drain()
                v.tensor_scalar(lnb[:, :], xv, colb[:, 7:8], colb[:, 8:9], ALU.mult, ALU.add)
                v.drain()
                v.tensor_tensor(lnb2[:, :], lnb[:, :], cmul_t, ALU.mult)
                v.drain()
                v.tensor_tensor(hstage[:, :], lnb2[:, :], cadd_t, ALU.add)
                v.drain()
                inc(v.nop(), "VV")
            V(v6)
            return n["VV"]

        p1_done, p1_var = {}, {}
        for ti in range(NT):
            xv = xt[:, (ti % 2) * D:(ti % 2) * D + D]

            sp_ = f"XL{ti % 2}"

            def syl(sy, ti=ti, xv=xv, sp_=sp_):
                if ti >= 2:
                    sy.wait_ge(sems["VV"], p1_done[ti - 2])
                inc(sy.dma_start(out=xv, in_=x3[ti]), sp_, 16)
            SY(syl)
            xln = (sp_, n[sp_])
            h1n = ln_tile(ti, xv, xln, ca1_t, sa_t, p1_done, p1_var)

            T(lambda t, h1n=h1n: t.wait_ge(sems["VV"], h1n))
            p1_done[ti] = tr_pipe(
                hstage,
                lambda v, j, ps, ti=ti: v.tensor_copy(
                    actq[:, j * SEG + ti * 128:j * SEG + (ti + 1) * 128], ps), 16)
        bar()
        ckstop("P1", xt)
        if DEBUG:
            def dmp1(sy):
                inc(sy.dma_start(out=dbg_h1.ap(), in_=actq[:, :]), "ST", 16)
                sy.wait_ge(sems["ST"], n["ST"])
            SY(dmp1)
            bar()
        esp1.close()
        esp01.close()

        espA = ExitStack()
        sbA = lambda n_, s_, dt: espA.enter_context(nc.sbuf_tensor(n_, s_, dt))
        qf = sbA("qf", [128, GH * SEG], BF16)
        kf = sbA("kf", [128, GH * SEG], BF16)
        Gf = sbA("Gf", [128, GH * SEG], BF16)
        vtm = sbA("vtm", [128, GH * SEG], BF16)
        sgtm = sbA("sgtm", [128, GH * SEG], BF16)
        EG = sbA("EG", [128, SEG], F32)
        EGi = sbA("EGi", [128, SEG], F32)
        tmpb = None
        hrow = sbA("hrow", [128, SEG], F32)
        hrow2 = sbA("hrow2", [128, SEG], F32)
        qhb = sbA("qhb", [128, SEG], BF16)
        khb = sbA("khb", [128, SEG], BF16)
        kwt = sbA("kwt", [128, 1280], BF16)
        Sst = sbA("Sst", [128, GH * HD], F32)
        Scur = sbA("Scur", [128, HD], F32)
        sgbuf = sbA("sgbuf", [128, HD], BF16)
        s2buf = sbA("s2buf", [128, HD], F32)
        otm = hrow2
        atb = sbA("atb", [128, 64], BF16)
        zbA = sbA("zbA", [128, 16], F32)
        zbB = sbA("zbB", [128, 16], F32)
        glc = sbA("glc", [128, 16], F32)
        ssq = sbA("ssq", [128, 8], F32)
        ctile = sbA("ctile", [128, 1056], F32)
        ogstage = sbA("ogstage", [128, SEG], BF16)
        kwb = ogstage
        tmpb = EG[:, :].bitcast(BF16)[0:16, 0:1024]
        state = {"tmpb_vv": 0, "ttog_prev": 0}
        V(lambda v: (v.memset(ctile[:, 1032:1056], 0.0), v.drain(), inc(v.nop(), "VV")))

        def proj(grp):
            co = grp * 1024
            if True:
                def gld(g):
                    wg1 = wgk1_i.rearrange("(c p) r -> p c r", p=128)
                    dst = kwt[:, 0:256].rearrange("p (c r) -> p c r", r=16)
                    inc(g.dma_start(out=dst, in_=wg1), "CN", 16)
                G(gld)
                wl0 = n["CN"]

                def t6(t):
                    t.wait_ge(sems["CN"], wl0)
                    for th in range(2):
                        for c in range(16):
                            mm = t.matmul(bk(6 + th)[0:16, :], kwt[:, c * 16:(c + 1) * 16],
                                          actq[:, c * SEG + th * 512:c * SEG + th * 512 + 512],
                                          start=(c == 0), stop=(c == 15))
                        inc(mm, "TT")
                T(t6)
                tt6 = n["TT"]

                def v8(v):
                    v.wait_ge(sems["TT"], tt6)
                    v.tensor_copy(tmpb[0:16, 0:512], bk(6)[0:16, :])
                    v.tensor_copy(tmpb[0:16, 512:1024], bk(7)[0:16, :])
                    v.drain()
                    state["tmpb_vv"] = inc(v.nop(), "VV")
                V(v8)

            def gld2(g, grp=grp):
                g.wait_ge(sems["VV"], state["tmpb_vv"])
                inc(g.dma_start(out=kwt[0:16, 256:1280], in_=wgk2_i[:, co:co + 1024]), "CN", 16)
            G(gld2)
            wlgk = n["CN"]
            wgk2b = kwt[0:16, 256:1280]

            gk_aa, gk_vv = {}, {}
            for hh in range(GH):
                def t7(t, hh=hh):
                    t.wait_ge(sems["CN"], wlgk)
                    if hh >= 1:
                        t.wait_ge(sems["VV"], gk_vv[hh - 1])
                    for th in range(2):
                        mm = t.matmul(bk(6 + th)[:, :], wgk2b[:, hh * 128:(hh + 1) * 128],
                                      tmpb[0:16, th * 512:(th + 1) * 512], start=True, stop=True)
                    inc(mm, "TT")
                T(t7)
                ttg = n["TT"]

                def a3(a, hh=hh, ttg=ttg, grp=grp):
                    a.wait_ge(sems["TT"], ttg)
                    if hh >= 1:
                        a.wait_ge(sems["VV"], gk_vv[hh - 1])
                    col = bgk2s[:, grp * 8 + hh:grp * 8 + hh + 1]
                    for th in range(2):
                        sl = hrow[:, th * 512:(th + 1) * 512]
                        a.activation(sl, bk(6 + th)[:, :], AF.Abs, bias=col)
                        a.drain()
                        a.activation(sl, sl, AF.Exp, scale=-1.0)
                        a.drain()
                        a.activation(sl, sl, AF.Ln, bias=1.0)
                        a.drain()
                    gk_aa[hh] = inc(a.nop(), "AA")
                A(a3)

                def v9(v, hh=hh, grp=grp):
                    v.wait_ge(sems["AA"], gk_aa[hh])
                    col = bgk2s[:, grp * 8 + hh:grp * 8 + hh + 1]
                    for th in range(2):
                        v.tensor_scalar(hrow2[:, th * 512:(th + 1) * 512], bk(6 + th)[:, :],
                                        col, 0.0, ALU.add, ALU.min)
                    v.drain()
                    v.tensor_tensor(hrow2[:, :], hrow2[:, :], hrow[:, :], ALU.subtract)
                    v.drain()
                    v.tensor_tensor_scan(Gf[:, hh * SEG:(hh + 1) * SEG], scanms[:, :],
                                         hrow2[:, :], 0.0, ALU.mult, ALU.add)
                    v.drain()
                    gk_vv[hh] = inc(v.nop(), "VV")
                V(v9)

            strip_tt = {}
            for mi, (mname, wsrc) in enumerate(
                    [("q", wq_i), ("k", wk_i), ("v", wv_i), ("g", wg_i)]):
                for s4 in range(4):
                    sidx = mi * 4 + s4
                    slot = wring[:, (sidx % 2) * 4096:(sidx % 2) * 4096 + 4096]
                    c0 = co + s4 * 256

                    spn = f"WL{sidx % 2}"

                    def gld3(g, wsrc=wsrc, slot=slot, c0=c0, sidx=sidx, spn=spn):
                        if sidx >= 2:
                            g.wait_ge(sems["TT"], strip_tt[sidx - 2])
                        src = wsrc.rearrange("(c p) j -> p c j", p=128)[:, :, c0:c0 + 256]
                        dst = slot.rearrange("p (c j) -> p c j", j=256)
                        for q4 in range(4):
                            inc(g.dma_start(out=dst[:, q4 * 4:(q4 + 1) * 4, :],
                                            in_=src[:, q4 * 4:(q4 + 1) * 4, :]), spn, 16)
                    G(gld3)
                    wln = (spn, n[spn])

                    if mname in ("q", "k"):
                        dstb = qf if mname == "q" else kf

                        def t8(t, slot=slot, wln=wln):
                            t.wait_ge(sems[wln[0]], wln[1])
                            for u in range(4):
                                m2, th = u // 2, u % 2
                                if u in pb_free:
                                    t.wait_ge(sems["VV"], pb_free[u])
                                for kc in range(16):
                                    mm = t.matmul(
                                        bk(u)[:, :],
                                        slot[:, kc * 256 + m2 * 128:kc * 256 + m2 * 128 + 128],
                                        actq[:, kc * SEG + th * 512:kc * SEG + th * 512 + 512],
                                        start=(kc == 0), stop=(kc == 15))
                                inc(mm, "TT")
                        T(t8)
                        ttn = n["TT"]
                        strip_tt[sidx] = ttn

                        def v10(v, s4=s4, ttn=ttn, dstb=dstb, mname=mname):
                            for u in range(4):
                                m2, th = u // 2, u % 2
                                v.wait_ge(sems["TT"], ttn - 4 + u + 1)
                                hcol = (s4 * 2 + m2) * SEG + th * 512
                                if mname == "q":
                                    cp = v.tensor_scalar(dstb[:, hcol:hcol + 512], bk(u)[:, :],
                                                         QSC, 0.0, ALU.mult, ALU.add)
                                else:
                                    cp = v.tensor_copy(dstb[:, hcol:hcol + 512], bk(u)[:, :])
                                pb_free[u] = inc(cp, "VV")
                        V(v10)
                    else:
                        vvb9 = n["VV"]
                        aab9 = n["AA"]
                        is_v = (mname == "v")

                        def t9(t, slot=slot, wln=wln, vvb9=vvb9, aab9=aab9, is_v=is_v):
                            t.wait_ge(sems[wln[0]], wln[1])
                            for ti in range(NT):
                                bnk = ti % 4
                                if ti < 4:
                                    if bnk in pb_free:
                                        t.wait_ge(sems["VV"], pb_free[bnk])
                                    if (bnk, "a") in pb_free:
                                        t.wait_ge(sems["AA"], pb_free[(bnk, "a")])
                                elif is_v:
                                    t.wait_ge(sems["VV"], vvb9 + (ti - 4) + 1)
                                else:
                                    t.wait_ge(sems["AA"], aab9 + (ti - 4) + 1)
                                for kc in range(16):
                                    mm = t.matmul(
                                        bk(bnk)[:, 0:256],
                                        actq[:, kc * SEG + ti * 128:kc * SEG + (ti + 1) * 128],
                                        slot[:, kc * 256:(kc + 1) * 256],
                                        start=(kc == 0), stop=(kc == 15))
                                inc(mm, "TT")
                        T(t9)
                        ttn = n["TT"]
                        strip_tt[sidx] = ttn

                        if mname == "v":
                            def v11(v, s4=s4, ttn=ttn):
                                for ti in range(NT):
                                    bnk = ti % 4
                                    v.wait_ge(sems["TT"], ttn - 8 + ti + 1)
                                    dst = vtm.rearrange("p (h t) -> p h t", t=SEG)[
                                        :, 2 * s4:2 * s4 + 2, ti * 128:(ti + 1) * 128]
                                    src = bk(bnk)[:, 0:256].rearrange("p (h d) -> p h d", d=128)
                                    pb_free[bnk] = inc(v.tensor_copy(dst, src), "VV")
                            V(v11)
                        else:
                            sg_aa, sg_vv = {}, {}
                            vvb4 = n["VV"]

                            def a4(a, ttn=ttn, vvb4=vvb4):
                                for ti in range(NT):
                                    bnk = ti % 4
                                    a.wait_ge(sems["TT"], ttn - 8 + ti + 1)
                                    if ti >= 2:
                                        a.wait_ge(sems["VV"], vvb4 + (ti - 2) + 1)
                                    else:
                                        a.wait_ge(sems["VV"], vvb4)
                                    a.activation(ogstage[:, (ti % 2) * 256:(ti % 2) * 256 + 256],
                                                 bk(bnk)[:, 0:256], AF.Silu)
                                    a.drain()
                                    cnt = inc(a.nop(), "AA")
                                    sg_aa[ti] = cnt
                                    pb_free[(bnk, "a")] = cnt
                            A(a4)

                            def v12(v, s4=s4):
                                for ti in range(NT):
                                    v.wait_ge(sems["AA"], sg_aa[ti])
                                    dst = sgtm.rearrange("p (h t) -> p h t", t=SEG)[
                                        :, 2 * s4:2 * s4 + 2, ti * 128:(ti + 1) * 128]
                                    src = ogstage[:, (ti % 2) * 256:(ti % 2) * 256 + 256].rearrange(
                                        "p (h d) -> p h d", d=128)
                                    gsrc = gnwt[:, :].rearrange("p (h d) -> p h d", d=128)
                                    sg_vv[ti] = inc(v.tensor_tensor(dst, src, gsrc, ALU.mult), "VV")
                            V(v12)
            bar()

        def sweep1(grp):
            ds_vv = {}
            for hh in range(GH):
                Gh = Gf[:, hh * SEG:(hh + 1) * SEG]
                g3 = Gh.rearrange("p (c t) -> p c t", t=64)

                def a5(a, Gh=Gh, hh=hh):
                    if hh >= 1:
                        a.wait_ge(sems["VV"], ds_vv[hh - 1])
                    inc(a.activation(EGi[:, :], Gh, AF.Exp, scale=-1.0 / 16.0), "AA")
                A(a5)
                aan = n["AA"]

                def v13(v, g3=g3, aan=aan):
                    v.wait_ge(sems["AA"], aan)
                    v.tensor_copy(glc[:, 0:16], g3[:, :, 63])
                    v.drain()
                    v.tensor_tensor_scan(zbA[:, 0:16], scanms[:, 1:17], glc[:, 0:16],
                                         0.0, ALU.mult, ALU.add)
                    v.drain()
                    v.tensor_tensor(zbB[:, 0:16], glc[:, 0:16], zbA[:, 0:16], ALU.subtract)
                    v.drain()
                    inc(v.tensor_scalar(zbB[:, 0:16], zbB[:, 0:16], zbA[:, 15:16], 0.0,
                                        ALU.add, ALU.add), "VV")
                V(v13)
                if hh == 0:
                    ckstop("s1v13", hrow)
                vv2 = n["VV"]

                def a6(a, vv2=vv2, hh=hh):
                    a.wait_ge(sems["VV"], vv2)
                    a.activation(zbB[:, 0:16], zbB[:, 0:16], AF.Exp, scale=1.0 / 16.0)
                    inc(a.activation(ctile[:, 1024 + hh:1025 + hh], zbA[:, 15:16], AF.Exp,
                                     scale=1.0 / 16.0), "AA")
                A(a6)
                aan2 = n["AA"]

                def v14(v, hh=hh, aan2=aan2):
                    v.wait_ge(sems["AA"], aan2)
                    ez3 = zbB[:, 0:16].unsqueeze(2).broadcast_to([128, 16, 64])
                    egi3 = EGi[:, :].rearrange("p (c t) -> p c t", t=64)
                    h3 = hrow[:, :].rearrange("p (c t) -> p c t", t=64)
                    v.tensor_tensor(h3, egi3, ez3, ALU.mult)
                    v.drain()
                    inc(v.tensor_tensor(kwb[:, :], kf[:, hh * SEG:(hh + 1) * SEG],
                                        hrow[:, :], ALU.mult), "VV")
                V(v14)
                if hh == 0:
                    ckstop("s1v14", hrow)
                kwn = n["VV"]

                T(lambda t, kwn=kwn: t.wait_ge(sems["VV"], kwn))
                vkw = tr_pipe(kwb, lambda v, j, ps: v.tensor_copy(
                    kwt[:, j * 128:(j + 1) * 128], ps), 8)
                if hh == 0:
                    ckstop("s1tr", hrow)

                def t11(t, hh=hh, vkw=vkw):
                    t.wait_ge(sems["VV"], vkw)
                    if hh >= 1:
                        t.wait_ge(sems["VV"], ds_vv[hh - 1])
                    for b2 in range(8):
                        lhsT = kwt[:, b2 * 128:(b2 + 1) * 128]
                        rhs = vtm.rearrange("p (h b t) -> p h b t", h=GH, t=128)[
                            :, hh, b2, :]
                        mm = t.matmul(bk(4)[:, 0:128], lhsT, rhs,
                                      start=(b2 == 0), stop=(b2 == 7))
                    inc(mm, "TT")
                T(t11)
                ttds = n["TT"]

                def v16(v, hh=hh, ttds=ttds):
                    v.wait_ge(sems["TT"], ttds)
                    ds_vv[hh] = inc(v.tensor_copy(ctile[:, hh * 128:(hh + 1) * 128],
                                                  bk(4)[:, 0:128]), "VV")
                V(v16)
                if hh == 0:
                    ckstop("s1mm", hrow)
            bar()

        def gather(grp):
            def g4(g, grp=grp):
                inc(g.dma_start(out=cc2i[grp].ap(), in_=ctile[:, :]), "CN", 16)
                g.wait_ge(sems["CN"], n["CN"])
                inc(g.collective_compute(
                    "AllGather", ALU.bypass, replica_groups=[list(range(8))],
                    ins=[cc2i[grp].ap().opt()], outs=[cc2o[grp].ap().opt()]), "CC")
            G(g4)
            ccn = n["CC"]

            V(lambda v: (v.memset(Sst[:, :], 0.0), v.drain(), inc(v.nop(), "VV")))
            ranks = cc2o[grp].ap().rearrange("(r p) n -> r p n", p=128)
            p4vv = {}
            for r in range(8):
                def syr(sy, r=r, ccn=ccn):
                    sy.wait_ge(sems["CC"], ccn)
                    if r >= 1:
                        sy.wait_ge(sems["VV"], p4vv[r - 1])
                    inc(sy.dma_start(out=ctile[:, :], in_=ranks[r]), "RNK", 16)
                SY(syr)
                xln = n["RNK"]

                def v17(v, r=r, xln=xln):
                    v.wait_ge(sems["RNK"], xln)
                    m = smasks[:, r:r + 1]
                    v.tensor_scalar(zbB[:, 0:1], m, -1.0, 1.0, ALU.mult, ALU.add)
                    v.drain()
                    v.tensor_scalar(zbA[:, 0:8], ctile[:, 1024:1032], m, zbB[:, 0:1],
                                    ALU.mult, ALU.add)
                    v.drain()
                    a3_ = zbA[:, 0:8].unsqueeze(2).broadcast_to([128, 8, 128])
                    s3_ = Sst[:, :].rearrange("p (h d) -> p h d", d=128)
                    v.tensor_tensor(s3_, s3_, a3_, ALU.mult)
                    v.drain()
                    v.scalar_tensor_tensor(Sst[:, :], ctile[:, 0:1024], m, Sst[:, :],
                                           ALU.mult, ALU.add)
                    v.drain()
                    p4vv[r] = inc(v.nop(), "VV")
                V(v17)
            bar()

        def sweep2(grp):
            for hh in range(GH):
                hg = grp * GH + hh
                Gh = Gf[:, hh * SEG:(hh + 1) * SEG]
                qh = qf[:, hh * SEG:(hh + 1) * SEG]
                kh = kf[:, hh * SEG:(hh + 1) * SEG]
                sgh = sgtm[:, hh * SEG:(hh + 1) * SEG]

                def a7(a, Gh=Gh):
                    a.wait_ge(sems["VV"], n["VV"])
                    a.wait_ge(sems["GG"], n["GG"])
                    a.activation(EG[:, :], Gh, AF.Exp, scale=1.0 / 16.0)
                    inc(a.activation(EGi[:, :], Gh, AF.Exp, scale=-1.0 / 16.0), "AA")
                A(a7)
                aan = n["AA"]
                eg3 = EG[:, :].rearrange("p (c t) -> p c t", t=64)
                egi3 = EGi[:, :].rearrange("p (c t) -> p c t", t=64)

                def gp(g, aan=aan):
                    g.wait_ge(sems["AA"], aan)
                    egb = eg3[:, :, 0].unsqueeze(2).broadcast_to([128, 16, 64])
                    egbi = egi3[:, :, 0].unsqueeze(2).broadcast_to([128, 16, 64])
                    h3 = hrow[:, :].rearrange("p (c t) -> p c t", t=64)
                    h23 = hrow2[:, :].rearrange("p (c t) -> p c t", t=64)
                    g.tensor_tensor(h3, eg3, egbi, ALU.mult)
                    inc(g.tensor_tensor(h23, egi3, egb, ALU.mult), "GG")
                G(gp)
                ggn = n["GG"]

                def v18(v, qh=qh, kh=kh, ggn=ggn):
                    v.wait_ge(sems["GG"], ggn)
                    v.tensor_tensor(qhb[:, :], qh, hrow[:, :], ALU.mult)
                    inc(v.tensor_tensor(khb[:, :], kh, hrow2[:, :], ALU.mult), "VV")
                V(v18)
                qkn = n["VV"]

                def gp2(g, qkn=qkn):
                    g.wait_ge(sems["VV"], qkn)
                    egl = eg3[:, :, 63].unsqueeze(2).broadcast_to([128, 16, 64])
                    h3 = hrow[:, :].rearrange("p (c t) -> p c t", t=64)
                    inc(g.tensor_tensor(h3, egi3, egl, ALU.mult), "GG")
                G(gp2)
                gg2 = n["GG"]

                def v19(v, kh=kh, gg2=gg2, hh=hh):
                    v.wait_ge(sems["GG"], gg2)
                    v.tensor_tensor(kwb[:, :], kh, hrow[:, :], ALU.mult)
                    inc(v.tensor_copy(Scur[:, :], Sst[:, hh * HD:(hh + 1) * HD]), "VV")
                V(v19)
                kwn = n["VV"]

                T(lambda t, kwn=kwn: t.wait_ge(sems["VV"], kwn))
                sready = tr_pipe(kwb, lambda v, j, ps: v.tensor_copy(
                    kwt[:, j * 128:(j + 1) * 128], ps), 8)

                ch_aa, ch_tt, ch_vv = {}, {}, {}
                for c in range(16):
                    p0 = (c % 2) * 64
                    b2 = c // 2
                    vslice = vtm.rearrange("p (h b t) -> p h b t", h=GH, t=128)[
                        p0:p0 + 64, hh, b2, :]
                    kwts = kwt[p0:p0 + 64, b2 * 128:(b2 + 1) * 128]

                    def a8(a, c=c, sready=sready):
                        if c == 0:
                            a.wait_ge(sems["VV"], sready)
                        else:
                            a.wait_ge(sems["VV"], ch_vv[(c - 1, "oev")])
                        a.activation(s2buf[:, :], Scur[:, :], AF.Copy,
                                     scale=eg3[:, c, 63].unsqueeze(1))
                        ch_aa[c] = inc(a.activation(sgbuf[:, :], Scur[:, :], AF.Copy,
                                                    scale=eg3[:, c, 0].unsqueeze(1)), "AA")
                    A(a8)

                    def t13(t, c=c, sready=sready):
                        if c == 0:
                            t.wait_ge(sems["VV"], sready)
                        else:
                            t.wait_ge(sems["VV"], ch_vv[(c - 1, "oev")])
                        mm = t.matmul(bk(5)[0:64, 0:64], khb[:, c * 64:(c + 1) * 64],
                                      qhb[:, c * 64:(c + 1) * 64], start=True, stop=True)
                        ch_tt[(c, "at")] = inc(mm, "TT")
                    T(t13)

                    def v21(v, c=c, p0=p0):
                        v.wait_ge(sems["TT"], ch_tt[(c, "at")])
                        ch_vv[(c, "atm")] = inc(
                            v.tensor_tensor(atb[p0:p0 + 64, :], bk(5)[0:64, 0:64],
                                            umasks[:, :], ALU.mult), "VV")
                    V(v21)

                    def t14(t, c=c, vslice=vslice, kwts=kwts, p0=p0):
                        t.wait_ge(sems["VV"], ch_vv[(c, "atm")])
                        t.wait_ge(sems["AA"], ch_aa[c])
                        t.matmul(bk(6)[0:64, 0:128], qhb[:, c * 64:(c + 1) * 64],
                                 sgbuf[:, :], start=True, stop=True)
                        mm2 = t.matmul(bk(3)[0:64, 0:128], atb[p0:p0 + 64, :], vslice,
                                       start=True, stop=True)
                        ch_tt[(c, "o")] = inc(mm2, "TT")
                        mm3 = t.matmul(bk(2)[:, 0:128], kwts, vslice, start=True, stop=True)
                        ch_tt[(c, "sd")] = inc(mm3, "TT")
                    T(t14)

                    def a22(a, c=c, b2=b2):
                        a.wait_ge(sems["TT"], ch_tt[(c, "o")])
                        dst = otm[(c % 2) * 64:(c % 2) * 64 + 64, b2 * 128:(b2 + 1) * 128]
                        ch_aa[(c, "oc")] = inc(
                            a.activation(dst, bk(3)[0:64, 0:128], AF.Copy), "AA")
                    A(a22)

                    def v22(v, c=c, b2=b2):
                        v.wait_ge(sems["TT"], ch_tt[(c, "sd")])
                        v.wait_ge(sems["AA"], ch_aa[(c, "oc")])
                        dst = otm[(c % 2) * 64:(c % 2) * 64 + 64, b2 * 128:(b2 + 1) * 128]
                        ch_vv[c] = inc(v.tensor_tensor(Scur[:, :], s2buf[:, :],
                                                       bk(2)[:, 0:128], ALU.add), "VV")
                        ch_vv[(c, "oev")] = inc(v.tensor_tensor(dst, dst, bk(6)[0:64, 0:128],
                                                                ALU.add), "VV")
                    V(v22)

                def a9(a):
                    for b2 in range(8):
                        a.wait_ge(sems["VV"], ch_vv[(b2 * 2 + 1, "oev")])
                        a.activation(qhb[:, b2 * 128:(b2 + 1) * 128],
                                     otm[:, b2 * 128:(b2 + 1) * 128],
                                     AF.Square, accum_out=ssq[:, b2:b2 + 1])
                    a.drain()
                    a.activation(zbA[:, 0:8], ssq[:, 0:8], AF.Ln, scale=1.0 / 128.0,
                                 bias=epsc[:, 0:1])
                    a.drain()
                    a.activation(zbA[:, 0:8], zbA[:, 0:8], AF.Exp, scale=-0.5)
                    a.drain()
                    inc(a.nop(), "AA")
                A(a9)
                arst = n["AA"]

                def v23(v, sgh=sgh, arst=arst):
                    v.wait_ge(sems["AA"], arst)
                    v.wait_ge(sems["TT"], state["ttog_prev"])
                    for b2 in range(8):
                        v.tensor_scalar(hrow[:, b2 * 128:(b2 + 1) * 128],
                                        otm[:, b2 * 128:(b2 + 1) * 128],
                                        zbA[:, b2:b2 + 1], 0.0, ALU.mult, ALU.add)
                    v.drain()
                    inc(v.tensor_tensor(ogstage[:, :], hrow[:, :], sgh, ALU.mult), "VV")
                V(v23)
                ogn = n["VV"]

                T(lambda t, ogn=ogn: t.wait_ge(sems["VV"], ogn))
                tr_pipe(ogstage, lambda v, j, ps, hg=hg: v.tensor_copy(
                    ogf[:, hg * SEG + j * 128:hg * SEG + (j + 1) * 128], ps), 8)
                state["ttog_prev"] = n["TT"]
            bar()

        for grp in range(2):
            proj(grp)
            ckstop(f"proj{grp}", hrow)
            sweep1(grp)
            ckstop(f"sw1_{grp}", hrow)
            gather(grp)
            ckstop(f"gth{grp}", hrow)
            sweep2(grp)
            ckstop(f"sw2_{grp}", hrow)
        espA.close()

        esp6 = ExitStack()
        sb6 = lambda n_, s_, dt: esp6.enter_context(nc.sbuf_tensor(n_, s_, dt))
        cpB = sb6("cpB", [128, 5 * D], BF16)
        xt6 = sb6("xt6", [128, 2 * D], F32)
        lnb = sb6("lnb6", [128, D], F32)
        lnb2 = sb6("lnb26", [128, D], F32)
        sqj = [sb6(f"sqj6{i}", [128, D], F32) for i in range(2)]
        hstage = sb6("hstage6", [128, D], BF16)
        hrow6 = sb6("hrow6", [128, D], F32)
        mrow6 = sb6("mrow6", [1, D], F32)
        qstage = sb6("qstage", [128, D], F32)
        grow = sb6("grow", [1, D], BF16)

        for sl, off in [(0, 2 * D), (1, 3 * D), (2, 4 * D), (3, 5 * D)]:
            def sy3(sy, sl=sl, off=off):
                if sl == 0:
                    sy.wait_ge(sems["SP"], sp_modscr)
                sy.wait_ge(sems["VV"], n["VV"])
                inc(sy.dma_start(out=mrow6[0:1, :], in_=modscr.ap()[0:1, off:off + D]),
                    "CS", 16)
            SY(sy3)
            xr = n["CS"]

            def t16(t, xr=xr):
                t.wait_ge(sems["CS"], xr)
                t.wait_ge(sems["VV"], n["VV"])
                for nn in range(4):
                    mm = t.matmul(bk(nn)[:, 0:512], onesr[0:1, :],
                                  mrow6[0:1, nn * 512:(nn + 1) * 512], start=True, stop=True)
                inc(mm, "TT")
            T(t16)
            ttb = n["TT"]

            def v25(v, sl=sl, ttb=ttb):
                v.wait_ge(sems["TT"], ttb)
                for nn in range(4):
                    v.tensor_copy(cpB[:, sl * D + nn * 512:sl * D + (nn + 1) * 512],
                                  bk(nn)[:, 0:512])
                v.drain()
                inc(v.nop(), "VV")
            V(v25)

        def sy4(sy):
            sy.wait_ge(sems["VV"], n["VV"])
            inc(sy.dma_start(out=mrow6[0:1, :], in_=nw_i), "CS", 16)
        SY(sy4)
        xr2 = n["CS"]

        def t17(t, xr2=xr2):
            t.wait_ge(sems["CS"], xr2)
            t.wait_ge(sems["VV"], n["VV"])
            for nn in range(4):
                mm = t.matmul(bk(nn)[:, 0:512], onesr[0:1, :],
                              mrow6[0:1, nn * 512:(nn + 1) * 512], start=True, stop=True)
            inc(mm, "TT")
        T(t17)
        ttnw = n["TT"]

        def v26(v, ttnw=ttnw):
            v.wait_ge(sems["TT"], ttnw)
            for nn in range(4):
                v.tensor_copy(cpB[:, 4 * D + nn * 512:4 * D + (nn + 1) * 512], bk(nn)[:, 0:512])
            v.drain()
            inc(v.nop(), "VV")
        V(v26)

        def g5(g):
            g.wait_ge(sems["VV"], n["VV"])
            inc(g.dma_start(out=grow[0:1, :], in_=modscr.ap()[0:1, 2 * D:3 * D]), "CN", 16)
        G(g5)
        wlga = n["CN"]

        def t18(t, wlga=wlga):
            t.wait_ge(sems["CN"], wlga)
            t.wait_ge(sems["VV"], n["VV"])
            for c in range(16):
                tr = t.transpose(bkb(5)[:, 2 * c:2 * c + 1], grow[0:1, c * 128:(c + 1) * 128],
                                 ident[0:1, 0:1])
            inc(tr, "TT")
        T(t18)
        ttga = n["TT"]

        def v27(v, ttga=ttga):
            v.wait_ge(sems["TT"], ttga)
            src = bkb(5)[:, 0:32].rearrange("p (c u) -> p c u", u=2)[:, :, 0]
            v.tensor_copy(gacol[:, 0:16], src)
            v.drain()
            inc(v.nop(), "VV")
        V(v27)
        ga_done = n["VV"]
        bar()

        afm = actq
        wo_tt = {}
        for s8 in range(8):
            slot = wring[:, (s8 % 2) * 4096:(s8 % 2) * 4096 + 4096]

            spn = f"WL{s8 % 2}"

            def g6(g, s8=s8, slot=slot, spn=spn):
                if s8 >= 2:
                    g.wait_ge(sems["TT"], wo_tt[s8 - 2])
                src = wo_i.rearrange("(c p) j -> p c j", p=128)[:, :, s8 * 256:(s8 + 1) * 256]
                dst = slot.rearrange("p (c j) -> p c j", j=256)
                for q4 in range(4):
                    inc(g.dma_start(out=dst[:, q4 * 4:(q4 + 1) * 4, :],
                                    in_=src[:, q4 * 4:(q4 + 1) * 4, :]), spn, 16)
            G(g6)
            wln = (spn, n[spn])

            def t19(t, slot=slot, wln=wln):
                t.wait_ge(sems[wln[0]], wln[1])
                t.wait_ge(sems["VV"], ga_done)
                for u in range(4):
                    m2, th = u // 2, u % 2
                    if u in pb_free:
                        t.wait_ge(sems["VV"], pb_free[u])
                    for kc in range(16):
                        mm = t.matmul(bk(u)[:, :],
                                      slot[:, kc * 256 + m2 * 128:kc * 256 + m2 * 128 + 128],
                                      ogf[:, kc * SEG + th * 512:kc * SEG + th * 512 + 512],
                                      start=(kc == 0), stop=(kc == 15))
                    inc(mm, "TT")
            T(t19)
            ttn = n["TT"]
            wo_tt[s8] = ttn

            def v28(v, s8=s8, ttn=ttn):
                for u in range(4):
                    m2, th = u // 2, u % 2
                    v.wait_ge(sems["TT"], ttn - 4 + u + 1)
                    oc = s8 * 2 + m2
                    cp = v.tensor_scalar(afm[:, oc * SEG + th * 512:oc * SEG + th * 512 + 512],
                                         bk(u)[:, :], gacol[:, oc:oc + 1], 0.0,
                                         ALU.mult, ALU.add)
                    pb_free[u] = inc(cp, "VV")
            V(v28)
        bar()
        ckstop("wo", lnb)
        if DEBUG:
            def dmp2(sy):
                inc(sy.dma_start(out=dbg_og.ap(), in_=ogf[:, :]), "ST", 16)
                inc(sy.dma_start(out=dbg_afm.ap(), in_=afm[:, :]), "ST", 16)
                sy.wait_ge(sems["ST"], n["ST"])
            SY(dmp2)
            bar()

        sb_t = cpB[:, 1 * D:2 * D]
        cb1_t = cpB[:, 2 * D:3 * D]
        gb_t = cpB[:, 3 * D:4 * D]
        nw_t = cpB[:, 4 * D:5 * D]
        p6_done, p6_var, p6_sp = {}, {}, {}
        for ti in range(NT):
            xv = xt6[:, (ti % 2) * D:(ti % 2) * D + D]

            spn = f"XL{ti % 2}"

            def sy5(sy, ti=ti, xv=xv, spn=spn):
                if ti >= 2:
                    sy.wait_ge(sems["VV"], p6_done[ti - 2])
                inc(sy.dma_start(out=xv, in_=x3[ti]), spn, 16)
            SY(sy5)
            xln = (spn, n[spn])

            vv0 = n["VV"]
            tt0b = n["TT"]

            def t20(t, ti=ti, vv0=vv0):
                for j in range(16):
                    if j >= 2:
                        t.wait_ge(sems["VV"], vv0 + (j - 2) + 1)
                    else:
                        bnk = 6 + j % 2
                        if bnk in pb_free:
                            t.wait_ge(sems["VV"], pb_free[bnk])
                    tr = t.transpose(bkb(6 + j % 2)[:, 0:128],
                                     afm[:, j * SEG + ti * 128:j * SEG + (ti + 1) * 128],
                                     ident[:, :])
                    inc(tr, "TT")
            T(t20)

            def v29(v, ti=ti, xln=xln, tt0b=tt0b):
                v.wait_ge(sems[xln[0]], xln[1])
                if ti >= 1:
                    v.wait_ge(sems["SP"], p6_sp[ti - 1])
                for j in range(16):
                    v.wait_ge(sems["TT"], tt0b + j + 1)
                    xs_ = xt6[:, (ti % 2) * D + j * 128:(ti % 2) * D + (j + 1) * 128]
                    cnt = inc(v.tensor_tensor(lnb[:, j * 128:(j + 1) * 128],
                                              bkb(6 + j % 2)[:, 0:128], xs_, ALU.add), "VV")
                    pb_free[6 + j % 2] = cnt
                v.drain()
                inc(v.nop(), "VV")
            V(v29)
            x2n = n["VV"]

            def a10(a, ti=ti, x2n=x2n):
                a.wait_ge(sems["VV"], x2n)
                if ti >= 1:
                    a.wait_ge(sems["VV"], p6_var[ti - 1])
                a.activation(sqj[ti % 2][:, :], lnb[:, :], AF.Square, accum_out=colb[:, 2:3])
                a.drain()
                inc(a.nop(), "AA")
            A(a10)
            aan = n["AA"]

            def v30(v, x2n=x2n, aan=aan, ti=ti):
                v.wait_ge(sems["VV"], x2n)
                v.tensor_reduce(colb[:, 0:1], lnb[:, :], AX.X, ALU.add)
                v.drain()
                v.tensor_scalar(colb[:, 1:2], colb[:, 0:1], 1.0 / D, 0.0, ALU.mult, ALU.add)
                v.drain()
                v.tensor_tensor(colb[:, 4:5], colb[:, 1:2], colb[:, 1:2], ALU.mult)
                v.wait_ge(sems["AA"], aan)
                v.drain()
                v.scalar_tensor_tensor(colb[:, 5:6], colb[:, 2:3], 1.0 / D, colb[:, 4:5],
                                       ALU.mult, ALU.subtract)
                v.drain()
                p6_var[ti] = inc(v.nop(), "VV")
            V(v30)
            vvn = n["VV"]

            def a11(a, vvn=vvn):
                a.wait_ge(sems["VV"], vvn)
                a.activation(colb[:, 6:7], colb[:, 5:6], AF.Ln, bias=epsc[:, 0:1])
                a.drain()
                a.activation(colb[:, 7:8], colb[:, 6:7], AF.Exp, scale=-0.5)
                a.drain()
                inc(a.nop(), "AA")
            A(a11)
            aan2 = n["AA"]

            def v31(v, aan2=aan2):
                v.wait_ge(sems["AA"], aan2)
                v.scalar_tensor_tensor(colb[:, 8:9], colb[:, 1:2], -1.0, colb[:, 7:8],
                                       ALU.mult, ALU.mult)
                v.drain()
                v.tensor_scalar(lnb2[:, :], lnb[:, :], colb[:, 7:8], colb[:, 8:9],
                                ALU.mult, ALU.add)
                v.drain()
                v.tensor_tensor(lnb2[:, :], lnb2[:, :], cb1_t, ALU.mult)
                v.drain()
                v.tensor_tensor(lnb2[:, :], lnb2[:, :], sb_t, ALU.add)
                v.drain()
                inc(v.nop(), "VV")
            V(v31)
            h2n = n["VV"]

            def g7(g, h2n=h2n, ti=ti):
                g.wait_ge(sems["VV"], h2n)
                inc(g.tensor_tensor(sqj[ti % 2][:, :], lnb2[:, :], gb_t, ALU.mult), "GG")
            G(g7)
            ggn = n["GG"]

            def v32(v, ti=ti, ggn=ggn):
                v.wait_ge(sems["GG"], ggn)
                inc(v.tensor_tensor(lnb[:, :], lnb[:, :], sqj[ti % 2][:, :], ALU.add), "VV")
            V(v32)
            x2pn = n["VV"]

            def a12(a, ti=ti, x2pn=x2pn):
                a.wait_ge(sems["VV"], x2pn)
                p6_sp[ti] = inc(a.dma_start(out=x2s3[ti], in_=lnb[:, :]), "SP", 16)
                a.activation(sqj[ti % 2][:, :], lnb2[:, :], AF.Square,
                             accum_out=colb[:, 10:11])
                a.drain()
                inc(a.nop(), "AA")
            A(a12)
            aan3 = n["AA"]

            def v33(v, aan3=aan3):
                v.tensor_tensor(hrow6[:, :], lnb2[:, :], nw_t, ALU.mult)
                v.drain()
                v.tensor_reduce(colb[:, 12:13], hrow6[:, :], AX.X, ALU.max,
                                apply_absolute_value=True)
                v.wait_ge(sems["AA"], aan3)
                v.drain()
                inc(v.nop(), "VV")
            V(v33)
            vvz = n["VV"]

            def a13(a, vvz=vvz):
                a.wait_ge(sems["VV"], vvz)
                a.activation(colb[:, 13:14], colb[:, 10:11], AF.Ln, scale=1.0 / D,
                             bias=epsc[:, 0:1])
                a.drain()
                a.activation(colb[:, 13:14], colb[:, 13:14], AF.Exp, scale=-0.5)
                a.drain()
                inc(a.nop(), "AA")
            A(a13)
            aan4 = n["AA"]

            def v34(v, ti=ti, aan4=aan4):
                v.wait_ge(sems["AA"], aan4)
                v.tensor_tensor(colb[:, 14:15], colb[:, 12:13], colb[:, 13:14], ALU.mult)
                v.drain()
                v.tensor_scalar(colb[:, 14:15], colb[:, 14:15], 1e-5, 0.0, ALU.max, ALU.add)
                v.drain()
                v.reciprocal(colb[:, 15:16], colb[:, 14:15])
                v.drain()
                v.tensor_scalar(colb[:, 16:17], colb[:, 15:16], 127.0, 0.0, ALU.mult, ALU.add)
                v.tensor_scalar(qcols[:, ti:ti + 1], colb[:, 14:15], 1.0 / 127.0, 0.0,
                                ALU.mult, ALU.add)
                v.drain()
                v.tensor_tensor(colb[:, 17:18], colb[:, 13:14], colb[:, 16:17], ALU.mult)
                v.drain()
                v.tensor_scalar(qstage[:, :], hrow6[:, :], colb[:, 17:18], RBF,
                                ALU.mult, ALU.add)
                v.drain()
                inc(v.nop(), "VV")
            V(v34)
            vq = n["VV"]

            def g8(g, vq=vq):
                g.wait_ge(sems["VV"], vq)
                inc(g.tensor_scalar(qstage[:, :], qstage[:, :], RBF + 127.0, RBF - 127.0,
                                    ALU.min, ALU.max), "GG")
            G(g8)
            gq = n["GG"]

            def v35(v, gq=gq):
                v.wait_ge(sems["GG"], gq)
                inc(v.tensor_scalar(hstage[:, :], qstage[:, :], RBF, 0.0, ALU.subtract,
                                    ALU.add), "VV")
            V(v35)
            hqn = n["VV"]

            T(lambda t, hqn=hqn: t.wait_ge(sems["VV"], hqn))
            p6_done[ti] = tr_pipe(
                hstage,
                lambda v, j, ps, ti=ti: v.tensor_copy(
                    actq[:, j * SEG + ti * 128:j * SEG + (ti + 1) * 128], ps), 16)
        bar()
        ckstop("P6", lnb)
        esp6.close()
        espW.close()

        esp7 = ExitStack()
        sb7 = lambda n_, s_, dt: esp7.enter_context(nc.sbuf_tensor(n_, s_, dt))
        wffn = sb7("wffn", [128, 16384], BF16)
        acc = sb7("acc", [128, 16 * SEG], F32)
        midst = sb7("midst", [128, 1024], BF16)
        qmidt = sb7("qmidt", [128, 1024], BF16)
        qmidf = sb7("qmidf", [128, 1024], BF16)
        qmstage = sb7("qmstage", [128, 1024], F32)
        sgst = sb7("sgst", [128, 256], BF16)
        valst = sb7("valst", [128, 256], F32)
        xf = sb7("xf", [128, D], F32)
        mrow7 = sb7("mrow7", [1, D], F32)
        gbt = sb7("gbt", [128, D], BF16)

        def sy9(sy):
            inc(sy.dma_start(out=mrow7[0:1, :], in_=modscr.ap()[0:1, 5 * D:6 * D]),
                "CS", 16)
        SY(sy9)
        xgb = n["CS"]

        def t26(t, xgb=xgb):
            t.wait_ge(sems["CS"], xgb)
            for nn in range(4):
                mm = t.matmul(bk(4 + nn)[:, 0:512], onesr[0:1, :],
                              mrow7[0:1, nn * 512:(nn + 1) * 512], start=True, stop=True)
                inc(mm, "TT")
        T(t26)
        ttgb = n["TT"]

        def vgb(v, ttgb=ttgb):
            for nn in range(4):
                v.wait_ge(sems["TT"], ttgb - 4 + nn + 1)
                v.tensor_copy(gbt[:, nn * 512:(nn + 1) * 512], bk(4 + nn)[:, 0:512])
            v.drain()
            inc(v.nop(), "VV")
        V(vgb)

        def v37(v):
            for ti in range(NT):
                v.tensor_scalar(svcol[:, ti:ti + 1], qcols[:, ti:ti + 1], scw[:, 0:1],
                                0.0, ALU.mult, ALU.add)
            v.memset(acc[:, :], 0.0)
            v.drain()
            inc(v.nop(), "VV")
        V(v37)
        bar()

        gv_tt = {}
        last_sp = {}
        for B in range(32):
            wslot = wffn[:, (B % 2) * 8192:(B % 2) * 8192 + 8192]
            ws3 = wslot.rearrange("p (c j) -> p c j", j=512)

            spn = f"WF{B % 2}"

            def g9(g, B=B, ws3=ws3, spn=spn):
                if B >= 2:
                    g.wait_ge(sems["TT"], gv_tt[B - 2])
                srcv = wgvT_i.rearrange("(c p) j -> p c j", p=128)[:, :, B * 256:(B + 1) * 256]
                srcg = wgvT_i.rearrange("(c p) j -> p c j", p=128)[
                    :, :, HID + B * 256:HID + (B + 1) * 256]
                for q4 in range(4):
                    sl = slice(q4 * 4, (q4 + 1) * 4)
                    inc(g.dma_start(out=ws3[:, sl, 0:256], in_=srcv[:, sl, :]), spn, 16)
                    inc(g.dma_start(out=ws3[:, sl, 256:512], in_=srcg[:, sl, :]), spn, 16)
            G(g9)
            wln = (spn, n[spn])

            def wq1(v, ws3=ws3, wln=wln):
                v.wait_ge(sems[wln[0]], wln[1])
                inc(v.tensor_scalar(ws3[:, :, :], ws3[:, :, :], scw[:, 2:3], RB,
                                    ALU.mult, ALU.add), "VV")
            V(wq1)
            q1n = n["VV"]

            def wq2(g, ws3=ws3, q1n=q1n):
                g.wait_ge(sems["VV"], q1n)
                inc(g.tensor_scalar(ws3[:, :, :], ws3[:, :, :], RB + 1.0, RB - 1.0,
                                    ALU.min, ALU.max), "GG")
            G(wq2)
            q2n = n["GG"]

            def wq3(v, ws3=ws3, q2n=q2n):
                v.wait_ge(sems["GG"], q2n)
                inc(v.tensor_scalar(ws3[:, :, :], ws3[:, :, :], RB, 0.0, ALU.subtract,
                                    ALU.add), "VV")
            V(wq3)
            wqn = n["VV"]

            for ti in range(NT):
                bv, bg = ti % 2, 2 + ti % 2

                def t22(t, ti=ti, bv=bv, bg=bg, ws3=ws3, wln=wln, wqn=wqn):
                    t.wait_ge(sems["VV"], wqn)
                    for bnk in (bv, bg):
                        if bnk in pb_free:
                            t.wait_ge(sems["VV"], pb_free[bnk])
                        if (bnk, "a") in pb_free:
                            t.wait_ge(sems["AA"], pb_free[(bnk, "a")])
                    for kc in range(16):
                        t.matmul(bk(bv)[:, 0:256],
                                 actq[:, kc * SEG + ti * 128:kc * SEG + (ti + 1) * 128],
                                 ws3[:, kc, 0:256], start=(kc == 0), stop=(kc == 15))
                    for kc in range(16):
                        mm = t.matmul(bk(bg)[:, 0:256],
                                      actq[:, kc * SEG + ti * 128:kc * SEG + (ti + 1) * 128],
                                      ws3[:, kc, 256:512], start=(kc == 0), stop=(kc == 15))
                    inc(mm, "TT")
                T(t22)
                ttn = n["TT"]
                gv_tt[B] = ttn

                def a14(a, ti=ti, bv=bv, bg=bg, ttn=ttn):
                    a.wait_ge(sems["TT"], ttn)
                    a.activation(valst[:, :], bk(bv)[:, 0:256], AF.Copy,
                                 scale=svcol[:, ti:ti + 1])
                    cnt = inc(a.activation(sgst[:, :], bk(bg)[:, 0:256], AF.Silu,
                                           scale=svcol[:, ti:ti + 1]), "AA")
                    pb_free[(bv, "a")] = cnt
                    pb_free[(bg, "a")] = cnt
                A(a14)
                aan = n["AA"]

                def v38(v, ti=ti, B=B, aan=aan, bv=bv, bg=bg):
                    v.wait_ge(sems["AA"], aan)
                    if last_sp:
                        v.wait_ge(sems["SP"], last_sp["c"])
                    v.tensor_tensor(midst[:, 0:256], valst[:, :], sgst[:, :], ALU.mult)
                    v.drain()
                    cnt = inc(v.tensor_reduce(ambuf[:, ti * 32 + B:ti * 32 + B + 1],
                                              midst[:, 0:256], AX.X, ALU.max,
                                              apply_absolute_value=True), "VV")
                    pb_free[bv] = cnt
                    pb_free[bg] = cnt
                V(v38)
                vvm = n["VV"]

                def a15(a, ti=ti, B=B, vvm=vvm):
                    a.wait_ge(sems["VV"], vvm)
                    last_sp["c"] = inc(a.dma_start(out=mid3[ti][:, B * 256:(B + 1) * 256],
                                                   in_=midst[:, 0:256]), "SP", 16)
                A(a15)
        sp_mid = n["SP"]
        bar()
        ckstop("gv", acc)

        def v39(v):
            for ti in range(NT):
                v.tensor_reduce(colb[:, 20:21], ambuf[:, ti * 32:(ti + 1) * 32], AX.X, ALU.max)
                v.drain()
                v.tensor_scalar(colb[:, 20:21], colb[:, 20:21], 1e-5, 0.0, ALU.max, ALU.add)
                v.drain()
                v.reciprocal(colb[:, 21:22], colb[:, 20:21])
                v.drain()
                v.tensor_scalar(qcols2[:, ti:ti + 1], colb[:, 21:22], 127.0, 0.0,
                                ALU.mult, ALU.add)
                v.tensor_scalar(icols2[:, ti:ti + 1], colb[:, 20:21], 1.0 / 127.0, 0.0,
                                ALU.mult, ALU.add)
                v.drain()
                v.tensor_scalar(icols2[:, ti:ti + 1], icols2[:, ti:ti + 1], scw[:, 1:2],
                                0.0, ALU.mult, ALU.add)
                v.drain()
            inc(v.nop(), "VV")
        V(v39)
        bar()

        wo2_tt = {}
        for SB in range(8):
            ws3 = wffn[:, 0:16384].rearrange("p (c j) -> p c j", j=2048)

            def g10(g, SB=SB, ws3=ws3):
                if SB >= 1:
                    g.wait_ge(sems["TT"], wo2_tt[SB - 1])
                src = woutT_i.rearrange("(c p) j -> p c j", p=128)[:, SB * 8:(SB + 1) * 8, :]
                inc(g.dma_start(out=ws3[:, 0:4, :], in_=src[:, 0:4, :]), "WF0", 16)
                inc(g.dma_start(out=ws3[:, 4:8, :], in_=src[:, 4:8, :]), "WF0", 16)
            G(g10)
            wln = n["WF0"]

            def wq1b(v, ws3=ws3, wln=wln):
                v.wait_ge(sems["WF0"], wln)
                inc(v.tensor_scalar(ws3[:, :, :], ws3[:, :, :], scw[:, 3:4], RB,
                                    ALU.mult, ALU.add), "VV")
            V(wq1b)
            q1n = n["VV"]

            def wq2b(g, ws3=ws3, q1n=q1n):
                g.wait_ge(sems["VV"], q1n)
                inc(g.tensor_scalar(ws3[:, :, :], ws3[:, :, :], RB + 1.0, RB - 1.0,
                                    ALU.min, ALU.max), "GG")
            G(wq2b)
            q2n = n["GG"]

            def wq3b(v, ws3=ws3, q2n=q2n):
                v.wait_ge(sems["GG"], q2n)
                inc(v.tensor_scalar(ws3[:, :, :], ws3[:, :, :], RB, 0.0, ALU.subtract,
                                    ALU.add), "VV")
            V(wq3b)
            wqnb = n["VV"]

            for ti in range(NT):
                def sy6(sy, SB=SB, ti=ti):
                    if (SB, ti) == (0, 0):
                        sy.wait_ge(sems["SP"], sp_mid)
                    sy.wait_ge(sems["VV"], n["VV"])
                    inc(sy.dma_start(out=midst[:, :],
                                     in_=mid3[ti][:, SB * 1024:(SB + 1) * 1024]), "MD", 16)
                SY(sy6)
                xlm = n["MD"]

                def v40(v, ti=ti, xlm=xlm):
                    v.wait_ge(sems["MD"], xlm)
                    inc(v.tensor_scalar(qmstage[:, :], midst[:, :], qcols2[:, ti:ti + 1],
                                        RBF, ALU.mult, ALU.add), "VV")
                V(v40)
                vq1 = n["VV"]

                def g11(g, vq1=vq1):
                    g.wait_ge(sems["VV"], vq1)
                    inc(g.tensor_scalar(qmstage[:, :], qmstage[:, :], RBF + 127.0,
                                        RBF - 127.0, ALU.min, ALU.max), "GG")
                G(g11)
                gq1 = n["GG"]

                def v41(v, gq1=gq1):
                    v.wait_ge(sems["GG"], gq1)
                    inc(v.tensor_scalar(qmidt[:, :], qmstage[:, :], RBF, 0.0,
                                        ALU.subtract, ALU.add), "VV")
                V(v41)
                qn = n["VV"]

                T(lambda t, qn=qn: t.wait_ge(sems["VV"], qn))
                qfn = tr_pipe(qmidt, lambda v, j, ps: v.tensor_copy(
                    qmidf[:, j * 128:(j + 1) * 128], ps), 8)

                def t24(t, ws3=ws3, wln=wln, qfn=qfn, wqnb=wqnb):
                    t.wait_ge(sems["VV"], wqnb)
                    t.wait_ge(sems["VV"], qfn)
                    for nn in range(4):
                        if nn in pb_free:
                            t.wait_ge(sems["VV"], pb_free[nn])
                        for hc in range(8):
                            mm = t.matmul(bk(nn)[:, :], qmidf[:, hc * 128:(hc + 1) * 128],
                                          ws3[:, hc, nn * 512:(nn + 1) * 512],
                                          start=(hc == 0), stop=(hc == 7))
                        inc(mm, "TT")
                T(t24)
                ttn = n["TT"]
                wo2_tt[SB] = ttn

                def v43(v, ti=ti, ttn=ttn):
                    for nn in range(4):
                        v.wait_ge(sems["TT"], ttn - 4 + nn + 1)
                        ac = acc[:, ti * D + nn * 512:ti * D + (nn + 1) * 512]
                        pb_free[nn] = inc(v.tensor_tensor(ac, ac, bk(nn)[:, :], ALU.add), "VV")
                V(v43)
        bar()

        xl_prev = {}
        for ti in range(NT):
            at = acc[:, ti * D:(ti + 1) * D]

            def sy7(sy, ti=ti):
                if ti >= 1:
                    sy.wait_ge(sems["VV"], xl_prev[ti - 1])
                inc(sy.dma_start(out=xf[:, :], in_=x2s3[ti]), "XL0", 16)
            SY(sy7)
            xln = n["XL0"]

            def v44(v, ti=ti, at=at):
                inc(v.tensor_scalar(at, at, icols2[:, ti:ti + 1], 0.0, ALU.mult,
                                    ALU.add), "VV")
            V(v44)
            vo1 = n["VV"]

            def g12(g, vo1=vo1, at=at):
                g.wait_ge(sems["VV"], vo1)
                inc(g.tensor_tensor(at, at, gbt[:, :], ALU.mult), "GG")
            G(g12)
            go2 = n["GG"]

            def v45(v, ti=ti, xln=xln, go2=go2, at=at):
                v.wait_ge(sems["XL0"], xln)
                v.wait_ge(sems["GG"], go2)
                xl_prev[ti] = inc(v.tensor_tensor(at, at, xf[:, :], ALU.add), "VV")
            V(v45)
            vfin = n["VV"]

            def sy8(sy, ti=ti, vfin=vfin, at=at):
                sy.wait_ge(sems["VV"], vfin)
                inc(sy.dma_start(out=y3[ti], in_=at), "ST", 16)
            SY(sy8)
        stf = n["ST"]
        SY(lambda sy: sy.wait_ge(sems["ST"], stf))
        esp7.close()

    return nc


_NC_CACHE = {}


def kernel(L0_attn_x, L0_attn_Wq, L0_attn_Wk, L0_attn_Wv, L0_attn_Wgk1,
           L0_attn_Wgk2, L0_attn_bgk2, L0_attn_Wg, L0_attn_gnorm_w, L0_attn_Wo,
           L0_ffn_norm_w, L0_ffn_w_gv, L0_ffn_w_out, L0_c, L0_wmod):
    f32 = np.float32
    ca = lambda a: np.ascontiguousarray(np.asarray(a, f32))
    x = ca(L0_attn_x)
    wq, wk, wv, wg, wo = ca(L0_attn_Wq), ca(L0_attn_Wk), ca(L0_attn_Wv), \
        ca(L0_attn_Wg), ca(L0_attn_Wo)
    wgk1, wgk2, bgk2 = ca(L0_attn_Wgk1), ca(L0_attn_Wgk2), ca(L0_attn_bgk2)
    gnw, nw = ca(L0_attn_gnorm_w), ca(L0_ffn_norm_w)
    wgv, wout = ca(L0_ffn_w_gv), ca(L0_ffn_w_out)
    cc, wmod = ca(L0_c), ca(L0_wmod)

    wgvT = np.ascontiguousarray(wgv.T)
    woutT = np.ascontiguousarray(wout.T)
    bgk2c = np.ascontiguousarray(bgk2.reshape(16, 128).T)
    gnw2 = np.ascontiguousarray(np.tile(gnw, 2)[None, :])
    nwr = np.ascontiguousarray(nw[None, :])
    scanm = np.ones((128, SEG), f32)
    scanm[:, 0::64] = 0.0
    umask = np.ascontiguousarray(np.triu(np.ones((64, 64), f32)))
    ident = np.eye(128, dtype=f32)

    if "nc" not in _NC_CACHE:
        try:
            _NC_CACHE["nc"] = build_nc()
        except StopBuild:
            _NC_CACHE["nc"] = _LAST["nc"]
    nc = _NC_CACHE["nc"]

    in_maps = []
    for c in range(8):
        b, s = c // 4, c % 4
        smask = np.zeros((128, 8), f32)
        for j in range(8):
            if j // 4 == b and j % 4 < s:
                smask[:, j] = 1.0
        bsel = np.zeros((8, 8), f32)
        for r in range(8):
            if r // 4 == b:
                bsel[r, r % 4] = 1.0
            bsel[r, 4] = 1.0
        in_maps.append({
            "x_seg": np.ascontiguousarray(x[b, s * SEG:(s + 1) * SEG, :]),
            "cbc": np.ascontiguousarray(cc[b].reshape(16, 128).T),
            "wmod_q": np.ascontiguousarray(wmod[:, s * 3072:(s + 1) * 3072]),
            "wq": wq, "wk": wk, "wv": wv, "wg": wg, "wo": wo,
            "wgk1": wgk1, "wgk2": wgk2, "bgk2c": bgk2c, "gnw2": gnw2,
            "nw_row": nwr, "wgvT": wgvT, "woutT": woutT,
            "wgv_sh": np.ascontiguousarray(wgv[c * 2048:(c + 1) * 2048, :]),
            "wout_sh": np.ascontiguousarray(wout[c * 256:(c + 1) * 256, :]),
            "scanm": scanm, "umask": umask, "ident": ident,
            "smask": smask, "bsel": bsel,
        })
    res = run_bass_kernel_spmd(nc, in_maps, core_ids=list(range(8)))
    out = np.empty((2, 4096, D), f32)
    for c in range(8):
        b, s = c // 4, c % 4
        out[b, s * SEG:(s + 1) * SEG, :] = res.results[c]["y"]
    return out


# version 09
# speedup vs baseline: 1.0911x; 1.0676x over previous
import numpy as np
from contextlib import ExitStack

import concourse.bass as bass
import concourse.mybir as mybir
from concourse.bass_utils import run_bass_kernel_spmd

F32 = mybir.dt.float32
BF16 = mybir.dt.bfloat16
AF = mybir.ActivationFunctionType
ALU = mybir.AluOpType
AX = mybir.AxisListType

D = 2048
HD = 128
SEG = 1024
NT = 8
GH = 8
EPS = 1e-5
QSC = 0.08838834764831845
RB = 192.0
RBF = 12582912.0
HID = 8192
MODW = 12288
MROW = MODW + 16
DEBUG = False
STOP_AFTER = None


class StopBuild(Exception):
    pass


_LAST = {}


def build_nc():
    nc = bass.Bass("TRN2", target_bir_lowering=False)
    _LAST["nc"] = nc
    di = lambda n_, s_, dt=F32: nc.dram_tensor(n_, s_, dt, kind="ExternalInput").ap()

    x_in = di("x_seg", [SEG, D])
    cbc = di("cbc", [128, 16])
    wmod_q = di("wmod_q", [D, 3072])
    wq_i, wk_i, wv_i, wg_i, wo_i = (di(n_, [D, D]) for n_ in ("wq", "wk", "wv", "wg", "wo"))
    wgk1_i = di("wgk1", [D, 16])
    wgk2_i = di("wgk2", [16, D])
    bgk2_i = di("bgk2c", [128, 16])
    gnw_i = di("gnw2", [1, 256])
    nw_i = di("nw_row", [1, D])
    wgvT_i = di("wgvT", [D, 2 * HID])
    woutT_i = di("woutT", [HID, D])
    wgvsh_i = di("wgv_sh", [D, D])
    woutsh_i = di("wout_sh", [256, HID])
    scanm_i = di("scanm", [128, SEG])
    umask_i = di("umask", [64, 64])
    ident_i = di("ident", [128, 128])
    smask_i = di("smask", [128, 8])
    bsel_i = di("bsel", [8, 8])
    y_out = nc.dram_tensor("y", [SEG, D], F32, kind="ExternalOutput").ap()

    cc1i = nc.dram_tensor("cc1i", [1, MROW], F32)
    cc1o = nc.dram_tensor("cc1o", [8, MROW], F32)
    cc2i = [nc.dram_tensor(f"cc2i{g}", [128, 1056], F32) for g in range(2)]
    cc2o = [nc.dram_tensor(f"cc2o{g}", [1024, 1056], F32) for g in range(2)]
    modscr = nc.dram_tensor("modscr", [1, MROW], F32)
    if DEBUG:
        dbg_h1 = nc.dram_tensor("dbg_h1", [128, 16 * SEG], BF16)
        dbg_og = nc.dram_tensor("dbg_og", [128, 16 * SEG], BF16)
        dbg_afm = nc.dram_tensor("dbg_afm", [128, 16 * SEG], BF16)
    x2scr = nc.dram_tensor("x2scr", [SEG, D], F32)
    midscr = nc.dram_tensor("midscr", [SEG, HID], BF16)

    x3 = x_in.rearrange("(t p) n -> t p n", p=128)
    y3 = y_out.rearrange("(t p) n -> t p n", p=128)
    x2s3 = x2scr.ap().rearrange("(t p) n -> t p n", p=128)
    mid3 = midscr.ap().rearrange("(t p) n -> t p n", p=128)

    es0 = ExitStack()
    sb = lambda n_, s_, dt: es0.enter_context(nc.sbuf_tensor(n_, s_, dt))
    pbank = [es0.enter_context(nc.psum_tensor(f"bk{i}", [128, 512], F32)) for i in range(8)]
    bk = lambda i: pbank[i].ap()
    bkb = lambda i: pbank[i].ap().bitcast(BF16)

    actq = sb("actq", [128, 16 * SEG], BF16)
    scanms = sb("scanms", [128, SEG], F32)
    umasks = sb("umasks", [64, 64], BF16)
    ident = sb("ident_s", [128, 128], BF16)
    smasks = sb("smasks", [128, 8], F32)
    bsels = sb("bsels", [8, 8], F32)
    bgk2s = sb("bgk2s", [128, 16], F32)
    cbs = sb("cbs", [128, 16], F32)
    gnwt = sb("gnwt", [128, 256], BF16)
    epsc = sb("epsc", [128, 1], F32)
    onesr = sb("onesr", [1, 128], F32)
    onescl = sb("onescl", [128, 1], F32)
    sbb16 = sb("sbb16", [128, 16], BF16)
    colb = sb("colb", [128, 32], F32)
    qcols = sb("qcols", [128, NT], F32)
    qcols2 = sb("qcols2", [128, NT], F32)
    icols2 = sb("icols2", [128, NT], F32)
    svcol = sb("svcol", [128, NT], F32)
    ambuf = sb("ambuf", [128, NT * 32], F32)
    scw = sb("scw", [128, 4], F32)
    gacol = sb("gacol", [128, 16], F32)
    rbneg = sb("rbneg", [128, 1], F32)

    espW = ExitStack()
    ogf = espW.enter_context(nc.sbuf_tensor("ogf", [128, 16 * SEG], BF16))
    wring = espW.enter_context(nc.sbuf_tensor("wring", [128, 2 * 4096], BF16))
    esp01 = ExitStack()
    cpA = esp01.enter_context(nc.sbuf_tensor("cpA", [128, 2 * D], BF16))

    sems = {k: es0.enter_context(nc.semaphore(k))
            for k in ["WL0", "WL1", "WF0", "WF1", "XL0", "XL1", "CN", "CS", "RNK",
                      "MD", "SP", "ST", "TT", "VV", "AA", "GG", "CC", "PB"]}
    n = {k: 0 for k in sems}

    with es0, nc.Block() as blk:
        T = lambda f: blk.tensor(f)
        V = lambda f: blk.vector(f)
        A = lambda f: blk.scalar(f)
        G = lambda f: blk.gpsimd(f)
        SY = lambda f: blk.sync(f)

        def inc(instr, nm, k=1):
            instr.then_inc(sems[nm], k)
            n[nm] += k
            return n[nm]

        def bar():
            dma_owned = {"gpsimd": ["WL0", "WL1", "WF0", "WF1", "CN"],
                         "sync": ["XL0", "XL1", "CS", "RNK", "MD", "ST"],
                         "scalar": ["SP"]}
            for attr in ("tensor", "vector", "scalar", "gpsimd", "sync"):
                def _f(e, attr=attr):
                    if attr in ("vector", "scalar", "gpsimd"):
                        e.drain()
                    for s_ in dma_owned.get(attr, []):
                        if n[s_]:
                            e.wait_ge(sems[s_], n[s_])
                    inc(e.nop(), "PB")
                getattr(blk, attr)(_f)
            tgt = n["PB"]
            for attr in ("tensor", "vector", "scalar", "gpsimd", "sync"):
                getattr(blk, attr)(lambda e, tgt=tgt: e.wait_ge(sems["PB"], tgt))

        pb_free = {}

        def ckstop(tag, zb=None):
            if STOP_AFTER is not None and tag == STOP_AFTER:
                def vz(v):
                    v.memset(zb[:, 0:1024], 0.0)
                    v.drain()
                    inc(v.nop(), "VV")
                V(vz)
                vzn = n["VV"]

                def fin(sy):
                    sy.wait_ge(sems["VV"], vzn)
                    for ti in range(NT):
                        inc(sy.dma_start(out=y3[ti][:, 0:1024], in_=zb[:, 0:1024]), "ST", 16)
                        inc(sy.dma_start(out=y3[ti][:, 1024:2048], in_=zb[:, 0:1024]), "ST", 16)
                    sy.wait_ge(sems["ST"], n["ST"])
                SY(fin)
                raise StopBuild()

        def tr_pipe(src_tile, dst_fn, nblk, bank0=6):
            vv0 = n["VV"]
            tt0 = n["TT"]

            def tf(t):
                for j in range(nblk):
                    if j >= 2:
                        t.wait_ge(sems["VV"], vv0 + (j - 2) + 1)
                    else:
                        bnk = bank0 + j % 2
                        if bnk in pb_free:
                            t.wait_ge(sems["VV"], pb_free[bnk])
                    tr = t.transpose(bkb(bank0 + j % 2)[:, 0:128],
                                     src_tile[:, j * 128:(j + 1) * 128], ident[:, :])
                    inc(tr, "TT")
            T(tf)

            def vf(v):
                for j in range(nblk):
                    v.wait_ge(sems["TT"], tt0 + j + 1)
                    cp = dst_fn(v, j, bkb(bank0 + j % 2)[:, 0:128])
                    cnt = inc(cp, "VV")
                    pb_free[bank0 + j % 2] = cnt
            V(vf)
            return n["VV"]

        esp0 = ExitStack()
        sb0 = lambda n_, s_, dt: esp0.enter_context(nc.sbuf_tensor(n_, s_, dt))
        modrow = sb0("modrow", [1, MROW], F32)
        g1s = sb0("g1s", [8, 3104], F32)
        xt0 = sb0("xt0", [128, 2 * D], F32)
        wsum = sb0("wsum", [128, 32], F32)
        wmslot = sb0("wmslot", [128, 2 * 3072], F32)

        def g0(g):
            for dst, src in [(scanms[:, :], scanm_i), (umasks[:, :], umask_i),
                             (ident[:, :], ident_i), (smasks[:, :], smask_i),
                             (bsels[:, :], bsel_i), (bgk2s[:, :], bgk2_i),
                             (cbs[:, :], cbc), (gnwt[0:1, :], gnw_i)]:
                inc(g.dma_start(out=dst, in_=src), "CN", 16)
        G(g0)
        cn_c = n["CN"]

        V(lambda v: (v.memset(epsc[:, :], EPS), v.memset(onesr[:, :], 1.0),
                     v.memset(onescl[:, :], 1.0), v.memset(rbneg[:, :], -RB),
                     v.drain(), inc(v.nop(), "VV")))

        def a0(a):
            a.wait_ge(sems["CN"], cn_c)
            a.activation(colb[:, 0:16], cbs[:, :], AF.Silu)
            a.drain()
            inc(a.nop(), "AA")
        A(a0)
        aa0 = n["AA"]
        V(lambda v: (v.wait_ge(sems["AA"], aa0), inc(v.nop(), "VV")))
        vv_scb = n["VV"]

        p0tt = {}
        wm = wmod_q.rearrange("(c p) j -> c p j", p=128)
        for c in range(16):
            sp_ = f"WL{c % 2}"

            def g1(g, c=c, sp_=sp_):
                if c >= 2:
                    g.wait_ge(sems["TT"], p0tt[c - 2])
                slot = wmslot[:, (c % 2) * 3072:(c % 2) * 3072 + 3072]
                inc(g.dma_start(out=slot, in_=wm[c]), sp_, 16)
            G(g1)
            wlc = n[sp_]

            def t0(t, c=c, sp_=sp_, wlc=wlc):
                if c == 0:
                    t.wait_ge(sems["VV"], vv_scb)
                t.wait_ge(sems[sp_], wlc)
                slot = wmslot[:, (c % 2) * 3072:(c % 2) * 3072 + 3072]
                for nn in range(6):
                    mm = t.matmul(bk(nn)[0:1, :], colb[:, c:c + 1],
                                  slot[:, nn * 512:(nn + 1) * 512],
                                  start=(c == 0), stop=(c == 15))
                p0tt[c] = inc(mm, "TT")
            T(t0)
        tt_mod = n["TT"]

        gv = wgvsh_i.rearrange("(c p) j -> c p j", p=128)
        wo4 = woutsh_i.rearrange("(c p) (d j) -> c d p j", p=128, j=D)
        wred_vv = {}
        for i in range(24):
            sp_ = f"XL{i % 2}"

            def sy0(sy, i=i, sp_=sp_):
                if i >= 2:
                    sy.wait_ge(sems["VV"], wred_vv[i - 2])
                src = gv[i] if i < 16 else wo4[(i - 16) // 4, (i - 16) % 4]
                inc(sy.dma_start(out=xt0[:, (i % 2) * D:(i % 2) * D + D], in_=src), sp_, 16)
            SY(sy0)
            xlc = n[sp_]

            def v0(v, i=i, sp_=sp_, xlc=xlc):
                v.wait_ge(sems[sp_], xlc)
                wred_vv[i] = inc(
                    v.tensor_reduce(wsum[:, i:i + 1], xt0[:, (i % 2) * D:(i % 2) * D + D],
                                    AX.X, ALU.add, apply_absolute_value=True), "VV")
            V(v0)

        def v0b(v):
            v.drain()
            v.tensor_reduce(wsum[:, 30:31], wsum[:, 0:16], AX.X, ALU.add)
            v.tensor_reduce(wsum[:, 31:32], wsum[:, 16:24], AX.X, ALU.add)
            v.drain()
            inc(v.nop(), "VV")
        V(v0b)
        vv_ws = n["VV"]

        def t1(t):
            t.wait_ge(sems["VV"], vv_ws)
            inc(t.matmul(bk(6)[0:1, 0:2], onescl[:, 0:1], wsum[:, 30:32],
                         start=True, stop=True), "TT")
        T(t1)
        tt_ws = n["TT"]

        def v1(v):
            v.memset(modrow[:, :], 0.0)
            v.drain()
            v.wait_ge(sems["TT"], tt_mod)
            for nn in range(6):
                v.tensor_copy(modrow[0:1, nn * 512:(nn + 1) * 512], bk(nn)[0:1, :])
            v.wait_ge(sems["TT"], tt_ws)
            v.tensor_copy(modrow[0:1, MODW:MODW + 2], bk(6)[0:1, 0:2])
            v.drain()
            inc(v.nop(), "VV")
        V(v1)
        vv_row = n["VV"]

        def g2(g):
            g.wait_ge(sems["VV"], vv_row)
            inc(g.dma_start(out=cc1i.ap(), in_=modrow[0:1, :]), "CN", 16)
            g.wait_ge(sems["CN"], n["CN"])
            inc(g.collective_compute(
                "AllGather", ALU.bypass, replica_groups=[list(range(8))],
                ins=[cc1i.ap().opt()], outs=[cc1o.ap().opt()]), "CC")
            g.wait_ge(sems["CC"], n["CC"])
            inc(g.dma_start(out=g1s[:, 0:3072], in_=cc1o.ap()[:, 0:3072]), "CN", 16)
            inc(g.dma_start(out=g1s[:, 3072:3074], in_=cc1o.ap()[:, MODW:MODW + 2]),
                "CN", 16)
        G(g2)
        wl_g1 = n["CN"]

        for q in range(4):
            def t2(t, q=q):
                if q == 0:
                    t.wait_ge(sems["CN"], wl_g1)
                else:
                    t.wait_ge(sems["VV"], n["VV"])
                for nn in range(6):
                    mm = t.matmul(bk(nn)[0:1, :], bsels[:, q:q + 1],
                                  g1s[:, nn * 512:(nn + 1) * 512], start=True, stop=True)
                inc(mm, "TT")
            T(t2)
            ttq = n["TT"]

            def v2(v, q=q, ttq=ttq):
                v.wait_ge(sems["TT"], ttq)
                for nn in range(6):
                    v.tensor_copy(modrow[0:1, q * 3072 + nn * 512:q * 3072 + (nn + 1) * 512],
                                  bk(nn)[0:1, :])
                v.drain()
                inc(v.nop(), "VV")
            V(v2)

        def t2b(t):
            t.wait_ge(sems["VV"], n["VV"])
            inc(t.matmul(bk(6)[0:1, 0:2], bsels[:, 4:5], g1s[:, 3072:3074],
                         start=True, stop=True), "TT")
        T(t2b)
        tt_tot = n["TT"]

        def v3(v):
            v.wait_ge(sems["TT"], tt_tot)
            v.tensor_copy(modrow[0:1, MODW:MODW + 2], bk(6)[0:1, 0:2])
            v.drain()
            v.tensor_scalar(modrow[0:1, MODW + 2:MODW + 3], modrow[0:1, MODW:MODW + 1],
                            1.0 / (2.0 * HID * D), 0.0, ALU.mult, ALU.add)
            v.tensor_scalar(modrow[0:1, MODW + 3:MODW + 4], modrow[0:1, MODW + 1:MODW + 2],
                            1.0 / (1.0 * HID * D), 0.0, ALU.mult, ALU.add)
            v.drain()
            for base in (2 * D, 5 * D):
                r0 = modrow[0:1, base:base + D]
                v.tensor_scalar(xt0[0:1, 0:D], r0, -1.0, 0.25, ALU.mult, ALU.max)
                v.drain()
                v.tensor_tensor(xt0[0:1, 0:D], r0, xt0[0:1, 0:D], ALU.max)
                v.tensor_scalar(xt0[0:1, D:2 * D], r0, 0.0, 2.0, ALU.is_ge, ALU.mult)
                v.drain()
                v.scalar_tensor_tensor(r0, xt0[0:1, D:2 * D], 1.0, xt0[0:1, 0:D],
                                       ALU.subtract, ALU.mult)
                v.drain()
            for base in (1 * D, 4 * D):
                v.tensor_scalar(modrow[0:1, base:base + D], modrow[0:1, base:base + D],
                                1.0, 0.0, ALU.add, ALU.add)
            v.drain()
            inc(v.nop(), "VV")
        V(v3)
        vv_mod = n["VV"]

        def sy1(a):
            a.wait_ge(sems["VV"], vv_mod)
            inc(a.dma_start(out=modscr.ap(), in_=modrow[0:1, :]), "SP", 16)
        A(sy1)
        sp_modscr = n["SP"]

        def bcast_row(row_ap, dst_tile, nparts=4):
            ttb = n["TT"]

            def tb(t):
                t.wait_ge(sems["VV"], n["VV"])
                for nn in range(nparts):
                    mm = t.matmul(bk(nn)[:, 0:512], onesr[0:1, :],
                                  row_ap[0:1, nn * 512:(nn + 1) * 512], start=True, stop=True)
                inc(mm, "TT")
            T(tb)
            ttb = n["TT"]

            def vb(v):
                v.wait_ge(sems["TT"], ttb)
                for nn in range(nparts):
                    v.tensor_copy(dst_tile[:, nn * 512:(nn + 1) * 512], bk(nn)[:, 0:512])
                v.drain()
                inc(v.nop(), "VV")
            V(vb)

        def tg(t):
            t.wait_ge(sems["VV"], vv_mod)
            inc(t.matmul(bk(7)[:, 0:2], onesr[0:1, :], modrow[0:1, MODW + 2:MODW + 4],
                         start=True, stop=True), "TT")
        T(tg)
        tt_s = n["TT"]

        def vg(v):
            v.wait_ge(sems["TT"], tt_s)
            v.tensor_copy(scw[:, 0:2], bk(7)[:, 0:2])
            v.drain()
            v.reciprocal(scw[:, 2:3], scw[:, 0:1])
            v.drain()
            v.reciprocal(scw[:, 3:4], scw[:, 1:2])
            v.drain()
            inc(v.nop(), "VV")
        V(vg)

        bcast_row(modrow[0:1, 0:D], cpA[:, 0:D])
        bcast_row(modrow[0:1, D:2 * D], cpA[:, D:2 * D])

        def sy2(sy):
            sy.wait_ge(sems["VV"], n["VV"])
            inc(sy.dma_start(out=xt0[0:1, 0:256], in_=gnw_i), "CS", 16)
        SY(sy2)
        xl_g = n["CS"]

        def tgn(t):
            t.wait_ge(sems["CS"], xl_g)
            t.wait_ge(sems["VV"], n["VV"])
            inc(t.matmul(bk(7)[:, 0:256], onesr[0:1, :], xt0[0:1, 0:256],
                         start=True, stop=True), "TT")
        T(tgn)
        tt_gn = n["TT"]

        def vgn(v):
            v.wait_ge(sems["TT"], tt_gn)
            v.tensor_copy(gnwt[:, :], bk(7)[:, 0:256])
            v.drain()
            inc(v.nop(), "VV")
        V(vgn)
        bar()
        ckstop("P0", xt0)
        esp0.close()

        esp1 = ExitStack()
        sb1 = lambda n_, s_, dt: esp1.enter_context(nc.sbuf_tensor(n_, s_, dt))
        xt = sb1("xt", [128, 2 * D], F32)
        lnb = sb1("lnb", [128, D], F32)
        lnb2 = sb1("lnb2", [128, D], F32)
        sqj = [sb1(f"sqj{i}", [128, D], F32) for i in range(2)]
        hstage = sb1("hstage", [128, D], BF16)

        sa_t, ca1_t = cpA[:, 0:D], cpA[:, D:2 * D]

        def ln_tile(ti, xv, xln, cmul_t, cadd_t, done_d, var_d):
            def a1(a, ti=ti, xv=xv, xln=xln):
                a.wait_ge(sems[xln[0]], xln[1])
                if ti >= 1:
                    a.wait_ge(sems["VV"], var_d[ti - 1])
                a.activation(sqj[ti % 2][:, :], xv, AF.Square, accum_out=colb[:, 2:3])
                a.drain()
                inc(a.nop(), "AA")
            A(a1)
            aan = n["AA"]

            def v5(v, xv=xv, xln=xln, aan=aan, ti=ti):
                v.wait_ge(sems[xln[0]], xln[1])
                v.tensor_reduce(colb[:, 0:1], xv, AX.X, ALU.add)
                v.drain()
                v.tensor_scalar(colb[:, 1:2], colb[:, 0:1], 1.0 / D, 0.0, ALU.mult, ALU.add)
                v.drain()
                v.tensor_tensor(colb[:, 4:5], colb[:, 1:2], colb[:, 1:2], ALU.mult)
                v.wait_ge(sems["AA"], aan)
                v.drain()
                v.scalar_tensor_tensor(colb[:, 5:6], colb[:, 2:3], 1.0 / D, colb[:, 4:5],
                                       ALU.mult, ALU.subtract)
                v.drain()
                var_d[ti] = inc(v.nop(), "VV")
            V(v5)
            vvn = n["VV"]

            def a2(a, vvn=vvn):
                a.wait_ge(sems["VV"], vvn)
                a.activation(colb[:, 6:7], colb[:, 5:6], AF.Ln, bias=epsc[:, 0:1])
                a.drain()
                a.activation(colb[:, 7:8], colb[:, 6:7], AF.Exp, scale=-0.5)
                a.drain()
                inc(a.nop(), "AA")
            A(a2)
            aan2 = n["AA"]

            def v6(v, xv=xv, aan2=aan2):
                v.wait_ge(sems["AA"], aan2)
                v.scalar_tensor_tensor(colb[:, 8:9], colb[:, 1:2], -1.0, colb[:, 7:8],
                                       ALU.mult, ALU.mult)
                v.drain()
                v.tensor_scalar(lnb[:, :], xv, colb[:, 7:8], colb[:, 8:9], ALU.mult, ALU.add)
                v.drain()
                v.tensor_tensor(lnb2[:, :], lnb[:, :], cmul_t, ALU.mult)
                v.drain()
                v.tensor_tensor(hstage[:, :], lnb2[:, :], cadd_t, ALU.add)
                v.drain()
                inc(v.nop(), "VV")
            V(v6)
            return n["VV"]

        p1_done, p1_var = {}, {}
        for ti in range(NT):
            xv = xt[:, (ti % 2) * D:(ti % 2) * D + D]

            sp_ = f"XL{ti % 2}"

            def syl(sy, ti=ti, xv=xv, sp_=sp_):
                if ti >= 2:
                    sy.wait_ge(sems["VV"], p1_done[ti - 2])
                inc(sy.dma_start(out=xv, in_=x3[ti]), sp_, 16)
            SY(syl)
            xln = (sp_, n[sp_])
            h1n = ln_tile(ti, xv, xln, ca1_t, sa_t, p1_done, p1_var)

            T(lambda t, h1n=h1n: t.wait_ge(sems["VV"], h1n))
            p1_done[ti] = tr_pipe(
                hstage,
                lambda v, j, ps, ti=ti: v.tensor_copy(
                    actq[:, j * SEG + ti * 128:j * SEG + (ti + 1) * 128], ps), 16)
        bar()
        ckstop("P1", xt)
        if DEBUG:
            def dmp1(sy):
                inc(sy.dma_start(out=dbg_h1.ap(), in_=actq[:, :]), "ST", 16)
                sy.wait_ge(sems["ST"], n["ST"])
            SY(dmp1)
            bar()
        esp1.close()
        esp01.close()

        espA = ExitStack()
        sbA = lambda n_, s_, dt: espA.enter_context(nc.sbuf_tensor(n_, s_, dt))
        qf = sbA("qf", [128, GH * SEG], BF16)
        kf = sbA("kf", [128, GH * SEG], BF16)
        Gf = sbA("Gf", [128, GH * SEG], BF16)
        vtm = sbA("vtm", [128, GH * SEG], BF16)
        sgtm = sbA("sgtm", [128, GH * SEG], BF16)
        EG = sbA("EG", [128, SEG], F32)
        EGi = sbA("EGi", [128, SEG], F32)
        tmpb = None
        hrow = sbA("hrow", [128, SEG], F32)
        hrow2 = sbA("hrow2", [128, SEG], F32)
        qhb = sbA("qhb", [128, SEG], BF16)
        khb = sbA("khb", [128, SEG], BF16)
        kwt = sbA("kwt", [128, 1280], BF16)
        Sst = sbA("Sst", [128, GH * HD], F32)
        Scur = sbA("Scur", [128, HD], F32)
        sgbuf = sbA("sgbuf", [128, HD], BF16)
        s2buf = sbA("s2buf", [128, HD], F32)
        otm = hrow2
        atb = sbA("atb", [128, 64], BF16)
        zbA = sbA("zbA", [128, 16], F32)
        zbB = sbA("zbB", [128, 16], F32)
        glc = sbA("glc", [128, 16], F32)
        ssq = sbA("ssq", [128, 8], F32)
        ctile = sbA("ctile", [128, 1056], F32)
        ogstage = sbA("ogstage", [128, SEG], BF16)
        kwb = ogstage
        tmpb = EG[:, :].bitcast(BF16)[0:16, 0:1024]
        state = {"tmpb_vv": 0, "ttog_prev": 0}
        V(lambda v: (v.memset(ctile[:, 1032:1056], 0.0), v.drain(), inc(v.nop(), "VV")))

        def proj(grp):
            co = grp * 1024
            if True:
                def gld(g):
                    wg1 = wgk1_i.rearrange("(c p) r -> p c r", p=128)
                    dst = kwt[:, 0:256].rearrange("p (c r) -> p c r", r=16)
                    inc(g.dma_start(out=dst, in_=wg1), "CN", 16)
                G(gld)
                wl0 = n["CN"]

                def t6(t):
                    t.wait_ge(sems["CN"], wl0)
                    for th in range(2):
                        for c in range(16):
                            mm = t.matmul(bk(6 + th)[0:16, :], kwt[:, c * 16:(c + 1) * 16],
                                          actq[:, c * SEG + th * 512:c * SEG + th * 512 + 512],
                                          start=(c == 0), stop=(c == 15))
                        inc(mm, "TT")
                T(t6)
                tt6 = n["TT"]

                def v8(v):
                    v.wait_ge(sems["TT"], tt6)
                    v.tensor_copy(tmpb[0:16, 0:512], bk(6)[0:16, :])
                    v.tensor_copy(tmpb[0:16, 512:1024], bk(7)[0:16, :])
                    v.drain()
                    state["tmpb_vv"] = inc(v.nop(), "VV")
                V(v8)

            def gld2(g, grp=grp):
                g.wait_ge(sems["VV"], state["tmpb_vv"])
                inc(g.dma_start(out=kwt[0:16, 256:1280], in_=wgk2_i[:, co:co + 1024]), "CN", 16)
            G(gld2)
            wlgk = n["CN"]
            wgk2b = kwt[0:16, 256:1280]

            gk_aa, gk_vv = {}, {}
            for hh in range(GH):
                def t7(t, hh=hh):
                    t.wait_ge(sems["CN"], wlgk)
                    if hh >= 1:
                        t.wait_ge(sems["VV"], gk_vv[hh - 1])
                    for th in range(2):
                        mm = t.matmul(bk(6 + th)[:, :], wgk2b[:, hh * 128:(hh + 1) * 128],
                                      tmpb[0:16, th * 512:(th + 1) * 512], start=True, stop=True)
                    inc(mm, "TT")
                T(t7)
                ttg = n["TT"]

                def a3(a, hh=hh, ttg=ttg, grp=grp):
                    a.wait_ge(sems["TT"], ttg)
                    if hh >= 1:
                        a.wait_ge(sems["VV"], gk_vv[hh - 1])
                    col = bgk2s[:, grp * 8 + hh:grp * 8 + hh + 1]
                    for th in range(2):
                        sl = hrow[:, th * 512:(th + 1) * 512]
                        a.activation(sl, bk(6 + th)[:, :], AF.Abs, bias=col)
                        a.drain()
                        a.activation(sl, sl, AF.Exp, scale=-1.0)
                        a.drain()
                        a.activation(sl, sl, AF.Ln, bias=1.0)
                        a.drain()
                    gk_aa[hh] = inc(a.nop(), "AA")
                A(a3)

                def v9(v, hh=hh, grp=grp):
                    v.wait_ge(sems["AA"], gk_aa[hh])
                    col = bgk2s[:, grp * 8 + hh:grp * 8 + hh + 1]
                    for th in range(2):
                        v.tensor_scalar(hrow2[:, th * 512:(th + 1) * 512], bk(6 + th)[:, :],
                                        col, 0.0, ALU.add, ALU.min)
                    v.drain()
                    v.tensor_tensor(hrow2[:, :], hrow2[:, :], hrow[:, :], ALU.subtract)
                    v.drain()
                    v.tensor_tensor_scan(Gf[:, hh * SEG:(hh + 1) * SEG], scanms[:, :],
                                         hrow2[:, :], 0.0, ALU.mult, ALU.add)
                    v.drain()
                    gk_vv[hh] = inc(v.nop(), "VV")
                V(v9)

            strip_tt = {}
            for mi, (mname, wsrc) in enumerate(
                    [("q", wq_i), ("k", wk_i), ("v", wv_i), ("g", wg_i)]):
                for s4 in range(4):
                    sidx = mi * 4 + s4
                    slot = wring[:, (sidx % 2) * 4096:(sidx % 2) * 4096 + 4096]
                    c0 = co + s4 * 256

                    spn = f"WL{sidx % 2}"

                    def gld3(g, wsrc=wsrc, slot=slot, c0=c0, sidx=sidx, spn=spn):
                        if sidx >= 2:
                            g.wait_ge(sems["TT"], strip_tt[sidx - 2])
                        src = wsrc.rearrange("(c p) j -> p c j", p=128)[:, :, c0:c0 + 256]
                        dst = slot.rearrange("p (c j) -> p c j", j=256)
                        for q4 in range(4):
                            inc(g.dma_start(out=dst[:, q4 * 4:(q4 + 1) * 4, :],
                                            in_=src[:, q4 * 4:(q4 + 1) * 4, :]), spn, 16)
                    G(gld3)
                    wln = (spn, n[spn])

                    if mname in ("q", "k"):
                        dstb = qf if mname == "q" else kf

                        def t8(t, slot=slot, wln=wln):
                            t.wait_ge(sems[wln[0]], wln[1])
                            for u in range(4):
                                m2, th = u // 2, u % 2
                                if u in pb_free:
                                    t.wait_ge(sems["VV"], pb_free[u])
                                for kc in range(16):
                                    mm = t.matmul(
                                        bk(u)[:, :],
                                        slot[:, kc * 256 + m2 * 128:kc * 256 + m2 * 128 + 128],
                                        actq[:, kc * SEG + th * 512:kc * SEG + th * 512 + 512],
                                        start=(kc == 0), stop=(kc == 15))
                                inc(mm, "TT")
                        T(t8)
                        ttn = n["TT"]
                        strip_tt[sidx] = ttn

                        def v10(v, s4=s4, ttn=ttn, dstb=dstb, mname=mname):
                            for u in range(4):
                                m2, th = u // 2, u % 2
                                v.wait_ge(sems["TT"], ttn - 4 + u + 1)
                                hcol = (s4 * 2 + m2) * SEG + th * 512
                                if mname == "q":
                                    cp = v.tensor_scalar(dstb[:, hcol:hcol + 512], bk(u)[:, :],
                                                         QSC, 0.0, ALU.mult, ALU.add)
                                else:
                                    cp = v.tensor_copy(dstb[:, hcol:hcol + 512], bk(u)[:, :])
                                pb_free[u] = inc(cp, "VV")
                        V(v10)
                    else:
                        vvb9 = n["VV"]
                        aab9 = n["AA"]
                        is_v = (mname == "v")

                        def t9(t, slot=slot, wln=wln, vvb9=vvb9, aab9=aab9, is_v=is_v):
                            t.wait_ge(sems[wln[0]], wln[1])
                            for ti in range(NT):
                                bnk = ti % 4
                                if ti < 4:
                                    if bnk in pb_free:
                                        t.wait_ge(sems["VV"], pb_free[bnk])
                                    if (bnk, "a") in pb_free:
                                        t.wait_ge(sems["AA"], pb_free[(bnk, "a")])
                                elif is_v:
                                    t.wait_ge(sems["VV"], vvb9 + (ti - 4) + 1)
                                else:
                                    t.wait_ge(sems["AA"], aab9 + (ti - 4) + 1)
                                for kc in range(16):
                                    mm = t.matmul(
                                        bk(bnk)[:, 0:256],
                                        actq[:, kc * SEG + ti * 128:kc * SEG + (ti + 1) * 128],
                                        slot[:, kc * 256:(kc + 1) * 256],
                                        start=(kc == 0), stop=(kc == 15))
                                inc(mm, "TT")
                        T(t9)
                        ttn = n["TT"]
                        strip_tt[sidx] = ttn

                        if mname == "v":
                            def v11(v, s4=s4, ttn=ttn):
                                for ti in range(NT):
                                    bnk = ti % 4
                                    v.wait_ge(sems["TT"], ttn - 8 + ti + 1)
                                    dst = vtm.rearrange("p (h t) -> p h t", t=SEG)[
                                        :, 2 * s4:2 * s4 + 2, ti * 128:(ti + 1) * 128]
                                    src = bk(bnk)[:, 0:256].rearrange("p (h d) -> p h d", d=128)
                                    pb_free[bnk] = inc(v.tensor_copy(dst, src), "VV")
                            V(v11)
                        else:
                            sg_aa, sg_vv = {}, {}
                            vvb4 = n["VV"]

                            def a4(a, ttn=ttn, vvb4=vvb4):
                                for ti in range(NT):
                                    bnk = ti % 4
                                    a.wait_ge(sems["TT"], ttn - 8 + ti + 1)
                                    if ti >= 2:
                                        a.wait_ge(sems["VV"], vvb4 + (ti - 2) + 1)
                                    else:
                                        a.wait_ge(sems["VV"], vvb4)
                                    a.activation(ogstage[:, (ti % 2) * 256:(ti % 2) * 256 + 256],
                                                 bk(bnk)[:, 0:256], AF.Silu)
                                    a.drain()
                                    cnt = inc(a.nop(), "AA")
                                    sg_aa[ti] = cnt
                                    pb_free[(bnk, "a")] = cnt
                            A(a4)

                            def v12(v, s4=s4):
                                for ti in range(NT):
                                    v.wait_ge(sems["AA"], sg_aa[ti])
                                    dst = sgtm.rearrange("p (h t) -> p h t", t=SEG)[
                                        :, 2 * s4:2 * s4 + 2, ti * 128:(ti + 1) * 128]
                                    src = ogstage[:, (ti % 2) * 256:(ti % 2) * 256 + 256].rearrange(
                                        "p (h d) -> p h d", d=128)
                                    gsrc = gnwt[:, :].rearrange("p (h d) -> p h d", d=128)
                                    sg_vv[ti] = inc(v.tensor_tensor(dst, src, gsrc, ALU.mult), "VV")
                            V(v12)
            bar()

        def sweep1(grp):
            ds_vv = {}
            for hh in range(GH):
                Gh = Gf[:, hh * SEG:(hh + 1) * SEG]
                g3 = Gh.rearrange("p (c t) -> p c t", t=64)

                def a5(a, Gh=Gh, hh=hh):
                    if hh >= 1:
                        a.wait_ge(sems["VV"], ds_vv[hh - 1])
                    inc(a.activation(EGi[:, :], Gh, AF.Exp, scale=-1.0 / 16.0), "AA")
                A(a5)
                aan = n["AA"]

                def v13(v, g3=g3, aan=aan):
                    v.wait_ge(sems["AA"], aan)
                    v.tensor_copy(glc[:, 0:16], g3[:, :, 63])
                    v.drain()
                    v.tensor_tensor_scan(zbA[:, 0:16], scanms[:, 1:17], glc[:, 0:16],
                                         0.0, ALU.mult, ALU.add)
                    v.drain()
                    v.tensor_tensor(zbB[:, 0:16], glc[:, 0:16], zbA[:, 0:16], ALU.subtract)
                    v.drain()
                    inc(v.tensor_scalar(zbB[:, 0:16], zbB[:, 0:16], zbA[:, 15:16], 0.0,
                                        ALU.add, ALU.add), "VV")
                V(v13)
                if hh == 0:
                    ckstop("s1v13", hrow)
                vv2 = n["VV"]

                def a6(a, vv2=vv2, hh=hh):
                    a.wait_ge(sems["VV"], vv2)
                    a.activation(zbB[:, 0:16], zbB[:, 0:16], AF.Exp, scale=1.0 / 16.0)
                    inc(a.activation(ctile[:, 1024 + hh:1025 + hh], zbA[:, 15:16], AF.Exp,
                                     scale=1.0 / 16.0), "AA")
                A(a6)
                aan2 = n["AA"]

                def v14(v, hh=hh, aan2=aan2):
                    v.wait_ge(sems["AA"], aan2)
                    ez3 = zbB[:, 0:16].unsqueeze(2).broadcast_to([128, 16, 64])
                    egi3 = EGi[:, :].rearrange("p (c t) -> p c t", t=64)
                    h3 = hrow[:, :].rearrange("p (c t) -> p c t", t=64)
                    v.tensor_tensor(h3, egi3, ez3, ALU.mult)
                    v.drain()
                    inc(v.tensor_tensor(kwb[:, :], kf[:, hh * SEG:(hh + 1) * SEG],
                                        hrow[:, :], ALU.mult), "VV")
                V(v14)
                if hh == 0:
                    ckstop("s1v14", hrow)
                kwn = n["VV"]

                T(lambda t, kwn=kwn: t.wait_ge(sems["VV"], kwn))
                vkw = tr_pipe(kwb, lambda v, j, ps: v.tensor_copy(
                    kwt[:, j * 128:(j + 1) * 128], ps), 8)
                if hh == 0:
                    ckstop("s1tr", hrow)

                def t11(t, hh=hh, vkw=vkw):
                    t.wait_ge(sems["VV"], vkw)
                    if hh >= 1:
                        t.wait_ge(sems["VV"], ds_vv[hh - 1])
                    for b2 in range(8):
                        lhsT = kwt[:, b2 * 128:(b2 + 1) * 128]
                        rhs = vtm.rearrange("p (h b t) -> p h b t", h=GH, t=128)[
                            :, hh, b2, :]
                        mm = t.matmul(bk(4)[:, 0:128], lhsT, rhs,
                                      start=(b2 == 0), stop=(b2 == 7))
                    inc(mm, "TT")
                T(t11)
                ttds = n["TT"]

                def v16(v, hh=hh, ttds=ttds):
                    v.wait_ge(sems["TT"], ttds)
                    ds_vv[hh] = inc(v.tensor_copy(ctile[:, hh * 128:(hh + 1) * 128],
                                                  bk(4)[:, 0:128]), "VV")
                V(v16)
                if hh == 0:
                    ckstop("s1mm", hrow)
            bar()

        def gather(grp):
            def g4(g, grp=grp):
                inc(g.dma_start(out=cc2i[grp].ap(), in_=ctile[:, :]), "CN", 16)
                g.wait_ge(sems["CN"], n["CN"])
                inc(g.collective_compute(
                    "AllGather", ALU.bypass, replica_groups=[list(range(8))],
                    ins=[cc2i[grp].ap().opt()], outs=[cc2o[grp].ap().opt()]), "CC")
            G(g4)
            ccn = n["CC"]

            V(lambda v: (v.memset(Sst[:, :], 0.0), v.drain(), inc(v.nop(), "VV")))
            ranks = cc2o[grp].ap().rearrange("(r p) n -> r p n", p=128)
            p4vv = {}
            for r in range(8):
                def syr(sy, r=r, ccn=ccn):
                    sy.wait_ge(sems["CC"], ccn)
                    if r >= 1:
                        sy.wait_ge(sems["VV"], p4vv[r - 1])
                    inc(sy.dma_start(out=ctile[:, :], in_=ranks[r]), "RNK", 16)
                SY(syr)
                xln = n["RNK"]

                def v17(v, r=r, xln=xln):
                    v.wait_ge(sems["RNK"], xln)
                    m = smasks[:, r:r + 1]
                    v.tensor_scalar(zbB[:, 0:1], m, -1.0, 1.0, ALU.mult, ALU.add)
                    v.drain()
                    v.tensor_scalar(zbA[:, 0:8], ctile[:, 1024:1032], m, zbB[:, 0:1],
                                    ALU.mult, ALU.add)
                    v.drain()
                    a3_ = zbA[:, 0:8].unsqueeze(2).broadcast_to([128, 8, 128])
                    s3_ = Sst[:, :].rearrange("p (h d) -> p h d", d=128)
                    v.tensor_tensor(s3_, s3_, a3_, ALU.mult)
                    v.drain()
                    v.scalar_tensor_tensor(Sst[:, :], ctile[:, 0:1024], m, Sst[:, :],
                                           ALU.mult, ALU.add)
                    v.drain()
                    p4vv[r] = inc(v.nop(), "VV")
                V(v17)
            bar()

        def sweep2(grp):
            for hh in range(GH):
                hg = grp * GH + hh
                Gh = Gf[:, hh * SEG:(hh + 1) * SEG]
                qh = qf[:, hh * SEG:(hh + 1) * SEG]
                kh = kf[:, hh * SEG:(hh + 1) * SEG]
                sgh = sgtm[:, hh * SEG:(hh + 1) * SEG]

                def a7(a, Gh=Gh):
                    a.wait_ge(sems["VV"], n["VV"])
                    a.wait_ge(sems["GG"], n["GG"])
                    a.activation(EG[:, :], Gh, AF.Exp, scale=1.0 / 16.0)
                    inc(a.activation(EGi[:, :], Gh, AF.Exp, scale=-1.0 / 16.0), "AA")
                A(a7)
                aan = n["AA"]
                eg3 = EG[:, :].rearrange("p (c t) -> p c t", t=64)
                egi3 = EGi[:, :].rearrange("p (c t) -> p c t", t=64)

                def gp(g, aan=aan):
                    g.wait_ge(sems["AA"], aan)
                    egb = eg3[:, :, 0].unsqueeze(2).broadcast_to([128, 16, 64])
                    egbi = egi3[:, :, 0].unsqueeze(2).broadcast_to([128, 16, 64])
                    h3 = hrow[:, :].rearrange("p (c t) -> p c t", t=64)
                    h23 = hrow2[:, :].rearrange("p (c t) -> p c t", t=64)
                    g.tensor_tensor(h3, eg3, egbi, ALU.mult)
                    inc(g.tensor_tensor(h23, egi3, egb, ALU.mult), "GG")
                G(gp)
                ggn = n["GG"]

                def v18(v, qh=qh, kh=kh, ggn=ggn):
                    v.wait_ge(sems["GG"], ggn)
                    v.tensor_tensor(qhb[:, :], qh, hrow[:, :], ALU.mult)
                    inc(v.tensor_tensor(khb[:, :], kh, hrow2[:, :], ALU.mult), "VV")
                V(v18)
                qkn = n["VV"]

                def gp2(g, qkn=qkn):
                    g.wait_ge(sems["VV"], qkn)
                    egl = eg3[:, :, 63].unsqueeze(2).broadcast_to([128, 16, 64])
                    h3 = hrow[:, :].rearrange("p (c t) -> p c t", t=64)
                    inc(g.tensor_tensor(h3, egi3, egl, ALU.mult), "GG")
                G(gp2)
                gg2 = n["GG"]

                def v19(v, kh=kh, gg2=gg2, hh=hh):
                    v.wait_ge(sems["GG"], gg2)
                    v.tensor_tensor(kwb[:, :], kh, hrow[:, :], ALU.mult)
                    inc(v.tensor_copy(Scur[:, :], Sst[:, hh * HD:(hh + 1) * HD]), "VV")
                V(v19)
                kwn = n["VV"]

                T(lambda t, kwn=kwn: t.wait_ge(sems["VV"], kwn))
                sready = tr_pipe(kwb, lambda v, j, ps: v.tensor_copy(
                    kwt[:, j * 128:(j + 1) * 128], ps), 8)

                def v20i(v):
                    v.drain()
                    v.tensor_scalar(s2buf[:, :], Scur[:, :], eg3[:, 0, 63].unsqueeze(1),
                                    0.0, ALU.mult, ALU.add)
                    inc(v.tensor_scalar(sgbuf[:, :], Scur[:, :],
                                        eg3[:, 0, 0].unsqueeze(1), 0.0,
                                        ALU.mult, ALU.add), "VV")
                V(v20i)
                sready = n["VV"]

                ch_aa, ch_tt, ch_vv = {}, {}, {}
                for c in range(16):
                    p0 = (c % 2) * 64
                    b2 = c // 2
                    vslice = vtm.rearrange("p (h b t) -> p h b t", h=GH, t=128)[
                        p0:p0 + 64, hh, b2, :]
                    kwts = kwt[p0:p0 + 64, b2 * 128:(b2 + 1) * 128]

                    def t13(t, c=c, sready=sready):
                        if c == 0:
                            t.wait_ge(sems["VV"], sready)
                        else:
                            t.wait_ge(sems["VV"], ch_vv[(c - 1, "oev")])
                        mm = t.matmul(bk(5)[0:64, 0:64], khb[:, c * 64:(c + 1) * 64],
                                      qhb[:, c * 64:(c + 1) * 64], start=True, stop=True)
                        ch_tt[(c, "at")] = inc(mm, "TT")
                    T(t13)

                    def v21(v, c=c, p0=p0):
                        v.wait_ge(sems["TT"], ch_tt[(c, "at")])
                        ch_vv[(c, "atm")] = inc(
                            v.tensor_tensor(atb[p0:p0 + 64, :], bk(5)[0:64, 0:64],
                                            umasks[:, :], ALU.mult), "VV")
                    V(v21)

                    def t14(t, c=c, vslice=vslice, kwts=kwts, p0=p0):
                        t.wait_ge(sems["VV"], ch_vv[(c, "atm")])
                        t.matmul(bk(6)[0:64, 0:128], qhb[:, c * 64:(c + 1) * 64],
                                 sgbuf[:, :], start=True, stop=True)
                        mm2 = t.matmul(bk(3)[0:64, 0:128], atb[p0:p0 + 64, :], vslice,
                                       start=True, stop=True)
                        ch_tt[(c, "o")] = inc(mm2, "TT")
                        mm3 = t.matmul(bk(2)[:, 0:128], kwts, vslice, start=True, stop=True)
                        ch_tt[(c, "sd")] = inc(mm3, "TT")
                    T(t14)

                    def v22(v, c=c, b2=b2):
                        v.wait_ge(sems["TT"], ch_tt[(c, "sd")])
                        dst = otm[(c % 2) * 64:(c % 2) * 64 + 64, b2 * 128:(b2 + 1) * 128]
                        v.tensor_copy(dst, bk(3)[0:64, 0:128])
                        if c < 15:
                            v.tensor_tensor(Scur[:, :], s2buf[:, :], bk(2)[:, 0:128],
                                            ALU.add)
                        v.drain()
                        if c < 15:
                            v.tensor_tensor(dst, dst, bk(6)[0:64, 0:128], ALU.add)
                            v.tensor_scalar(s2buf[:, :], Scur[:, :],
                                            eg3[:, c + 1, 63].unsqueeze(1), 0.0,
                                            ALU.mult, ALU.add)
                            ch_vv[(c, "oev")] = inc(
                                v.tensor_scalar(sgbuf[:, :], Scur[:, :],
                                                eg3[:, c + 1, 0].unsqueeze(1), 0.0,
                                                ALU.mult, ALU.add), "VV")
                        else:
                            ch_vv[(c, "oev")] = inc(
                                v.tensor_tensor(dst, dst, bk(6)[0:64, 0:128],
                                                ALU.add), "VV")
                    V(v22)

                def a9(a):
                    for b2 in range(8):
                        a.wait_ge(sems["VV"], ch_vv[(b2 * 2 + 1, "oev")])
                        a.activation(qhb[:, b2 * 128:(b2 + 1) * 128],
                                     otm[:, b2 * 128:(b2 + 1) * 128],
                                     AF.Square, accum_out=ssq[:, b2:b2 + 1])
                    a.drain()
                    a.activation(zbA[:, 0:8], ssq[:, 0:8], AF.Ln, scale=1.0 / 128.0,
                                 bias=epsc[:, 0:1])
                    a.drain()
                    a.activation(zbA[:, 0:8], zbA[:, 0:8], AF.Exp, scale=-0.5)
                    a.drain()
                    inc(a.nop(), "AA")
                A(a9)
                arst = n["AA"]

                def v23(v, sgh=sgh, arst=arst):
                    v.wait_ge(sems["AA"], arst)
                    v.wait_ge(sems["TT"], state["ttog_prev"])
                    for b2 in range(8):
                        v.tensor_scalar(hrow[:, b2 * 128:(b2 + 1) * 128],
                                        otm[:, b2 * 128:(b2 + 1) * 128],
                                        zbA[:, b2:b2 + 1], 0.0, ALU.mult, ALU.add)
                    v.drain()
                    inc(v.tensor_tensor(ogstage[:, :], hrow[:, :], sgh, ALU.mult), "VV")
                V(v23)
                ogn = n["VV"]

                T(lambda t, ogn=ogn: t.wait_ge(sems["VV"], ogn))
                tr_pipe(ogstage, lambda v, j, ps, hg=hg: v.tensor_copy(
                    ogf[:, hg * SEG + j * 128:hg * SEG + (j + 1) * 128], ps), 8)
                state["ttog_prev"] = n["TT"]
            bar()

        for grp in range(2):
            proj(grp)
            ckstop(f"proj{grp}", hrow)
            sweep1(grp)
            ckstop(f"sw1_{grp}", hrow)
            gather(grp)
            ckstop(f"gth{grp}", hrow)
            sweep2(grp)
            ckstop(f"sw2_{grp}", hrow)
        espA.close()

        esp6 = ExitStack()
        sb6 = lambda n_, s_, dt: esp6.enter_context(nc.sbuf_tensor(n_, s_, dt))
        cpB = sb6("cpB", [128, 5 * D], BF16)
        xt6 = sb6("xt6", [128, 2 * D], F32)
        lnb = sb6("lnb6", [128, D], F32)
        lnb2 = sb6("lnb26", [128, D], F32)
        sqj = [sb6(f"sqj6{i}", [128, D], F32) for i in range(2)]
        hstage = sb6("hstage6", [128, D], BF16)
        hrow6 = sb6("hrow6", [128, D], F32)
        mrow6 = sb6("mrow6", [1, D], F32)
        qstage = sb6("qstage", [128, D], F32)
        grow = sb6("grow", [1, D], BF16)

        for sl, off in [(0, 2 * D), (1, 3 * D), (2, 4 * D), (3, 5 * D)]:
            def sy3(sy, sl=sl, off=off):
                if sl == 0:
                    sy.wait_ge(sems["SP"], sp_modscr)
                sy.wait_ge(sems["VV"], n["VV"])
                inc(sy.dma_start(out=mrow6[0:1, :], in_=modscr.ap()[0:1, off:off + D]),
                    "CS", 16)
            SY(sy3)
            xr = n["CS"]

            def t16(t, xr=xr):
                t.wait_ge(sems["CS"], xr)
                t.wait_ge(sems["VV"], n["VV"])
                for nn in range(4):
                    mm = t.matmul(bk(nn)[:, 0:512], onesr[0:1, :],
                                  mrow6[0:1, nn * 512:(nn + 1) * 512], start=True, stop=True)
                inc(mm, "TT")
            T(t16)
            ttb = n["TT"]

            def v25(v, sl=sl, ttb=ttb):
                v.wait_ge(sems["TT"], ttb)
                for nn in range(4):
                    v.tensor_copy(cpB[:, sl * D + nn * 512:sl * D + (nn + 1) * 512],
                                  bk(nn)[:, 0:512])
                v.drain()
                inc(v.nop(), "VV")
            V(v25)

        def sy4(sy):
            sy.wait_ge(sems["VV"], n["VV"])
            inc(sy.dma_start(out=mrow6[0:1, :], in_=nw_i), "CS", 16)
        SY(sy4)
        xr2 = n["CS"]

        def t17(t, xr2=xr2):
            t.wait_ge(sems["CS"], xr2)
            t.wait_ge(sems["VV"], n["VV"])
            for nn in range(4):
                mm = t.matmul(bk(nn)[:, 0:512], onesr[0:1, :],
                              mrow6[0:1, nn * 512:(nn + 1) * 512], start=True, stop=True)
            inc(mm, "TT")
        T(t17)
        ttnw = n["TT"]

        def v26(v, ttnw=ttnw):
            v.wait_ge(sems["TT"], ttnw)
            for nn in range(4):
                v.tensor_copy(cpB[:, 4 * D + nn * 512:4 * D + (nn + 1) * 512], bk(nn)[:, 0:512])
            v.drain()
            inc(v.nop(), "VV")
        V(v26)

        def g5(g):
            g.wait_ge(sems["VV"], n["VV"])
            inc(g.dma_start(out=grow[0:1, :], in_=modscr.ap()[0:1, 2 * D:3 * D]), "CN", 16)
        G(g5)
        wlga = n["CN"]

        def t18(t, wlga=wlga):
            t.wait_ge(sems["CN"], wlga)
            t.wait_ge(sems["VV"], n["VV"])
            for c in range(16):
                tr = t.transpose(bkb(5)[:, 2 * c:2 * c + 1], grow[0:1, c * 128:(c + 1) * 128],
                                 ident[0:1, 0:1])
            inc(tr, "TT")
        T(t18)
        ttga = n["TT"]

        def v27(v, ttga=ttga):
            v.wait_ge(sems["TT"], ttga)
            src = bkb(5)[:, 0:32].rearrange("p (c u) -> p c u", u=2)[:, :, 0]
            v.tensor_copy(gacol[:, 0:16], src)
            v.drain()
            inc(v.nop(), "VV")
        V(v27)
        ga_done = n["VV"]
        bar()

        afm = actq
        wo_tt = {}
        for s8 in range(8):
            slot = wring[:, (s8 % 2) * 4096:(s8 % 2) * 4096 + 4096]

            spn = f"WL{s8 % 2}"

            def g6(g, s8=s8, slot=slot, spn=spn):
                if s8 >= 2:
                    g.wait_ge(sems["TT"], wo_tt[s8 - 2])
                src = wo_i.rearrange("(c p) j -> p c j", p=128)[:, :, s8 * 256:(s8 + 1) * 256]
                dst = slot.rearrange("p (c j) -> p c j", j=256)
                for q4 in range(4):
                    inc(g.dma_start(out=dst[:, q4 * 4:(q4 + 1) * 4, :],
                                    in_=src[:, q4 * 4:(q4 + 1) * 4, :]), spn, 16)
            G(g6)
            wln = (spn, n[spn])

            def t19(t, slot=slot, wln=wln):
                t.wait_ge(sems[wln[0]], wln[1])
                t.wait_ge(sems["VV"], ga_done)
                for u in range(4):
                    m2, th = u // 2, u % 2
                    if u in pb_free:
                        t.wait_ge(sems["VV"], pb_free[u])
                    for kc in range(16):
                        mm = t.matmul(bk(u)[:, :],
                                      slot[:, kc * 256 + m2 * 128:kc * 256 + m2 * 128 + 128],
                                      ogf[:, kc * SEG + th * 512:kc * SEG + th * 512 + 512],
                                      start=(kc == 0), stop=(kc == 15))
                    inc(mm, "TT")
            T(t19)
            ttn = n["TT"]
            wo_tt[s8] = ttn

            def v28(v, s8=s8, ttn=ttn):
                for u in range(4):
                    m2, th = u // 2, u % 2
                    v.wait_ge(sems["TT"], ttn - 4 + u + 1)
                    oc = s8 * 2 + m2
                    cp = v.tensor_scalar(afm[:, oc * SEG + th * 512:oc * SEG + th * 512 + 512],
                                         bk(u)[:, :], gacol[:, oc:oc + 1], 0.0,
                                         ALU.mult, ALU.add)
                    pb_free[u] = inc(cp, "VV")
            V(v28)
        bar()
        ckstop("wo", lnb)
        if DEBUG:
            def dmp2(sy):
                inc(sy.dma_start(out=dbg_og.ap(), in_=ogf[:, :]), "ST", 16)
                inc(sy.dma_start(out=dbg_afm.ap(), in_=afm[:, :]), "ST", 16)
                sy.wait_ge(sems["ST"], n["ST"])
            SY(dmp2)
            bar()

        sb_t = cpB[:, 1 * D:2 * D]
        cb1_t = cpB[:, 2 * D:3 * D]
        gb_t = cpB[:, 3 * D:4 * D]
        nw_t = cpB[:, 4 * D:5 * D]
        p6_done, p6_var, p6_sp = {}, {}, {}
        for ti in range(NT):
            xv = xt6[:, (ti % 2) * D:(ti % 2) * D + D]

            spn = f"XL{ti % 2}"

            def sy5(sy, ti=ti, xv=xv, spn=spn):
                if ti >= 2:
                    sy.wait_ge(sems["VV"], p6_done[ti - 2])
                inc(sy.dma_start(out=xv, in_=x3[ti]), spn, 16)
            SY(sy5)
            xln = (spn, n[spn])

            vv0 = n["VV"]
            tt0b = n["TT"]

            def t20(t, ti=ti, vv0=vv0):
                for j in range(16):
                    if j >= 2:
                        t.wait_ge(sems["VV"], vv0 + (j - 2) + 1)
                    else:
                        bnk = 6 + j % 2
                        if bnk in pb_free:
                            t.wait_ge(sems["VV"], pb_free[bnk])
                    tr = t.transpose(bkb(6 + j % 2)[:, 0:128],
                                     afm[:, j * SEG + ti * 128:j * SEG + (ti + 1) * 128],
                                     ident[:, :])
                    inc(tr, "TT")
            T(t20)

            def v29(v, ti=ti, xln=xln, tt0b=tt0b):
                v.wait_ge(sems[xln[0]], xln[1])
                if ti >= 1:
                    v.wait_ge(sems["SP"], p6_sp[ti - 1])
                for j in range(16):
                    v.wait_ge(sems["TT"], tt0b + j + 1)
                    xs_ = xt6[:, (ti % 2) * D + j * 128:(ti % 2) * D + (j + 1) * 128]
                    cnt = inc(v.tensor_tensor(lnb[:, j * 128:(j + 1) * 128],
                                              bkb(6 + j % 2)[:, 0:128], xs_, ALU.add), "VV")
                    pb_free[6 + j % 2] = cnt
                v.drain()
                inc(v.nop(), "VV")
            V(v29)
            x2n = n["VV"]

            def a10(a, ti=ti, x2n=x2n):
                a.wait_ge(sems["VV"], x2n)
                if ti >= 1:
                    a.wait_ge(sems["VV"], p6_var[ti - 1])
                a.activation(sqj[ti % 2][:, :], lnb[:, :], AF.Square, accum_out=colb[:, 2:3])
                a.drain()
                inc(a.nop(), "AA")
            A(a10)
            aan = n["AA"]

            def v30(v, x2n=x2n, aan=aan, ti=ti):
                v.wait_ge(sems["VV"], x2n)
                v.tensor_reduce(colb[:, 0:1], lnb[:, :], AX.X, ALU.add)
                v.drain()
                v.tensor_scalar(colb[:, 1:2], colb[:, 0:1], 1.0 / D, 0.0, ALU.mult, ALU.add)
                v.drain()
                v.tensor_tensor(colb[:, 4:5], colb[:, 1:2], colb[:, 1:2], ALU.mult)
                v.wait_ge(sems["AA"], aan)
                v.drain()
                v.scalar_tensor_tensor(colb[:, 5:6], colb[:, 2:3], 1.0 / D, colb[:, 4:5],
                                       ALU.mult, ALU.subtract)
                v.drain()
                p6_var[ti] = inc(v.nop(), "VV")
            V(v30)
            vvn = n["VV"]

            def a11(a, vvn=vvn):
                a.wait_ge(sems["VV"], vvn)
                a.activation(colb[:, 6:7], colb[:, 5:6], AF.Ln, bias=epsc[:, 0:1])
                a.drain()
                a.activation(colb[:, 7:8], colb[:, 6:7], AF.Exp, scale=-0.5)
                a.drain()
                inc(a.nop(), "AA")
            A(a11)
            aan2 = n["AA"]

            def v31(v, aan2=aan2):
                v.wait_ge(sems["AA"], aan2)
                v.scalar_tensor_tensor(colb[:, 8:9], colb[:, 1:2], -1.0, colb[:, 7:8],
                                       ALU.mult, ALU.mult)
                v.drain()
                v.tensor_scalar(lnb2[:, :], lnb[:, :], colb[:, 7:8], colb[:, 8:9],
                                ALU.mult, ALU.add)
                v.drain()
                v.tensor_tensor(lnb2[:, :], lnb2[:, :], cb1_t, ALU.mult)
                v.drain()
                v.tensor_tensor(lnb2[:, :], lnb2[:, :], sb_t, ALU.add)
                v.drain()
                inc(v.nop(), "VV")
            V(v31)
            h2n = n["VV"]

            def g7(g, h2n=h2n, ti=ti):
                g.wait_ge(sems["VV"], h2n)
                inc(g.tensor_tensor(sqj[ti % 2][:, :], lnb2[:, :], gb_t, ALU.mult), "GG")
            G(g7)
            ggn = n["GG"]

            def v32(v, ti=ti, ggn=ggn):
                v.wait_ge(sems["GG"], ggn)
                inc(v.tensor_tensor(lnb[:, :], lnb[:, :], sqj[ti % 2][:, :], ALU.add), "VV")
            V(v32)
            x2pn = n["VV"]

            def a12(a, ti=ti, x2pn=x2pn):
                a.wait_ge(sems["VV"], x2pn)
                p6_sp[ti] = inc(a.dma_start(out=x2s3[ti], in_=lnb[:, :]), "SP", 16)
                a.activation(sqj[ti % 2][:, :], lnb2[:, :], AF.Square,
                             accum_out=colb[:, 10:11])
                a.drain()
                inc(a.nop(), "AA")
            A(a12)
            aan3 = n["AA"]

            def v33(v, aan3=aan3):
                v.tensor_tensor(hrow6[:, :], lnb2[:, :], nw_t, ALU.mult)
                v.drain()
                v.tensor_reduce(colb[:, 12:13], hrow6[:, :], AX.X, ALU.max,
                                apply_absolute_value=True)
                v.wait_ge(sems["AA"], aan3)
                v.drain()
                inc(v.nop(), "VV")
            V(v33)
            vvz = n["VV"]

            def a13(a, vvz=vvz):
                a.wait_ge(sems["VV"], vvz)
                a.activation(colb[:, 13:14], colb[:, 10:11], AF.Ln, scale=1.0 / D,
                             bias=epsc[:, 0:1])
                a.drain()
                a.activation(colb[:, 13:14], colb[:, 13:14], AF.Exp, scale=-0.5)
                a.drain()
                inc(a.nop(), "AA")
            A(a13)
            aan4 = n["AA"]

            def v34(v, ti=ti, aan4=aan4):
                v.wait_ge(sems["AA"], aan4)
                v.tensor_tensor(colb[:, 14:15], colb[:, 12:13], colb[:, 13:14], ALU.mult)
                v.drain()
                v.tensor_scalar(colb[:, 14:15], colb[:, 14:15], 1e-5, 0.0, ALU.max, ALU.add)
                v.drain()
                v.reciprocal(colb[:, 15:16], colb[:, 14:15])
                v.drain()
                v.tensor_scalar(colb[:, 16:17], colb[:, 15:16], 127.0, 0.0, ALU.mult, ALU.add)
                v.tensor_scalar(qcols[:, ti:ti + 1], colb[:, 14:15], 1.0 / 127.0, 0.0,
                                ALU.mult, ALU.add)
                v.drain()
                v.tensor_tensor(colb[:, 17:18], colb[:, 13:14], colb[:, 16:17], ALU.mult)
                v.drain()
                v.tensor_scalar(qstage[:, :], hrow6[:, :], colb[:, 17:18], RBF,
                                ALU.mult, ALU.add)
                v.drain()
                inc(v.nop(), "VV")
            V(v34)
            vq = n["VV"]

            def g8(g, vq=vq):
                g.wait_ge(sems["VV"], vq)
                inc(g.tensor_scalar(qstage[:, :], qstage[:, :], RBF + 127.0, RBF - 127.0,
                                    ALU.min, ALU.max), "GG")
            G(g8)
            gq = n["GG"]

            def v35(v, gq=gq):
                v.wait_ge(sems["GG"], gq)
                inc(v.tensor_scalar(hstage[:, :], qstage[:, :], RBF, 0.0, ALU.subtract,
                                    ALU.add), "VV")
            V(v35)
            hqn = n["VV"]

            T(lambda t, hqn=hqn: t.wait_ge(sems["VV"], hqn))
            p6_done[ti] = tr_pipe(
                hstage,
                lambda v, j, ps, ti=ti: v.tensor_copy(
                    actq[:, j * SEG + ti * 128:j * SEG + (ti + 1) * 128], ps), 16)
        bar()
        ckstop("P6", lnb)
        esp6.close()
        espW.close()

        esp7 = ExitStack()
        sb7 = lambda n_, s_, dt: esp7.enter_context(nc.sbuf_tensor(n_, s_, dt))
        wffn = sb7("wffn", [128, 16384], BF16)
        acc = sb7("acc", [128, 16 * SEG], F32)
        midst = sb7("midst", [128, 1024], BF16)
        qmidt = sb7("qmidt", [128, 1024], BF16)
        qmidf = sb7("qmidf", [128, 1024], BF16)
        qmstage = sb7("qmstage", [128, 1024], F32)
        sgst = sb7("sgst", [128, 256], BF16)
        valst = sb7("valst", [128, 256], F32)
        xf = sb7("xf", [128, D], F32)
        mrow7 = sb7("mrow7", [1, D], F32)
        gbt = sb7("gbt", [128, D], BF16)

        def sy9(sy):
            inc(sy.dma_start(out=mrow7[0:1, :], in_=modscr.ap()[0:1, 5 * D:6 * D]),
                "CS", 16)
        SY(sy9)
        xgb = n["CS"]

        def t26(t, xgb=xgb):
            t.wait_ge(sems["CS"], xgb)
            for nn in range(4):
                mm = t.matmul(bk(4 + nn)[:, 0:512], onesr[0:1, :],
                              mrow7[0:1, nn * 512:(nn + 1) * 512], start=True, stop=True)
                inc(mm, "TT")
        T(t26)
        ttgb = n["TT"]

        def vgb(v, ttgb=ttgb):
            for nn in range(4):
                v.wait_ge(sems["TT"], ttgb - 4 + nn + 1)
                v.tensor_copy(gbt[:, nn * 512:(nn + 1) * 512], bk(4 + nn)[:, 0:512])
            v.drain()
            inc(v.nop(), "VV")
        V(vgb)

        def v37(v):
            for ti in range(NT):
                v.tensor_scalar(svcol[:, ti:ti + 1], qcols[:, ti:ti + 1], scw[:, 0:1],
                                0.0, ALU.mult, ALU.add)
            v.memset(acc[:, :], 0.0)
            v.drain()
            inc(v.nop(), "VV")
        V(v37)
        bar()

        gv_tt = {}
        last_sp = {}
        for B in range(32):
            wslot = wffn[:, (B % 2) * 8192:(B % 2) * 8192 + 8192]
            ws3 = wslot.rearrange("p (c j) -> p c j", j=512)

            spn = f"WF{B % 2}"

            def g9(g, B=B, ws3=ws3, spn=spn):
                if B >= 2:
                    g.wait_ge(sems["TT"], gv_tt[B - 2])
                srcv = wgvT_i.rearrange("(c p) j -> p c j", p=128)[:, :, B * 256:(B + 1) * 256]
                srcg = wgvT_i.rearrange("(c p) j -> p c j", p=128)[
                    :, :, HID + B * 256:HID + (B + 1) * 256]
                for q4 in range(4):
                    sl = slice(q4 * 4, (q4 + 1) * 4)
                    inc(g.dma_start(out=ws3[:, sl, 0:256], in_=srcv[:, sl, :]), spn, 16)
                    inc(g.dma_start(out=ws3[:, sl, 256:512], in_=srcg[:, sl, :]), spn, 16)
            G(g9)
            wln = (spn, n[spn])

            def wq1(v, ws3=ws3, wln=wln):
                v.wait_ge(sems[wln[0]], wln[1])
                inc(v.tensor_scalar(ws3[:, :, :], ws3[:, :, :], scw[:, 2:3], RB,
                                    ALU.mult, ALU.add), "VV")
            V(wq1)
            q1n = n["VV"]

            def wq2(g, ws3=ws3, q1n=q1n):
                g.wait_ge(sems["VV"], q1n)
                inc(g.tensor_scalar(ws3[:, :, :], ws3[:, :, :], RB + 1.0, RB - 1.0,
                                    ALU.min, ALU.max), "GG")
            G(wq2)
            q2n = n["GG"]

            def wq3(v, ws3=ws3, q2n=q2n):
                v.wait_ge(sems["GG"], q2n)
                inc(v.tensor_scalar(ws3[:, :, :], ws3[:, :, :], RB, 0.0, ALU.subtract,
                                    ALU.add), "VV")
            V(wq3)
            wqn = n["VV"]

            for ti in range(NT):
                bv, bg = ti % 2, 2 + ti % 2

                def t22(t, ti=ti, bv=bv, bg=bg, ws3=ws3, wln=wln, wqn=wqn):
                    t.wait_ge(sems["VV"], wqn)
                    for bnk in (bv, bg):
                        if bnk in pb_free:
                            t.wait_ge(sems["VV"], pb_free[bnk])
                        if (bnk, "a") in pb_free:
                            t.wait_ge(sems["AA"], pb_free[(bnk, "a")])
                    for kc in range(16):
                        t.matmul(bk(bv)[:, 0:256],
                                 actq[:, kc * SEG + ti * 128:kc * SEG + (ti + 1) * 128],
                                 ws3[:, kc, 0:256], start=(kc == 0), stop=(kc == 15))
                    for kc in range(16):
                        mm = t.matmul(bk(bg)[:, 0:256],
                                      actq[:, kc * SEG + ti * 128:kc * SEG + (ti + 1) * 128],
                                      ws3[:, kc, 256:512], start=(kc == 0), stop=(kc == 15))
                    inc(mm, "TT")
                T(t22)
                ttn = n["TT"]
                gv_tt[B] = ttn

                def a14(a, ti=ti, bv=bv, bg=bg, ttn=ttn):
                    a.wait_ge(sems["TT"], ttn)
                    cnt = inc(a.activation(sgst[:, :], bk(bg)[:, 0:256], AF.Silu,
                                           scale=svcol[:, ti:ti + 1]), "AA")
                    pb_free[(bg, "a")] = cnt
                A(a14)
                aan = n["AA"]

                def v38(v, ti=ti, B=B, aan=aan, bv=bv, bg=bg):
                    v.wait_ge(sems["AA"], aan)
                    if last_sp:
                        v.wait_ge(sems["SP"], last_sp["c"])
                    v.scalar_tensor_tensor(midst[:, 0:256], bk(bv)[:, 0:256],
                                           svcol[:, ti:ti + 1], sgst[:, :],
                                           ALU.mult, ALU.mult)
                    v.drain()
                    cnt = inc(v.tensor_reduce(ambuf[:, ti * 32 + B:ti * 32 + B + 1],
                                              midst[:, 0:256], AX.X, ALU.max,
                                              apply_absolute_value=True), "VV")
                    pb_free[bv] = cnt
                    pb_free[bg] = cnt
                V(v38)
                vvm = n["VV"]

                def a15(a, ti=ti, B=B, vvm=vvm):
                    a.wait_ge(sems["VV"], vvm)
                    last_sp["c"] = inc(a.dma_start(out=mid3[ti][:, B * 256:(B + 1) * 256],
                                                   in_=midst[:, 0:256]), "SP", 16)
                A(a15)
        sp_mid = n["SP"]
        bar()
        ckstop("gv", acc)

        def v39(v):
            for ti in range(NT):
                v.tensor_reduce(colb[:, 20:21], ambuf[:, ti * 32:(ti + 1) * 32], AX.X, ALU.max)
                v.drain()
                v.tensor_scalar(colb[:, 20:21], colb[:, 20:21], 1e-5, 0.0, ALU.max, ALU.add)
                v.drain()
                v.reciprocal(colb[:, 21:22], colb[:, 20:21])
                v.drain()
                v.tensor_scalar(qcols2[:, ti:ti + 1], colb[:, 21:22], 127.0, 0.0,
                                ALU.mult, ALU.add)
                v.tensor_scalar(icols2[:, ti:ti + 1], colb[:, 20:21], 1.0 / 127.0, 0.0,
                                ALU.mult, ALU.add)
                v.drain()
                v.tensor_scalar(icols2[:, ti:ti + 1], icols2[:, ti:ti + 1], scw[:, 1:2],
                                0.0, ALU.mult, ALU.add)
                v.drain()
            inc(v.nop(), "VV")
        V(v39)
        bar()

        wo2_tt = {}
        for SB in range(8):
            ws3 = wffn[:, 0:16384].rearrange("p (c j) -> p c j", j=2048)

            def g10(g, SB=SB, ws3=ws3):
                if SB >= 1:
                    g.wait_ge(sems["TT"], wo2_tt[SB - 1])
                src = woutT_i.rearrange("(c p) j -> p c j", p=128)[:, SB * 8:(SB + 1) * 8, :]
                inc(g.dma_start(out=ws3[:, 0:4, :], in_=src[:, 0:4, :]), "WF0", 16)
                inc(g.dma_start(out=ws3[:, 4:8, :], in_=src[:, 4:8, :]), "WF0", 16)
            G(g10)
            wln = n["WF0"]

            def wq1b(v, ws3=ws3, wln=wln):
                v.wait_ge(sems["WF0"], wln)
                inc(v.tensor_scalar(ws3[:, :, :], ws3[:, :, :], scw[:, 3:4], RB,
                                    ALU.mult, ALU.add), "VV")
            V(wq1b)
            q1n = n["VV"]

            def wq2b(g, ws3=ws3, q1n=q1n):
                g.wait_ge(sems["VV"], q1n)
                inc(g.tensor_scalar(ws3[:, :, :], ws3[:, :, :], RB + 1.0, RB - 1.0,
                                    ALU.min, ALU.max), "GG")
            G(wq2b)
            q2n = n["GG"]

            def wq3b(v, ws3=ws3, q2n=q2n):
                v.wait_ge(sems["GG"], q2n)
                inc(v.tensor_scalar(ws3[:, :, :], ws3[:, :, :], RB, 0.0, ALU.subtract,
                                    ALU.add), "VV")
            V(wq3b)
            wqnb = n["VV"]

            for ti in range(NT):
                def sy6(sy, SB=SB, ti=ti):
                    if (SB, ti) == (0, 0):
                        sy.wait_ge(sems["SP"], sp_mid)
                    sy.wait_ge(sems["VV"], n["VV"])
                    inc(sy.dma_start(out=midst[:, :],
                                     in_=mid3[ti][:, SB * 1024:(SB + 1) * 1024]), "MD", 16)
                SY(sy6)
                xlm = n["MD"]

                def v40(v, ti=ti, xlm=xlm):
                    v.wait_ge(sems["MD"], xlm)
                    inc(v.tensor_scalar(qmstage[:, :], midst[:, :], qcols2[:, ti:ti + 1],
                                        RBF, ALU.mult, ALU.add), "VV")
                V(v40)
                vq1 = n["VV"]

                def g11(g, vq1=vq1):
                    g.wait_ge(sems["VV"], vq1)
                    inc(g.tensor_scalar(qmstage[:, :], qmstage[:, :], RBF + 127.0,
                                        RBF - 127.0, ALU.min, ALU.max), "GG")
                G(g11)
                gq1 = n["GG"]

                def v41(v, gq1=gq1):
                    v.wait_ge(sems["GG"], gq1)
                    inc(v.tensor_scalar(qmidt[:, :], qmstage[:, :], RBF, 0.0,
                                        ALU.subtract, ALU.add), "VV")
                V(v41)
                qn = n["VV"]

                T(lambda t, qn=qn: t.wait_ge(sems["VV"], qn))
                qfn = tr_pipe(qmidt, lambda v, j, ps: v.tensor_copy(
                    qmidf[:, j * 128:(j + 1) * 128], ps), 8)

                def t24(t, ws3=ws3, wln=wln, qfn=qfn, wqnb=wqnb):
                    t.wait_ge(sems["VV"], wqnb)
                    t.wait_ge(sems["VV"], qfn)
                    for nn in range(4):
                        if nn in pb_free:
                            t.wait_ge(sems["VV"], pb_free[nn])
                        for hc in range(8):
                            mm = t.matmul(bk(nn)[:, :], qmidf[:, hc * 128:(hc + 1) * 128],
                                          ws3[:, hc, nn * 512:(nn + 1) * 512],
                                          start=(hc == 0), stop=(hc == 7))
                        inc(mm, "TT")
                T(t24)
                ttn = n["TT"]
                wo2_tt[SB] = ttn

                def v43(v, ti=ti, ttn=ttn):
                    for nn in range(4):
                        v.wait_ge(sems["TT"], ttn - 4 + nn + 1)
                        ac = acc[:, ti * D + nn * 512:ti * D + (nn + 1) * 512]
                        pb_free[nn] = inc(v.tensor_tensor(ac, ac, bk(nn)[:, :], ALU.add), "VV")
                V(v43)
        bar()

        xl_prev = {}
        for ti in range(NT):
            at = acc[:, ti * D:(ti + 1) * D]

            def sy7(sy, ti=ti):
                if ti >= 1:
                    sy.wait_ge(sems["VV"], xl_prev[ti - 1])
                inc(sy.dma_start(out=xf[:, :], in_=x2s3[ti]), "XL0", 16)
            SY(sy7)
            xln = n["XL0"]

            def v44(v, ti=ti, at=at):
                inc(v.tensor_scalar(at, at, icols2[:, ti:ti + 1], 0.0, ALU.mult,
                                    ALU.add), "VV")
            V(v44)
            vo1 = n["VV"]

            def g12(g, vo1=vo1, at=at):
                g.wait_ge(sems["VV"], vo1)
                inc(g.tensor_tensor(at, at, gbt[:, :], ALU.mult), "GG")
            G(g12)
            go2 = n["GG"]

            def v45(v, ti=ti, xln=xln, go2=go2, at=at):
                v.wait_ge(sems["XL0"], xln)
                v.wait_ge(sems["GG"], go2)
                xl_prev[ti] = inc(v.tensor_tensor(at, at, xf[:, :], ALU.add), "VV")
            V(v45)
            vfin = n["VV"]

            def sy8(sy, ti=ti, vfin=vfin, at=at):
                sy.wait_ge(sems["VV"], vfin)
                inc(sy.dma_start(out=y3[ti], in_=at), "ST", 16)
            SY(sy8)
        stf = n["ST"]
        SY(lambda sy: sy.wait_ge(sems["ST"], stf))
        esp7.close()

    return nc


_NC_CACHE = {}


def kernel(L0_attn_x, L0_attn_Wq, L0_attn_Wk, L0_attn_Wv, L0_attn_Wgk1,
           L0_attn_Wgk2, L0_attn_bgk2, L0_attn_Wg, L0_attn_gnorm_w, L0_attn_Wo,
           L0_ffn_norm_w, L0_ffn_w_gv, L0_ffn_w_out, L0_c, L0_wmod):
    f32 = np.float32
    ca = lambda a: np.ascontiguousarray(np.asarray(a, f32))
    x = ca(L0_attn_x)
    wq, wk, wv, wg, wo = ca(L0_attn_Wq), ca(L0_attn_Wk), ca(L0_attn_Wv), \
        ca(L0_attn_Wg), ca(L0_attn_Wo)
    wgk1, wgk2, bgk2 = ca(L0_attn_Wgk1), ca(L0_attn_Wgk2), ca(L0_attn_bgk2)
    gnw, nw = ca(L0_attn_gnorm_w), ca(L0_ffn_norm_w)
    wgv, wout = ca(L0_ffn_w_gv), ca(L0_ffn_w_out)
    cc, wmod = ca(L0_c), ca(L0_wmod)

    wgvT = np.ascontiguousarray(wgv.T)
    woutT = np.ascontiguousarray(wout.T)
    bgk2c = np.ascontiguousarray(bgk2.reshape(16, 128).T)
    gnw2 = np.ascontiguousarray(np.tile(gnw, 2)[None, :])
    nwr = np.ascontiguousarray(nw[None, :])
    scanm = np.ones((128, SEG), f32)
    scanm[:, 0::64] = 0.0
    umask = np.ascontiguousarray(np.triu(np.ones((64, 64), f32)))
    ident = np.eye(128, dtype=f32)

    if "nc" not in _NC_CACHE:
        try:
            _NC_CACHE["nc"] = build_nc()
        except StopBuild:
            _NC_CACHE["nc"] = _LAST["nc"]
    nc = _NC_CACHE["nc"]

    in_maps = []
    for c in range(8):
        b, s = c // 4, c % 4
        smask = np.zeros((128, 8), f32)
        for j in range(8):
            if j // 4 == b and j % 4 < s:
                smask[:, j] = 1.0
        bsel = np.zeros((8, 8), f32)
        for r in range(8):
            if r // 4 == b:
                bsel[r, r % 4] = 1.0
            bsel[r, 4] = 1.0
        in_maps.append({
            "x_seg": np.ascontiguousarray(x[b, s * SEG:(s + 1) * SEG, :]),
            "cbc": np.ascontiguousarray(cc[b].reshape(16, 128).T),
            "wmod_q": np.ascontiguousarray(wmod[:, s * 3072:(s + 1) * 3072]),
            "wq": wq, "wk": wk, "wv": wv, "wg": wg, "wo": wo,
            "wgk1": wgk1, "wgk2": wgk2, "bgk2c": bgk2c, "gnw2": gnw2,
            "nw_row": nwr, "wgvT": wgvT, "woutT": woutT,
            "wgv_sh": np.ascontiguousarray(wgv[c * 2048:(c + 1) * 2048, :]),
            "wout_sh": np.ascontiguousarray(wout[c * 256:(c + 1) * 256, :]),
            "scanm": scanm, "umask": umask, "ident": ident,
            "smask": smask, "bsel": bsel,
        })
    res = run_bass_kernel_spmd(nc, in_maps, core_ids=list(range(8)))
    out = np.empty((2, 4096, D), f32)
    for c in range(8):
        b, s = c // 4, c % 4
        out[b, s * SEG:(s + 1) * SEG, :] = res.results[c]["y"]
    return out


# version 10 (final)
# speedup vs baseline: 1.1740x; 1.0759x over previous
import numpy as np
from contextlib import ExitStack

import concourse.bass as bass
import concourse.mybir as mybir
from concourse.bass_utils import run_bass_kernel_spmd

F32 = mybir.dt.float32
BF16 = mybir.dt.bfloat16
AF = mybir.ActivationFunctionType
ALU = mybir.AluOpType
AX = mybir.AxisListType

D = 2048
HD = 128
SEG = 1024
NT = 8
GH = 8
EPS = 1e-5
QSC = 0.08838834764831845
RB = 192.0
RBF = 12582912.0
HID = 8192
MODW = 12288
MROW = MODW + 16
DEBUG = False
STOP_AFTER = None


class StopBuild(Exception):
    pass


_LAST = {}


def build_nc():
    nc = bass.Bass("TRN2", target_bir_lowering=False)
    _LAST["nc"] = nc
    di = lambda n_, s_, dt=F32: nc.dram_tensor(n_, s_, dt, kind="ExternalInput").ap()

    x_in = di("x_seg", [SEG, D])
    cbc = di("cbc", [128, 16])
    wmod_q = di("wmod_q", [D, 3072])
    wq_i, wk_i, wv_i, wg_i, wo_i = (di(n_, [D, D]) for n_ in ("wq", "wk", "wv", "wg", "wo"))
    wgk1_i = di("wgk1", [D, 16])
    wgk2_i = di("wgk2", [16, D])
    bgk2_i = di("bgk2c", [128, 16])
    gnw_i = di("gnw2", [1, 256])
    nw_i = di("nw_row", [1, D])
    wgvT_i = di("wgvT", [D, 2 * HID])
    woutT_i = di("woutT", [HID, D])
    wgvsh_i = di("wgv_sh", [D, D])
    woutsh_i = di("wout_sh", [256, HID])
    scanm_i = di("scanm", [128, SEG])
    umask_i = di("umask", [64, 64])
    ident_i = di("ident", [128, 128])
    smask_i = di("smask", [128, 8])
    bsel_i = di("bsel", [8, 8])
    y_out = nc.dram_tensor("y", [SEG, D], F32, kind="ExternalOutput").ap()

    cc1i = nc.dram_tensor("cc1i", [1, MROW], F32)
    cc1o = nc.dram_tensor("cc1o", [8, MROW], F32)
    cc2i = [nc.dram_tensor(f"cc2i{g}", [128, 1056], F32) for g in range(2)]
    cc2o = [nc.dram_tensor(f"cc2o{g}", [1024, 1056], F32) for g in range(2)]
    modscr = nc.dram_tensor("modscr", [1, MROW], F32)
    if DEBUG:
        dbg_h1 = nc.dram_tensor("dbg_h1", [128, 16 * SEG], BF16)
        dbg_og = nc.dram_tensor("dbg_og", [128, 16 * SEG], BF16)
        dbg_afm = nc.dram_tensor("dbg_afm", [128, 16 * SEG], BF16)
    x2scr = nc.dram_tensor("x2scr", [SEG, D], F32)
    midscr = nc.dram_tensor("midscr", [SEG, HID], BF16)

    x3 = x_in.rearrange("(t p) n -> t p n", p=128)
    y3 = y_out.rearrange("(t p) n -> t p n", p=128)
    x2s3 = x2scr.ap().rearrange("(t p) n -> t p n", p=128)
    mid3 = midscr.ap().rearrange("(t p) n -> t p n", p=128)

    es0 = ExitStack()
    sb = lambda n_, s_, dt: es0.enter_context(nc.sbuf_tensor(n_, s_, dt))
    pbank = [es0.enter_context(nc.psum_tensor(f"bk{i}", [128, 512], F32)) for i in range(8)]
    bk = lambda i: pbank[i].ap()
    bkb = lambda i: pbank[i].ap().bitcast(BF16)

    actq = sb("actq", [128, 16 * SEG], BF16)
    scanms = sb("scanms", [128, SEG], F32)
    umasks = sb("umasks", [64, 64], BF16)
    ident = sb("ident_s", [128, 128], BF16)
    smasks = sb("smasks", [128, 8], F32)
    bsels = sb("bsels", [8, 8], F32)
    bgk2s = sb("bgk2s", [128, 16], F32)
    cbs = sb("cbs", [128, 16], F32)
    gnwt = sb("gnwt", [128, 256], BF16)
    epsc = sb("epsc", [128, 1], F32)
    onesr = sb("onesr", [1, 128], F32)
    onescl = sb("onescl", [128, 1], F32)
    sbb16 = sb("sbb16", [128, 16], BF16)
    colb = sb("colb", [128, 32], F32)
    qcols = sb("qcols", [128, NT], F32)
    qcols2 = sb("qcols2", [128, NT], F32)
    icols2 = sb("icols2", [128, NT], F32)
    svcol = sb("svcol", [128, NT], F32)
    ambuf = sb("ambuf", [128, NT * 32], F32)
    scw = sb("scw", [128, 4], F32)
    gacol = sb("gacol", [128, 16], F32)
    rbneg = sb("rbneg", [128, 1], F32)

    espW = ExitStack()
    ogf = espW.enter_context(nc.sbuf_tensor("ogf", [128, 16 * SEG], BF16))
    wring = espW.enter_context(nc.sbuf_tensor("wring", [128, 2 * 4096], BF16))
    esp01 = ExitStack()
    cpA = esp01.enter_context(nc.sbuf_tensor("cpA", [128, 2 * D], BF16))

    sems = {k: es0.enter_context(nc.semaphore(k))
            for k in ["WL0", "WL1", "WF0", "WF1", "XL0", "XL1", "CN", "CS", "RNK",
                      "MD", "SP", "ST", "TT", "VV", "AA", "GG", "CC", "PB"]}
    n = {k: 0 for k in sems}

    with es0, nc.Block() as blk:
        T = lambda f: blk.tensor(f)
        V = lambda f: blk.vector(f)
        A = lambda f: blk.scalar(f)
        G = lambda f: blk.gpsimd(f)
        SY = lambda f: blk.sync(f)

        def inc(instr, nm, k=1):
            instr.then_inc(sems[nm], k)
            n[nm] += k
            return n[nm]

        def bar():
            dma_owned = {"gpsimd": ["WL0", "WL1", "WF0", "WF1", "CN"],
                         "sync": ["XL0", "XL1", "CS", "RNK", "MD", "ST"],
                         "scalar": ["SP"]}
            for attr in ("tensor", "vector", "scalar", "gpsimd", "sync"):
                def _f(e, attr=attr):
                    if attr in ("vector", "scalar", "gpsimd"):
                        e.drain()
                    for s_ in dma_owned.get(attr, []):
                        if n[s_]:
                            e.wait_ge(sems[s_], n[s_])
                    inc(e.nop(), "PB")
                getattr(blk, attr)(_f)
            tgt = n["PB"]
            for attr in ("tensor", "vector", "scalar", "gpsimd", "sync"):
                getattr(blk, attr)(lambda e, tgt=tgt: e.wait_ge(sems["PB"], tgt))

        pb_free = {}

        def ckstop(tag, zb=None):
            if STOP_AFTER is not None and tag == STOP_AFTER:
                def vz(v):
                    v.memset(zb[:, 0:1024], 0.0)
                    v.drain()
                    inc(v.nop(), "VV")
                V(vz)
                vzn = n["VV"]

                def fin(sy):
                    sy.wait_ge(sems["VV"], vzn)
                    for ti in range(NT):
                        inc(sy.dma_start(out=y3[ti][:, 0:1024], in_=zb[:, 0:1024]), "ST", 16)
                        inc(sy.dma_start(out=y3[ti][:, 1024:2048], in_=zb[:, 0:1024]), "ST", 16)
                    sy.wait_ge(sems["ST"], n["ST"])
                SY(fin)
                raise StopBuild()

        def tr_pipe(src_tile, dst_fn, nblk, bank0=6):
            vv0 = n["VV"]
            tt0 = n["TT"]

            def tf(t):
                for j in range(nblk):
                    if j >= 2:
                        t.wait_ge(sems["VV"], vv0 + (j - 2) + 1)
                    else:
                        bnk = bank0 + j % 2
                        if bnk in pb_free:
                            t.wait_ge(sems["VV"], pb_free[bnk])
                    tr = t.transpose(bkb(bank0 + j % 2)[:, 0:128],
                                     src_tile[:, j * 128:(j + 1) * 128], ident[:, :])
                    inc(tr, "TT")
            T(tf)

            def vf(v):
                for j in range(nblk):
                    v.wait_ge(sems["TT"], tt0 + j + 1)
                    cp = dst_fn(v, j, bkb(bank0 + j % 2)[:, 0:128])
                    cnt = inc(cp, "VV")
                    pb_free[bank0 + j % 2] = cnt
            V(vf)
            return n["VV"]

        esp0 = ExitStack()
        sb0 = lambda n_, s_, dt: esp0.enter_context(nc.sbuf_tensor(n_, s_, dt))
        modrow = sb0("modrow", [1, MROW], F32)
        g1s = sb0("g1s", [8, 3104], F32)
        xt0 = sb0("xt0", [128, 2 * D], F32)
        wsum = sb0("wsum", [128, 32], F32)
        wmslot = sb0("wmslot", [128, 2 * 3072], F32)

        def g0(g):
            for dst, src in [(scanms[:, :], scanm_i), (umasks[:, :], umask_i),
                             (ident[:, :], ident_i), (smasks[:, :], smask_i),
                             (bsels[:, :], bsel_i), (bgk2s[:, :], bgk2_i),
                             (cbs[:, :], cbc), (gnwt[0:1, :], gnw_i)]:
                inc(g.dma_start(out=dst, in_=src), "CN", 16)
        G(g0)
        cn_c = n["CN"]

        V(lambda v: (v.memset(epsc[:, :], EPS), v.memset(onesr[:, :], 1.0),
                     v.memset(onescl[:, :], 1.0), v.memset(rbneg[:, :], -RB),
                     v.drain(), inc(v.nop(), "VV")))

        def a0(a):
            a.wait_ge(sems["CN"], cn_c)
            a.activation(colb[:, 0:16], cbs[:, :], AF.Silu)
            a.drain()
            inc(a.nop(), "AA")
        A(a0)
        aa0 = n["AA"]
        V(lambda v: (v.wait_ge(sems["AA"], aa0), inc(v.nop(), "VV")))
        vv_scb = n["VV"]

        p0tt = {}
        wm = wmod_q.rearrange("(c p) j -> c p j", p=128)
        for c in range(16):
            sp_ = f"WL{c % 2}"

            def g1(g, c=c, sp_=sp_):
                if c >= 2:
                    g.wait_ge(sems["TT"], p0tt[c - 2])
                slot = wmslot[:, (c % 2) * 3072:(c % 2) * 3072 + 3072]
                inc(g.dma_start(out=slot, in_=wm[c]), sp_, 16)
            G(g1)
            wlc = n[sp_]

            def t0(t, c=c, sp_=sp_, wlc=wlc):
                if c == 0:
                    t.wait_ge(sems["VV"], vv_scb)
                t.wait_ge(sems[sp_], wlc)
                slot = wmslot[:, (c % 2) * 3072:(c % 2) * 3072 + 3072]
                for nn in range(6):
                    mm = t.matmul(bk(nn)[0:1, :], colb[:, c:c + 1],
                                  slot[:, nn * 512:(nn + 1) * 512],
                                  start=(c == 0), stop=(c == 15))
                p0tt[c] = inc(mm, "TT")
            T(t0)
        tt_mod = n["TT"]

        gv = wgvsh_i.rearrange("(c p) j -> c p j", p=128)
        wo4 = woutsh_i.rearrange("(c p) (d j) -> c d p j", p=128, j=D)
        wred_vv = {}
        for i in range(24):
            sp_ = f"XL{i % 2}"

            def sy0(sy, i=i, sp_=sp_):
                if i >= 2:
                    sy.wait_ge(sems["VV"], wred_vv[i - 2])
                src = gv[i] if i < 16 else wo4[(i - 16) // 4, (i - 16) % 4]
                inc(sy.dma_start(out=xt0[:, (i % 2) * D:(i % 2) * D + D], in_=src), sp_, 16)
            SY(sy0)
            xlc = n[sp_]

            def v0(v, i=i, sp_=sp_, xlc=xlc):
                v.wait_ge(sems[sp_], xlc)
                wred_vv[i] = inc(
                    v.tensor_reduce(wsum[:, i:i + 1], xt0[:, (i % 2) * D:(i % 2) * D + D],
                                    AX.X, ALU.add, apply_absolute_value=True), "VV")
            V(v0)

        def v0b(v):
            v.drain()
            v.tensor_reduce(wsum[:, 30:31], wsum[:, 0:16], AX.X, ALU.add)
            v.tensor_reduce(wsum[:, 31:32], wsum[:, 16:24], AX.X, ALU.add)
            v.drain()
            inc(v.nop(), "VV")
        V(v0b)
        vv_ws = n["VV"]

        def t1(t):
            t.wait_ge(sems["VV"], vv_ws)
            inc(t.matmul(bk(6)[0:1, 0:2], onescl[:, 0:1], wsum[:, 30:32],
                         start=True, stop=True), "TT")
        T(t1)
        tt_ws = n["TT"]

        def v1(v):
            v.memset(modrow[:, :], 0.0)
            v.drain()
            v.wait_ge(sems["TT"], tt_mod)
            for nn in range(6):
                v.tensor_copy(modrow[0:1, nn * 512:(nn + 1) * 512], bk(nn)[0:1, :])
            v.wait_ge(sems["TT"], tt_ws)
            v.tensor_copy(modrow[0:1, MODW:MODW + 2], bk(6)[0:1, 0:2])
            v.drain()
            inc(v.nop(), "VV")
        V(v1)
        vv_row = n["VV"]

        def g2(g):
            g.wait_ge(sems["VV"], vv_row)
            inc(g.dma_start(out=cc1i.ap(), in_=modrow[0:1, :]), "CN", 16)
            g.wait_ge(sems["CN"], n["CN"])
            inc(g.collective_compute(
                "AllGather", ALU.bypass, replica_groups=[list(range(8))],
                ins=[cc1i.ap().opt()], outs=[cc1o.ap().opt()]), "CC")
            g.wait_ge(sems["CC"], n["CC"])
            inc(g.dma_start(out=g1s[:, 0:3072], in_=cc1o.ap()[:, 0:3072]), "CN", 16)
            inc(g.dma_start(out=g1s[:, 3072:3074], in_=cc1o.ap()[:, MODW:MODW + 2]),
                "CN", 16)
        G(g2)
        wl_g1 = n["CN"]

        for q in range(4):
            def t2(t, q=q):
                if q == 0:
                    t.wait_ge(sems["CN"], wl_g1)
                else:
                    t.wait_ge(sems["VV"], n["VV"])
                for nn in range(6):
                    mm = t.matmul(bk(nn)[0:1, :], bsels[:, q:q + 1],
                                  g1s[:, nn * 512:(nn + 1) * 512], start=True, stop=True)
                inc(mm, "TT")
            T(t2)
            ttq = n["TT"]

            def v2(v, q=q, ttq=ttq):
                v.wait_ge(sems["TT"], ttq)
                for nn in range(6):
                    v.tensor_copy(modrow[0:1, q * 3072 + nn * 512:q * 3072 + (nn + 1) * 512],
                                  bk(nn)[0:1, :])
                v.drain()
                inc(v.nop(), "VV")
            V(v2)

        def t2b(t):
            t.wait_ge(sems["VV"], n["VV"])
            inc(t.matmul(bk(6)[0:1, 0:2], bsels[:, 4:5], g1s[:, 3072:3074],
                         start=True, stop=True), "TT")
        T(t2b)
        tt_tot = n["TT"]

        def v3(v):
            v.wait_ge(sems["TT"], tt_tot)
            v.tensor_copy(modrow[0:1, MODW:MODW + 2], bk(6)[0:1, 0:2])
            v.drain()
            v.tensor_scalar(modrow[0:1, MODW + 2:MODW + 3], modrow[0:1, MODW:MODW + 1],
                            1.0 / (2.0 * HID * D), 0.0, ALU.mult, ALU.add)
            v.tensor_scalar(modrow[0:1, MODW + 3:MODW + 4], modrow[0:1, MODW + 1:MODW + 2],
                            1.0 / (1.0 * HID * D), 0.0, ALU.mult, ALU.add)
            v.drain()
            for base in (2 * D, 5 * D):
                r0 = modrow[0:1, base:base + D]
                v.tensor_scalar(xt0[0:1, 0:D], r0, -1.0, 0.25, ALU.mult, ALU.max)
                v.drain()
                v.tensor_tensor(xt0[0:1, 0:D], r0, xt0[0:1, 0:D], ALU.max)
                v.tensor_scalar(xt0[0:1, D:2 * D], r0, 0.0, 2.0, ALU.is_ge, ALU.mult)
                v.drain()
                v.scalar_tensor_tensor(r0, xt0[0:1, D:2 * D], 1.0, xt0[0:1, 0:D],
                                       ALU.subtract, ALU.mult)
                v.drain()
            for base in (1 * D, 4 * D):
                v.tensor_scalar(modrow[0:1, base:base + D], modrow[0:1, base:base + D],
                                1.0, 0.0, ALU.add, ALU.add)
            v.drain()
            inc(v.nop(), "VV")
        V(v3)
        vv_mod = n["VV"]

        def sy1(a):
            a.wait_ge(sems["VV"], vv_mod)
            inc(a.dma_start(out=modscr.ap(), in_=modrow[0:1, :]), "SP", 16)
        A(sy1)
        sp_modscr = n["SP"]

        def bcast_row(row_ap, dst_tile, nparts=4):
            ttb = n["TT"]

            def tb(t):
                t.wait_ge(sems["VV"], n["VV"])
                for nn in range(nparts):
                    mm = t.matmul(bk(nn)[:, 0:512], onesr[0:1, :],
                                  row_ap[0:1, nn * 512:(nn + 1) * 512], start=True, stop=True)
                inc(mm, "TT")
            T(tb)
            ttb = n["TT"]

            def vb(v):
                v.wait_ge(sems["TT"], ttb)
                for nn in range(nparts):
                    v.tensor_copy(dst_tile[:, nn * 512:(nn + 1) * 512], bk(nn)[:, 0:512])
                v.drain()
                inc(v.nop(), "VV")
            V(vb)

        def tg(t):
            t.wait_ge(sems["VV"], vv_mod)
            inc(t.matmul(bk(7)[:, 0:2], onesr[0:1, :], modrow[0:1, MODW + 2:MODW + 4],
                         start=True, stop=True), "TT")
        T(tg)
        tt_s = n["TT"]

        def vg(v):
            v.wait_ge(sems["TT"], tt_s)
            v.tensor_copy(scw[:, 0:2], bk(7)[:, 0:2])
            v.drain()
            v.reciprocal(scw[:, 2:3], scw[:, 0:1])
            v.drain()
            v.reciprocal(scw[:, 3:4], scw[:, 1:2])
            v.drain()
            inc(v.nop(), "VV")
        V(vg)

        bcast_row(modrow[0:1, 0:D], cpA[:, 0:D])
        bcast_row(modrow[0:1, D:2 * D], cpA[:, D:2 * D])

        def sy2(sy):
            sy.wait_ge(sems["VV"], n["VV"])
            inc(sy.dma_start(out=xt0[0:1, 0:256], in_=gnw_i), "CS", 16)
        SY(sy2)
        xl_g = n["CS"]

        def tgn(t):
            t.wait_ge(sems["CS"], xl_g)
            t.wait_ge(sems["VV"], n["VV"])
            inc(t.matmul(bk(7)[:, 0:256], onesr[0:1, :], xt0[0:1, 0:256],
                         start=True, stop=True), "TT")
        T(tgn)
        tt_gn = n["TT"]

        def vgn(v):
            v.wait_ge(sems["TT"], tt_gn)
            v.tensor_copy(gnwt[:, :], bk(7)[:, 0:256])
            v.drain()
            inc(v.nop(), "VV")
        V(vgn)
        bar()
        ckstop("P0", xt0)
        esp0.close()

        esp1 = ExitStack()
        sb1 = lambda n_, s_, dt: esp1.enter_context(nc.sbuf_tensor(n_, s_, dt))
        xt = sb1("xt", [128, 2 * D], F32)
        lnb = sb1("lnb", [128, D], F32)
        lnb2 = sb1("lnb2", [128, D], F32)
        sqj = [sb1(f"sqj{i}", [128, D], F32) for i in range(2)]
        hstage = sb1("hstage", [128, D], BF16)

        sa_t, ca1_t = cpA[:, 0:D], cpA[:, D:2 * D]

        def ln_tile(ti, xv, xln, cmul_t, cadd_t, done_d, var_d):
            def a1(a, ti=ti, xv=xv, xln=xln):
                a.wait_ge(sems[xln[0]], xln[1])
                if ti >= 1:
                    a.wait_ge(sems["VV"], var_d[ti - 1])
                a.activation(sqj[ti % 2][:, :], xv, AF.Square, accum_out=colb[:, 2:3])
                a.drain()
                inc(a.nop(), "AA")
            A(a1)
            aan = n["AA"]

            def v5(v, xv=xv, xln=xln, aan=aan, ti=ti):
                v.wait_ge(sems[xln[0]], xln[1])
                v.tensor_reduce(colb[:, 0:1], xv, AX.X, ALU.add)
                v.drain()
                v.tensor_scalar(colb[:, 1:2], colb[:, 0:1], 1.0 / D, 0.0, ALU.mult, ALU.add)
                v.drain()
                v.tensor_tensor(colb[:, 4:5], colb[:, 1:2], colb[:, 1:2], ALU.mult)
                v.wait_ge(sems["AA"], aan)
                v.drain()
                v.scalar_tensor_tensor(colb[:, 5:6], colb[:, 2:3], 1.0 / D, colb[:, 4:5],
                                       ALU.mult, ALU.subtract)
                v.drain()
                var_d[ti] = inc(v.nop(), "VV")
            V(v5)
            vvn = n["VV"]

            def a2(a, vvn=vvn):
                a.wait_ge(sems["VV"], vvn)
                a.activation(colb[:, 6:7], colb[:, 5:6], AF.Ln, bias=epsc[:, 0:1])
                a.drain()
                a.activation(colb[:, 7:8], colb[:, 6:7], AF.Exp, scale=-0.5)
                a.drain()
                inc(a.nop(), "AA")
            A(a2)
            aan2 = n["AA"]

            def v6(v, xv=xv, aan2=aan2):
                v.wait_ge(sems["AA"], aan2)
                v.scalar_tensor_tensor(colb[:, 8:9], colb[:, 1:2], -1.0, colb[:, 7:8],
                                       ALU.mult, ALU.mult)
                v.drain()
                v.tensor_scalar(lnb[:, :], xv, colb[:, 7:8], colb[:, 8:9], ALU.mult, ALU.add)
                v.drain()
                v.tensor_tensor(lnb2[:, :], lnb[:, :], cmul_t, ALU.mult)
                v.drain()
                v.tensor_tensor(hstage[:, :], lnb2[:, :], cadd_t, ALU.add)
                v.drain()
                inc(v.nop(), "VV")
            V(v6)
            return n["VV"]

        p1_done, p1_var = {}, {}
        for ti in range(NT):
            xv = xt[:, (ti % 2) * D:(ti % 2) * D + D]

            sp_ = f"XL{ti % 2}"

            def syl(sy, ti=ti, xv=xv, sp_=sp_):
                if ti >= 2:
                    sy.wait_ge(sems["VV"], p1_done[ti - 2])
                inc(sy.dma_start(out=xv, in_=x3[ti]), sp_, 16)
            SY(syl)
            xln = (sp_, n[sp_])
            h1n = ln_tile(ti, xv, xln, ca1_t, sa_t, p1_done, p1_var)

            T(lambda t, h1n=h1n: t.wait_ge(sems["VV"], h1n))
            p1_done[ti] = tr_pipe(
                hstage,
                lambda v, j, ps, ti=ti: v.tensor_copy(
                    actq[:, j * SEG + ti * 128:j * SEG + (ti + 1) * 128], ps), 16)
        bar()
        ckstop("P1", xt)
        if DEBUG:
            def dmp1(sy):
                inc(sy.dma_start(out=dbg_h1.ap(), in_=actq[:, :]), "ST", 16)
                sy.wait_ge(sems["ST"], n["ST"])
            SY(dmp1)
            bar()
        esp1.close()
        esp01.close()

        espA = ExitStack()
        sbA = lambda n_, s_, dt: espA.enter_context(nc.sbuf_tensor(n_, s_, dt))
        qf = sbA("qf", [128, GH * SEG], BF16)
        kf = sbA("kf", [128, GH * SEG], BF16)
        Gf = sbA("Gf", [128, GH * SEG], BF16)
        vtm = sbA("vtm", [128, GH * SEG], BF16)
        sgtm = sbA("sgtm", [128, GH * SEG], BF16)
        EG = sbA("EG", [128, SEG], F32)
        EGi = sbA("EGi", [128, SEG], F32)
        tmpb = None
        hrow = sbA("hrow", [128, SEG], F32)
        hrow2 = sbA("hrow2", [128, SEG], F32)
        qhb = sbA("qhb", [128, SEG], BF16)
        khb = sbA("khb", [128, SEG], BF16)
        kwt = sbA("kwt", [128, 1280], BF16)
        Sst = sbA("Sst", [128, GH * HD], F32)
        Scur = sbA("Scur", [128, HD], F32)
        sgbuf = sbA("sgbuf", [128, HD], BF16)
        s2buf = sbA("s2buf", [128, HD], F32)
        otm = hrow2
        atb = sbA("atb", [128, 64], BF16)
        zbA = sbA("zbA", [128, 16], F32)
        zbB = sbA("zbB", [128, 16], F32)
        glc = sbA("glc", [128, 16], F32)
        ssq = sbA("ssq", [128, 8], F32)
        ctile = sbA("ctile", [128, 1056], F32)
        ogstage = sbA("ogstage", [128, SEG], BF16)
        kwb = ogstage
        tmpb = EG[:, :].bitcast(BF16)[0:16, 0:1024]
        state = {"tmpb_vv": 0, "ttog_prev": 0}
        V(lambda v: (v.memset(ctile[:, 1032:1056], 0.0), v.drain(), inc(v.nop(), "VV")))

        def proj(grp):
            co = grp * 1024
            if True:
                def gld(g):
                    wg1 = wgk1_i.rearrange("(c p) r -> p c r", p=128)
                    dst = kwt[:, 0:256].rearrange("p (c r) -> p c r", r=16)
                    inc(g.dma_start(out=dst, in_=wg1), "CN", 16)
                G(gld)
                wl0 = n["CN"]

                def t6(t):
                    t.wait_ge(sems["CN"], wl0)
                    for th in range(2):
                        for c in range(16):
                            mm = t.matmul(bk(6 + th)[0:16, :], kwt[:, c * 16:(c + 1) * 16],
                                          actq[:, c * SEG + th * 512:c * SEG + th * 512 + 512],
                                          start=(c == 0), stop=(c == 15))
                        inc(mm, "TT")
                T(t6)
                tt6 = n["TT"]

                def v8(v):
                    v.wait_ge(sems["TT"], tt6)
                    v.tensor_copy(tmpb[0:16, 0:512], bk(6)[0:16, :])
                    v.tensor_copy(tmpb[0:16, 512:1024], bk(7)[0:16, :])
                    v.drain()
                    state["tmpb_vv"] = inc(v.nop(), "VV")
                V(v8)

            def gld2(g, grp=grp):
                g.wait_ge(sems["VV"], state["tmpb_vv"])
                inc(g.dma_start(out=kwt[0:16, 256:1280], in_=wgk2_i[:, co:co + 1024]), "CN", 16)
            G(gld2)
            wlgk = n["CN"]
            wgk2b = kwt[0:16, 256:1280]

            gk_aa, gk_vv = {}, {}
            for hh in range(GH):
                def t7(t, hh=hh):
                    t.wait_ge(sems["CN"], wlgk)
                    if hh >= 1:
                        t.wait_ge(sems["VV"], gk_vv[hh - 1])
                    for th in range(2):
                        mm = t.matmul(bk(6 + th)[:, :], wgk2b[:, hh * 128:(hh + 1) * 128],
                                      tmpb[0:16, th * 512:(th + 1) * 512], start=True, stop=True)
                    inc(mm, "TT")
                T(t7)
                ttg = n["TT"]

                def a3(a, hh=hh, ttg=ttg, grp=grp):
                    a.wait_ge(sems["TT"], ttg)
                    if hh >= 1:
                        a.wait_ge(sems["VV"], gk_vv[hh - 1])
                    col = bgk2s[:, grp * 8 + hh:grp * 8 + hh + 1]
                    for th in range(2):
                        sl = hrow[:, th * 512:(th + 1) * 512]
                        a.activation(sl, bk(6 + th)[:, :], AF.Abs, bias=col)
                        a.drain()
                        a.activation(sl, sl, AF.Exp, scale=-1.0)
                        a.drain()
                        a.activation(sl, sl, AF.Ln, bias=1.0)
                        a.drain()
                    gk_aa[hh] = inc(a.nop(), "AA")
                A(a3)

                def v9(v, hh=hh, grp=grp):
                    v.wait_ge(sems["AA"], gk_aa[hh])
                    col = bgk2s[:, grp * 8 + hh:grp * 8 + hh + 1]
                    for th in range(2):
                        v.tensor_scalar(hrow2[:, th * 512:(th + 1) * 512], bk(6 + th)[:, :],
                                        col, 0.0, ALU.add, ALU.min)
                    v.drain()
                    v.tensor_tensor(hrow2[:, :], hrow2[:, :], hrow[:, :], ALU.subtract)
                    v.drain()
                    v.tensor_tensor_scan(Gf[:, hh * SEG:(hh + 1) * SEG], scanms[:, :],
                                         hrow2[:, :], 0.0, ALU.mult, ALU.add)
                    v.drain()
                    gk_vv[hh] = inc(v.nop(), "VV")
                V(v9)

            strip_tt = {}
            for mi, (mname, wsrc) in enumerate(
                    [("q", wq_i), ("k", wk_i), ("v", wv_i), ("g", wg_i)]):
                for s4 in range(4):
                    sidx = mi * 4 + s4
                    slot = wring[:, (sidx % 2) * 4096:(sidx % 2) * 4096 + 4096]
                    c0 = co + s4 * 256

                    spn = f"WL{sidx % 2}"

                    def gld3(g, wsrc=wsrc, slot=slot, c0=c0, sidx=sidx, spn=spn):
                        if sidx >= 2:
                            g.wait_ge(sems["TT"], strip_tt[sidx - 2])
                        src = wsrc.rearrange("(c p) j -> p c j", p=128)[:, :, c0:c0 + 256]
                        dst = slot.rearrange("p (c j) -> p c j", j=256)
                        for q4 in range(4):
                            inc(g.dma_start(out=dst[:, q4 * 4:(q4 + 1) * 4, :],
                                            in_=src[:, q4 * 4:(q4 + 1) * 4, :]), spn, 16)
                    G(gld3)
                    wln = (spn, n[spn])

                    if mname in ("q", "k"):
                        dstb = qf if mname == "q" else kf

                        def t8(t, slot=slot, wln=wln):
                            t.wait_ge(sems[wln[0]], wln[1])
                            for u in range(4):
                                m2, th = u // 2, u % 2
                                if u in pb_free:
                                    t.wait_ge(sems["VV"], pb_free[u])
                                for kc in range(16):
                                    mm = t.matmul(
                                        bk(u)[:, :],
                                        slot[:, kc * 256 + m2 * 128:kc * 256 + m2 * 128 + 128],
                                        actq[:, kc * SEG + th * 512:kc * SEG + th * 512 + 512],
                                        start=(kc == 0), stop=(kc == 15))
                                inc(mm, "TT")
                        T(t8)
                        ttn = n["TT"]
                        strip_tt[sidx] = ttn

                        def v10(v, s4=s4, ttn=ttn, dstb=dstb, mname=mname):
                            for u in range(4):
                                m2, th = u // 2, u % 2
                                v.wait_ge(sems["TT"], ttn - 4 + u + 1)
                                hcol = (s4 * 2 + m2) * SEG + th * 512
                                if mname == "q":
                                    cp = v.tensor_scalar(dstb[:, hcol:hcol + 512], bk(u)[:, :],
                                                         QSC, 0.0, ALU.mult, ALU.add)
                                else:
                                    cp = v.tensor_copy(dstb[:, hcol:hcol + 512], bk(u)[:, :])
                                pb_free[u] = inc(cp, "VV")
                        V(v10)
                    else:
                        vvb9 = n["VV"]
                        aab9 = n["AA"]
                        is_v = (mname == "v")

                        def t9(t, slot=slot, wln=wln, vvb9=vvb9, aab9=aab9, is_v=is_v):
                            t.wait_ge(sems[wln[0]], wln[1])
                            for ti in range(NT):
                                bnk = ti % 4
                                if ti < 4:
                                    if bnk in pb_free:
                                        t.wait_ge(sems["VV"], pb_free[bnk])
                                    if (bnk, "a") in pb_free:
                                        t.wait_ge(sems["AA"], pb_free[(bnk, "a")])
                                elif is_v:
                                    t.wait_ge(sems["VV"], vvb9 + (ti - 4) + 1)
                                else:
                                    t.wait_ge(sems["AA"], aab9 + (ti - 4) + 1)
                                for kc in range(16):
                                    mm = t.matmul(
                                        bk(bnk)[:, 0:256],
                                        actq[:, kc * SEG + ti * 128:kc * SEG + (ti + 1) * 128],
                                        slot[:, kc * 256:(kc + 1) * 256],
                                        start=(kc == 0), stop=(kc == 15))
                                inc(mm, "TT")
                        T(t9)
                        ttn = n["TT"]
                        strip_tt[sidx] = ttn

                        if mname == "v":
                            def v11(v, s4=s4, ttn=ttn):
                                for ti in range(NT):
                                    bnk = ti % 4
                                    v.wait_ge(sems["TT"], ttn - 8 + ti + 1)
                                    dst = vtm.rearrange("p (h t) -> p h t", t=SEG)[
                                        :, 2 * s4:2 * s4 + 2, ti * 128:(ti + 1) * 128]
                                    src = bk(bnk)[:, 0:256].rearrange("p (h d) -> p h d", d=128)
                                    pb_free[bnk] = inc(v.tensor_copy(dst, src), "VV")
                            V(v11)
                        else:
                            sg_aa, sg_vv = {}, {}
                            vvb4 = n["VV"]

                            def a4(a, ttn=ttn, vvb4=vvb4):
                                for ti in range(NT):
                                    bnk = ti % 4
                                    a.wait_ge(sems["TT"], ttn - 8 + ti + 1)
                                    if ti >= 2:
                                        a.wait_ge(sems["VV"], vvb4 + (ti - 2) + 1)
                                    else:
                                        a.wait_ge(sems["VV"], vvb4)
                                    a.activation(ogstage[:, (ti % 2) * 256:(ti % 2) * 256 + 256],
                                                 bk(bnk)[:, 0:256], AF.Silu)
                                    a.drain()
                                    cnt = inc(a.nop(), "AA")
                                    sg_aa[ti] = cnt
                                    pb_free[(bnk, "a")] = cnt
                            A(a4)

                            def v12(v, s4=s4):
                                for ti in range(NT):
                                    v.wait_ge(sems["AA"], sg_aa[ti])
                                    dst = sgtm.rearrange("p (h t) -> p h t", t=SEG)[
                                        :, 2 * s4:2 * s4 + 2, ti * 128:(ti + 1) * 128]
                                    src = ogstage[:, (ti % 2) * 256:(ti % 2) * 256 + 256].rearrange(
                                        "p (h d) -> p h d", d=128)
                                    gsrc = gnwt[:, :].rearrange("p (h d) -> p h d", d=128)
                                    sg_vv[ti] = inc(v.tensor_tensor(dst, src, gsrc, ALU.mult), "VV")
                            V(v12)
            bar()

        def sweep1(grp):
            ds_vv = {}
            for hh in range(GH):
                Gh = Gf[:, hh * SEG:(hh + 1) * SEG]
                g3 = Gh.rearrange("p (c t) -> p c t", t=64)

                def a5(a, Gh=Gh, hh=hh):
                    if hh >= 1:
                        a.wait_ge(sems["VV"], ds_vv[hh - 1])
                    inc(a.activation(EGi[:, :], Gh, AF.Exp, scale=-1.0 / 16.0), "AA")
                A(a5)
                aan = n["AA"]

                def v13(v, g3=g3, aan=aan):
                    v.wait_ge(sems["AA"], aan)
                    v.tensor_copy(glc[:, 0:16], g3[:, :, 63])
                    v.drain()
                    v.tensor_tensor_scan(zbA[:, 0:16], scanms[:, 1:17], glc[:, 0:16],
                                         0.0, ALU.mult, ALU.add)
                    v.drain()
                    v.tensor_tensor(zbB[:, 0:16], glc[:, 0:16], zbA[:, 0:16], ALU.subtract)
                    v.drain()
                    inc(v.tensor_scalar(zbB[:, 0:16], zbB[:, 0:16], zbA[:, 15:16], 0.0,
                                        ALU.add, ALU.add), "VV")
                V(v13)
                if hh == 0:
                    ckstop("s1v13", hrow)
                vv2 = n["VV"]

                def a6(a, vv2=vv2, hh=hh):
                    a.wait_ge(sems["VV"], vv2)
                    a.activation(zbB[:, 0:16], zbB[:, 0:16], AF.Exp, scale=1.0 / 16.0)
                    inc(a.activation(ctile[:, 1024 + hh:1025 + hh], zbA[:, 15:16], AF.Exp,
                                     scale=1.0 / 16.0), "AA")
                A(a6)
                aan2 = n["AA"]

                def v14(v, hh=hh, aan2=aan2):
                    v.wait_ge(sems["AA"], aan2)
                    ez3 = zbB[:, 0:16].unsqueeze(2).broadcast_to([128, 16, 64])
                    egi3 = EGi[:, :].rearrange("p (c t) -> p c t", t=64)
                    h3 = hrow[:, :].rearrange("p (c t) -> p c t", t=64)
                    v.tensor_tensor(h3, egi3, ez3, ALU.mult)
                    v.drain()
                    inc(v.tensor_tensor(kwb[:, :], kf[:, hh * SEG:(hh + 1) * SEG],
                                        hrow[:, :], ALU.mult), "VV")
                V(v14)
                if hh == 0:
                    ckstop("s1v14", hrow)
                kwn = n["VV"]

                T(lambda t, kwn=kwn: t.wait_ge(sems["VV"], kwn))
                vkw = tr_pipe(kwb, lambda v, j, ps: v.tensor_copy(
                    kwt[:, j * 128:(j + 1) * 128], ps), 8)
                if hh == 0:
                    ckstop("s1tr", hrow)

                def t11(t, hh=hh, vkw=vkw):
                    t.wait_ge(sems["VV"], vkw)
                    if hh >= 1:
                        t.wait_ge(sems["VV"], ds_vv[hh - 1])
                    for b2 in range(8):
                        lhsT = kwt[:, b2 * 128:(b2 + 1) * 128]
                        rhs = vtm.rearrange("p (h b t) -> p h b t", h=GH, t=128)[
                            :, hh, b2, :]
                        mm = t.matmul(bk(4)[:, 0:128], lhsT, rhs,
                                      start=(b2 == 0), stop=(b2 == 7))
                    inc(mm, "TT")
                T(t11)
                ttds = n["TT"]

                def v16(v, hh=hh, ttds=ttds):
                    v.wait_ge(sems["TT"], ttds)
                    ds_vv[hh] = inc(v.tensor_copy(ctile[:, hh * 128:(hh + 1) * 128],
                                                  bk(4)[:, 0:128]), "VV")
                V(v16)
                if hh == 0:
                    ckstop("s1mm", hrow)
            bar()

        def gather(grp):
            def g4(g, grp=grp):
                inc(g.dma_start(out=cc2i[grp].ap(), in_=ctile[:, :]), "CN", 16)
                g.wait_ge(sems["CN"], n["CN"])
                inc(g.collective_compute(
                    "AllGather", ALU.bypass, replica_groups=[list(range(8))],
                    ins=[cc2i[grp].ap().opt()], outs=[cc2o[grp].ap().opt()]), "CC")
            G(g4)
            ccn = n["CC"]

            V(lambda v: (v.memset(Sst[:, :], 0.0), v.drain(), inc(v.nop(), "VV")))
            ranks = cc2o[grp].ap().rearrange("(r p) n -> r p n", p=128)
            p4vv = {}
            for r in range(8):
                def syr(sy, r=r, ccn=ccn):
                    sy.wait_ge(sems["CC"], ccn)
                    if r >= 1:
                        sy.wait_ge(sems["VV"], p4vv[r - 1])
                    inc(sy.dma_start(out=ctile[:, :], in_=ranks[r]), "RNK", 16)
                SY(syr)
                xln = n["RNK"]

                def v17(v, r=r, xln=xln):
                    v.wait_ge(sems["RNK"], xln)
                    m = smasks[:, r:r + 1]
                    v.tensor_scalar(zbB[:, 0:1], m, -1.0, 1.0, ALU.mult, ALU.add)
                    v.drain()
                    v.tensor_scalar(zbA[:, 0:8], ctile[:, 1024:1032], m, zbB[:, 0:1],
                                    ALU.mult, ALU.add)
                    v.drain()
                    a3_ = zbA[:, 0:8].unsqueeze(2).broadcast_to([128, 8, 128])
                    s3_ = Sst[:, :].rearrange("p (h d) -> p h d", d=128)
                    v.tensor_tensor(s3_, s3_, a3_, ALU.mult)
                    v.drain()
                    v.scalar_tensor_tensor(Sst[:, :], ctile[:, 0:1024], m, Sst[:, :],
                                           ALU.mult, ALU.add)
                    v.drain()
                    p4vv[r] = inc(v.nop(), "VV")
                V(v17)
            bar()

        def sweep2(grp):
            for hh in range(GH):
                hg = grp * GH + hh
                Gh = Gf[:, hh * SEG:(hh + 1) * SEG]
                qh = qf[:, hh * SEG:(hh + 1) * SEG]
                kh = kf[:, hh * SEG:(hh + 1) * SEG]
                sgh = sgtm[:, hh * SEG:(hh + 1) * SEG]

                def a7(a, Gh=Gh):
                    a.wait_ge(sems["VV"], n["VV"])
                    a.wait_ge(sems["GG"], n["GG"])
                    a.activation(EG[:, :], Gh, AF.Exp, scale=1.0 / 16.0)
                    inc(a.activation(EGi[:, :], Gh, AF.Exp, scale=-1.0 / 16.0), "AA")
                A(a7)
                aan = n["AA"]
                eg3 = EG[:, :].rearrange("p (c t) -> p c t", t=64)
                egi3 = EGi[:, :].rearrange("p (c t) -> p c t", t=64)

                def gp(g, aan=aan):
                    g.wait_ge(sems["AA"], aan)
                    egb = eg3[:, :, 0].unsqueeze(2).broadcast_to([128, 16, 64])
                    egbi = egi3[:, :, 0].unsqueeze(2).broadcast_to([128, 16, 64])
                    h3 = hrow[:, :].rearrange("p (c t) -> p c t", t=64)
                    h23 = hrow2[:, :].rearrange("p (c t) -> p c t", t=64)
                    g.tensor_tensor(h3, eg3, egbi, ALU.mult)
                    inc(g.tensor_tensor(h23, egi3, egb, ALU.mult), "GG")
                G(gp)
                ggn = n["GG"]

                def v18(v, qh=qh, kh=kh, ggn=ggn):
                    v.wait_ge(sems["GG"], ggn)
                    v.tensor_tensor(qhb[:, :], qh, hrow[:, :], ALU.mult)
                    inc(v.tensor_tensor(khb[:, :], kh, hrow2[:, :], ALU.mult), "VV")
                V(v18)
                qkn = n["VV"]

                def gp2(g, qkn=qkn):
                    g.wait_ge(sems["VV"], qkn)
                    egl = eg3[:, :, 63].unsqueeze(2).broadcast_to([128, 16, 64])
                    h3 = hrow[:, :].rearrange("p (c t) -> p c t", t=64)
                    inc(g.tensor_tensor(h3, egi3, egl, ALU.mult), "GG")
                G(gp2)
                gg2 = n["GG"]

                def v19(v, kh=kh, gg2=gg2, hh=hh):
                    v.wait_ge(sems["GG"], gg2)
                    v.tensor_tensor(kwb[:, :], kh, hrow[:, :], ALU.mult)
                    inc(v.tensor_copy(Scur[:, :], Sst[:, hh * HD:(hh + 1) * HD]), "VV")
                V(v19)
                kwn = n["VV"]

                T(lambda t, kwn=kwn: t.wait_ge(sems["VV"], kwn))
                sready = tr_pipe(kwb, lambda v, j, ps: v.tensor_copy(
                    kwt[:, j * 128:(j + 1) * 128], ps), 8)

                def v20i(v):
                    v.drain()
                    v.tensor_scalar(s2buf[:, :], Scur[:, :], eg3[:, 0, 63].unsqueeze(1),
                                    0.0, ALU.mult, ALU.add)
                    inc(v.tensor_scalar(sgbuf[:, :], Scur[:, :],
                                        eg3[:, 0, 0].unsqueeze(1), 0.0,
                                        ALU.mult, ALU.add), "VV")
                V(v20i)
                sready = n["VV"]

                ch_aa, ch_tt, ch_vv = {}, {}, {}
                for c in range(16):
                    p0 = (c % 2) * 64
                    b2 = c // 2
                    vslice = vtm.rearrange("p (h b t) -> p h b t", h=GH, t=128)[
                        p0:p0 + 64, hh, b2, :]
                    kwts = kwt[p0:p0 + 64, b2 * 128:(b2 + 1) * 128]

                    def t13(t, c=c, sready=sready):
                        if c == 0:
                            t.wait_ge(sems["VV"], sready)
                        else:
                            t.wait_ge(sems["VV"], ch_vv[(c - 1, "atm")])
                        mm = t.matmul(bk(5)[0:64, 0:64], khb[:, c * 64:(c + 1) * 64],
                                      qhb[:, c * 64:(c + 1) * 64], start=True, stop=True)
                        ch_tt[(c, "at")] = inc(mm, "TT")
                    T(t13)

                    def v21(v, c=c, p0=p0):
                        v.wait_ge(sems["TT"], ch_tt[(c, "at")])
                        ch_vv[(c, "atm")] = inc(
                            v.tensor_tensor(atb[p0:p0 + 64, :], bk(5)[0:64, 0:64],
                                            umasks[:, :], ALU.mult), "VV")
                    V(v21)

                    def t14(t, c=c, vslice=vslice, kwts=kwts, p0=p0):
                        t.wait_ge(sems["VV"], ch_vv[(c, "atm")])
                        t.matmul(bk(6)[0:64, 0:128], qhb[:, c * 64:(c + 1) * 64],
                                 sgbuf[:, :], start=True, stop=True)
                        mm2 = t.matmul(bk(3)[0:64, 0:128], atb[p0:p0 + 64, :], vslice,
                                       start=True, stop=True)
                        ch_tt[(c, "o")] = inc(mm2, "TT")
                        mm3 = t.matmul(bk(2)[:, 0:128], kwts, vslice, start=True, stop=True)
                        ch_tt[(c, "sd")] = inc(mm3, "TT")
                    T(t14)

                    def v22(v, c=c, b2=b2):
                        v.wait_ge(sems["TT"], ch_tt[(c, "sd")])
                        dst = otm[(c % 2) * 64:(c % 2) * 64 + 64, b2 * 128:(b2 + 1) * 128]
                        v.tensor_copy(dst, bk(3)[0:64, 0:128])
                        if c < 15:
                            v.tensor_tensor(Scur[:, :], s2buf[:, :], bk(2)[:, 0:128],
                                            ALU.add)
                        v.drain()
                        if c < 15:
                            v.tensor_tensor(dst, dst, bk(6)[0:64, 0:128], ALU.add)
                            v.tensor_scalar(s2buf[:, :], Scur[:, :],
                                            eg3[:, c + 1, 63].unsqueeze(1), 0.0,
                                            ALU.mult, ALU.add)
                            ch_vv[(c, "oev")] = inc(
                                v.tensor_scalar(sgbuf[:, :], Scur[:, :],
                                                eg3[:, c + 1, 0].unsqueeze(1), 0.0,
                                                ALU.mult, ALU.add), "VV")
                        else:
                            ch_vv[(c, "oev")] = inc(
                                v.tensor_tensor(dst, dst, bk(6)[0:64, 0:128],
                                                ALU.add), "VV")
                    V(v22)

                def a9(a):
                    for b2 in range(8):
                        a.wait_ge(sems["VV"], ch_vv[(b2 * 2 + 1, "oev")])
                        a.activation(qhb[:, b2 * 128:(b2 + 1) * 128],
                                     otm[:, b2 * 128:(b2 + 1) * 128],
                                     AF.Square, accum_out=ssq[:, b2:b2 + 1])
                    a.drain()
                    a.activation(zbA[:, 0:8], ssq[:, 0:8], AF.Ln, scale=1.0 / 128.0,
                                 bias=epsc[:, 0:1])
                    a.drain()
                    a.activation(zbA[:, 0:8], zbA[:, 0:8], AF.Exp, scale=-0.5)
                    a.drain()
                    inc(a.nop(), "AA")
                A(a9)
                arst = n["AA"]

                def v23(v, sgh=sgh, arst=arst):
                    v.wait_ge(sems["AA"], arst)
                    v.wait_ge(sems["TT"], state["ttog_prev"])
                    for b2 in range(8):
                        v.tensor_scalar(hrow[:, b2 * 128:(b2 + 1) * 128],
                                        otm[:, b2 * 128:(b2 + 1) * 128],
                                        zbA[:, b2:b2 + 1], 0.0, ALU.mult, ALU.add)
                    v.drain()
                    inc(v.tensor_tensor(ogstage[:, :], hrow[:, :], sgh, ALU.mult), "VV")
                V(v23)
                ogn = n["VV"]

                T(lambda t, ogn=ogn: t.wait_ge(sems["VV"], ogn))
                tr_pipe(ogstage, lambda v, j, ps, hg=hg: v.tensor_copy(
                    ogf[:, hg * SEG + j * 128:hg * SEG + (j + 1) * 128], ps), 8)
                state["ttog_prev"] = n["TT"]
            bar()

        for grp in range(2):
            proj(grp)
            ckstop(f"proj{grp}", hrow)
            sweep1(grp)
            ckstop(f"sw1_{grp}", hrow)
            gather(grp)
            ckstop(f"gth{grp}", hrow)
            sweep2(grp)
            ckstop(f"sw2_{grp}", hrow)
        espA.close()

        esp6 = ExitStack()
        sb6 = lambda n_, s_, dt: esp6.enter_context(nc.sbuf_tensor(n_, s_, dt))
        cpB = sb6("cpB", [128, 5 * D], BF16)
        xt6 = sb6("xt6", [128, 2 * D], F32)
        lnb = sb6("lnb6", [128, D], F32)
        lnb2 = sb6("lnb26", [128, D], F32)
        sqj = [sb6(f"sqj6{i}", [128, D], F32) for i in range(2)]
        hstage = sb6("hstage6", [128, D], BF16)
        hrow6 = sb6("hrow6", [128, D], F32)
        mrow6 = sb6("mrow6", [1, D], F32)
        qstage = sb6("qstage", [128, D], F32)
        grow = sb6("grow", [1, D], BF16)

        for sl, off in [(0, 2 * D), (1, 3 * D), (2, 4 * D), (3, 5 * D)]:
            def sy3(sy, sl=sl, off=off):
                if sl == 0:
                    sy.wait_ge(sems["SP"], sp_modscr)
                sy.wait_ge(sems["VV"], n["VV"])
                inc(sy.dma_start(out=mrow6[0:1, :], in_=modscr.ap()[0:1, off:off + D]),
                    "CS", 16)
            SY(sy3)
            xr = n["CS"]

            def t16(t, xr=xr):
                t.wait_ge(sems["CS"], xr)
                t.wait_ge(sems["VV"], n["VV"])
                for nn in range(4):
                    mm = t.matmul(bk(nn)[:, 0:512], onesr[0:1, :],
                                  mrow6[0:1, nn * 512:(nn + 1) * 512], start=True, stop=True)
                inc(mm, "TT")
            T(t16)
            ttb = n["TT"]

            def v25(v, sl=sl, ttb=ttb):
                v.wait_ge(sems["TT"], ttb)
                for nn in range(4):
                    v.tensor_copy(cpB[:, sl * D + nn * 512:sl * D + (nn + 1) * 512],
                                  bk(nn)[:, 0:512])
                v.drain()
                inc(v.nop(), "VV")
            V(v25)

        def sy4(sy):
            sy.wait_ge(sems["VV"], n["VV"])
            inc(sy.dma_start(out=mrow6[0:1, :], in_=nw_i), "CS", 16)
        SY(sy4)
        xr2 = n["CS"]

        def t17(t, xr2=xr2):
            t.wait_ge(sems["CS"], xr2)
            t.wait_ge(sems["VV"], n["VV"])
            for nn in range(4):
                mm = t.matmul(bk(nn)[:, 0:512], onesr[0:1, :],
                              mrow6[0:1, nn * 512:(nn + 1) * 512], start=True, stop=True)
            inc(mm, "TT")
        T(t17)
        ttnw = n["TT"]

        def v26(v, ttnw=ttnw):
            v.wait_ge(sems["TT"], ttnw)
            for nn in range(4):
                v.tensor_copy(cpB[:, 4 * D + nn * 512:4 * D + (nn + 1) * 512], bk(nn)[:, 0:512])
            v.drain()
            inc(v.nop(), "VV")
        V(v26)

        def g5(g):
            g.wait_ge(sems["VV"], n["VV"])
            inc(g.dma_start(out=grow[0:1, :], in_=modscr.ap()[0:1, 2 * D:3 * D]), "CN", 16)
        G(g5)
        wlga = n["CN"]

        def t18(t, wlga=wlga):
            t.wait_ge(sems["CN"], wlga)
            t.wait_ge(sems["VV"], n["VV"])
            for c in range(16):
                tr = t.transpose(bkb(5)[:, 2 * c:2 * c + 1], grow[0:1, c * 128:(c + 1) * 128],
                                 ident[0:1, 0:1])
            inc(tr, "TT")
        T(t18)
        ttga = n["TT"]

        def v27(v, ttga=ttga):
            v.wait_ge(sems["TT"], ttga)
            src = bkb(5)[:, 0:32].rearrange("p (c u) -> p c u", u=2)[:, :, 0]
            v.tensor_copy(gacol[:, 0:16], src)
            v.drain()
            inc(v.nop(), "VV")
        V(v27)
        ga_done = n["VV"]
        bar()

        afm = actq
        wo_tt = {}
        for s8 in range(8):
            slot = wring[:, (s8 % 2) * 4096:(s8 % 2) * 4096 + 4096]

            spn = f"WL{s8 % 2}"

            def g6(g, s8=s8, slot=slot, spn=spn):
                if s8 >= 2:
                    g.wait_ge(sems["TT"], wo_tt[s8 - 2])
                src = wo_i.rearrange("(c p) j -> p c j", p=128)[:, :, s8 * 256:(s8 + 1) * 256]
                dst = slot.rearrange("p (c j) -> p c j", j=256)
                for q4 in range(4):
                    inc(g.dma_start(out=dst[:, q4 * 4:(q4 + 1) * 4, :],
                                    in_=src[:, q4 * 4:(q4 + 1) * 4, :]), spn, 16)
            G(g6)
            wln = (spn, n[spn])

            def t19(t, slot=slot, wln=wln):
                t.wait_ge(sems[wln[0]], wln[1])
                t.wait_ge(sems["VV"], ga_done)
                for u in range(4):
                    m2, th = u // 2, u % 2
                    if u in pb_free:
                        t.wait_ge(sems["VV"], pb_free[u])
                    for kc in range(16):
                        mm = t.matmul(bk(u)[:, :],
                                      slot[:, kc * 256 + m2 * 128:kc * 256 + m2 * 128 + 128],
                                      ogf[:, kc * SEG + th * 512:kc * SEG + th * 512 + 512],
                                      start=(kc == 0), stop=(kc == 15))
                    inc(mm, "TT")
            T(t19)
            ttn = n["TT"]
            wo_tt[s8] = ttn

            def v28(v, s8=s8, ttn=ttn):
                for u in range(4):
                    m2, th = u // 2, u % 2
                    v.wait_ge(sems["TT"], ttn - 4 + u + 1)
                    oc = s8 * 2 + m2
                    cp = v.tensor_scalar(afm[:, oc * SEG + th * 512:oc * SEG + th * 512 + 512],
                                         bk(u)[:, :], gacol[:, oc:oc + 1], 0.0,
                                         ALU.mult, ALU.add)
                    pb_free[u] = inc(cp, "VV")
            V(v28)
        bar()
        ckstop("wo", lnb)
        if DEBUG:
            def dmp2(sy):
                inc(sy.dma_start(out=dbg_og.ap(), in_=ogf[:, :]), "ST", 16)
                inc(sy.dma_start(out=dbg_afm.ap(), in_=afm[:, :]), "ST", 16)
                sy.wait_ge(sems["ST"], n["ST"])
            SY(dmp2)
            bar()

        sb_t = cpB[:, 1 * D:2 * D]
        cb1_t = cpB[:, 2 * D:3 * D]
        gb_t = cpB[:, 3 * D:4 * D]
        nw_t = cpB[:, 4 * D:5 * D]
        p6_done, p6_var, p6_sp = {}, {}, {}
        for ti in range(NT):
            xv = xt6[:, (ti % 2) * D:(ti % 2) * D + D]

            spn = f"XL{ti % 2}"

            def sy5(sy, ti=ti, xv=xv, spn=spn):
                if ti >= 2:
                    sy.wait_ge(sems["VV"], p6_done[ti - 2])
                inc(sy.dma_start(out=xv, in_=x3[ti]), spn, 16)
            SY(sy5)
            xln = (spn, n[spn])

            vv0 = n["VV"]
            tt0b = n["TT"]

            def t20(t, ti=ti, vv0=vv0):
                for j in range(16):
                    if j >= 2:
                        t.wait_ge(sems["VV"], vv0 + (j - 2) + 1)
                    else:
                        bnk = 6 + j % 2
                        if bnk in pb_free:
                            t.wait_ge(sems["VV"], pb_free[bnk])
                    tr = t.transpose(bkb(6 + j % 2)[:, 0:128],
                                     afm[:, j * SEG + ti * 128:j * SEG + (ti + 1) * 128],
                                     ident[:, :])
                    inc(tr, "TT")
            T(t20)

            def v29(v, ti=ti, xln=xln, tt0b=tt0b):
                v.wait_ge(sems[xln[0]], xln[1])
                if ti >= 1:
                    v.wait_ge(sems["SP"], p6_sp[ti - 1])
                for j in range(16):
                    v.wait_ge(sems["TT"], tt0b + j + 1)
                    xs_ = xt6[:, (ti % 2) * D + j * 128:(ti % 2) * D + (j + 1) * 128]
                    cnt = inc(v.tensor_tensor(lnb[:, j * 128:(j + 1) * 128],
                                              bkb(6 + j % 2)[:, 0:128], xs_, ALU.add), "VV")
                    pb_free[6 + j % 2] = cnt
                v.drain()
                inc(v.nop(), "VV")
            V(v29)
            x2n = n["VV"]

            def a10(a, ti=ti, x2n=x2n):
                a.wait_ge(sems["VV"], x2n)
                if ti >= 1:
                    a.wait_ge(sems["VV"], p6_var[ti - 1])
                a.activation(sqj[ti % 2][:, :], lnb[:, :], AF.Square, accum_out=colb[:, 2:3])
                a.drain()
                inc(a.nop(), "AA")
            A(a10)
            aan = n["AA"]

            def v30(v, x2n=x2n, aan=aan, ti=ti):
                v.wait_ge(sems["VV"], x2n)
                v.tensor_reduce(colb[:, 0:1], lnb[:, :], AX.X, ALU.add)
                v.drain()
                v.tensor_scalar(colb[:, 1:2], colb[:, 0:1], 1.0 / D, 0.0, ALU.mult, ALU.add)
                v.drain()
                v.tensor_tensor(colb[:, 4:5], colb[:, 1:2], colb[:, 1:2], ALU.mult)
                v.wait_ge(sems["AA"], aan)
                v.drain()
                v.scalar_tensor_tensor(colb[:, 5:6], colb[:, 2:3], 1.0 / D, colb[:, 4:5],
                                       ALU.mult, ALU.subtract)
                v.drain()
                p6_var[ti] = inc(v.nop(), "VV")
            V(v30)
            vvn = n["VV"]

            def a11(a, vvn=vvn):
                a.wait_ge(sems["VV"], vvn)
                a.activation(colb[:, 6:7], colb[:, 5:6], AF.Ln, bias=epsc[:, 0:1])
                a.drain()
                a.activation(colb[:, 7:8], colb[:, 6:7], AF.Exp, scale=-0.5)
                a.drain()
                inc(a.nop(), "AA")
            A(a11)
            aan2 = n["AA"]

            def v31(v, aan2=aan2):
                v.wait_ge(sems["AA"], aan2)
                v.scalar_tensor_tensor(colb[:, 8:9], colb[:, 1:2], -1.0, colb[:, 7:8],
                                       ALU.mult, ALU.mult)
                v.drain()
                v.tensor_scalar(lnb2[:, :], lnb[:, :], colb[:, 7:8], colb[:, 8:9],
                                ALU.mult, ALU.add)
                v.drain()
                v.tensor_tensor(lnb2[:, :], lnb2[:, :], cb1_t, ALU.mult)
                v.drain()
                v.tensor_tensor(lnb2[:, :], lnb2[:, :], sb_t, ALU.add)
                v.drain()
                inc(v.nop(), "VV")
            V(v31)
            h2n = n["VV"]

            def g7(g, h2n=h2n, ti=ti):
                g.wait_ge(sems["VV"], h2n)
                inc(g.tensor_tensor(sqj[ti % 2][:, :], lnb2[:, :], gb_t, ALU.mult), "GG")
            G(g7)
            ggn = n["GG"]

            def v32(v, ti=ti, ggn=ggn):
                v.wait_ge(sems["GG"], ggn)
                inc(v.tensor_tensor(lnb[:, :], lnb[:, :], sqj[ti % 2][:, :], ALU.add), "VV")
            V(v32)
            x2pn = n["VV"]

            def a12(a, ti=ti, x2pn=x2pn):
                a.wait_ge(sems["VV"], x2pn)
                p6_sp[ti] = inc(a.dma_start(out=x2s3[ti], in_=lnb[:, :]), "SP", 16)
                a.activation(sqj[ti % 2][:, :], lnb2[:, :], AF.Square,
                             accum_out=colb[:, 10:11])
                a.drain()
                inc(a.nop(), "AA")
            A(a12)
            aan3 = n["AA"]

            def v33(v, aan3=aan3):
                v.tensor_tensor(hrow6[:, :], lnb2[:, :], nw_t, ALU.mult)
                v.drain()
                v.tensor_reduce(colb[:, 12:13], hrow6[:, :], AX.X, ALU.max,
                                apply_absolute_value=True)
                v.wait_ge(sems["AA"], aan3)
                v.drain()
                inc(v.nop(), "VV")
            V(v33)
            vvz = n["VV"]

            def a13(a, vvz=vvz):
                a.wait_ge(sems["VV"], vvz)
                a.activation(colb[:, 13:14], colb[:, 10:11], AF.Ln, scale=1.0 / D,
                             bias=epsc[:, 0:1])
                a.drain()
                a.activation(colb[:, 13:14], colb[:, 13:14], AF.Exp, scale=-0.5)
                a.drain()
                inc(a.nop(), "AA")
            A(a13)
            aan4 = n["AA"]

            def v34(v, ti=ti, aan4=aan4):
                v.wait_ge(sems["AA"], aan4)
                v.tensor_tensor(colb[:, 14:15], colb[:, 12:13], colb[:, 13:14], ALU.mult)
                v.drain()
                v.tensor_scalar(colb[:, 14:15], colb[:, 14:15], 1e-5, 0.0, ALU.max, ALU.add)
                v.drain()
                v.reciprocal(colb[:, 15:16], colb[:, 14:15])
                v.drain()
                v.tensor_scalar(colb[:, 16:17], colb[:, 15:16], 127.0, 0.0, ALU.mult, ALU.add)
                v.tensor_scalar(qcols[:, ti:ti + 1], colb[:, 14:15], 1.0 / 127.0, 0.0,
                                ALU.mult, ALU.add)
                v.drain()
                v.tensor_tensor(colb[:, 17:18], colb[:, 13:14], colb[:, 16:17], ALU.mult)
                v.drain()
                v.tensor_scalar(qstage[:, :], hrow6[:, :], colb[:, 17:18], RBF,
                                ALU.mult, ALU.add)
                v.drain()
                inc(v.nop(), "VV")
            V(v34)
            vq = n["VV"]

            def g8(g, vq=vq):
                g.wait_ge(sems["VV"], vq)
                inc(g.tensor_scalar(qstage[:, :], qstage[:, :], RBF + 127.0, RBF - 127.0,
                                    ALU.min, ALU.max), "GG")
            G(g8)
            gq = n["GG"]

            def v35(v, gq=gq):
                v.wait_ge(sems["GG"], gq)
                inc(v.tensor_scalar(hstage[:, :], qstage[:, :], RBF, 0.0, ALU.subtract,
                                    ALU.add), "VV")
            V(v35)
            hqn = n["VV"]

            T(lambda t, hqn=hqn: t.wait_ge(sems["VV"], hqn))
            p6_done[ti] = tr_pipe(
                hstage,
                lambda v, j, ps, ti=ti: v.tensor_copy(
                    actq[:, j * SEG + ti * 128:j * SEG + (ti + 1) * 128], ps), 16)
        bar()
        ckstop("P6", lnb)
        esp6.close()
        espW.close()

        esp7 = ExitStack()
        sb7 = lambda n_, s_, dt: esp7.enter_context(nc.sbuf_tensor(n_, s_, dt))
        wffn = sb7("wffn", [128, 16384], BF16)
        acc = sb7("acc", [128, 16 * SEG], F32)
        midst = sb7("midst", [128, 1024], BF16)
        qmidt = sb7("qmidt", [128, 1024], BF16)
        qmidf = sb7("qmidf", [128, 1024], BF16)
        qmstage = sb7("qmstage", [128, 1024], F32)
        sgst = sb7("sgst", [128, 256], BF16)
        valst = sb7("valst", [128, 256], F32)
        xf = sb7("xf", [128, D], F32)
        mrow7 = sb7("mrow7", [1, D], F32)
        gbt = sb7("gbt", [128, D], BF16)

        def sy9(sy):
            inc(sy.dma_start(out=mrow7[0:1, :], in_=modscr.ap()[0:1, 5 * D:6 * D]),
                "CS", 16)
        SY(sy9)
        xgb = n["CS"]

        def t26(t, xgb=xgb):
            t.wait_ge(sems["CS"], xgb)
            for nn in range(4):
                mm = t.matmul(bk(4 + nn)[:, 0:512], onesr[0:1, :],
                              mrow7[0:1, nn * 512:(nn + 1) * 512], start=True, stop=True)
                inc(mm, "TT")
        T(t26)
        ttgb = n["TT"]

        def vgb(v, ttgb=ttgb):
            for nn in range(4):
                v.wait_ge(sems["TT"], ttgb - 4 + nn + 1)
                v.tensor_copy(gbt[:, nn * 512:(nn + 1) * 512], bk(4 + nn)[:, 0:512])
            v.drain()
            inc(v.nop(), "VV")
        V(vgb)

        def v37(v):
            for ti in range(NT):
                v.tensor_scalar(svcol[:, ti:ti + 1], qcols[:, ti:ti + 1], scw[:, 0:1],
                                0.0, ALU.mult, ALU.add)
            v.memset(acc[:, :], 0.0)
            v.drain()
            inc(v.nop(), "VV")
        V(v37)
        bar()

        gv_tt = {}
        last_sp = {}
        for B in range(32):
            wslot = wffn[:, (B % 2) * 8192:(B % 2) * 8192 + 8192]
            ws3 = wslot.rearrange("p (c j) -> p c j", j=512)

            spn = f"WF{B % 2}"

            def g9(g, B=B, ws3=ws3, spn=spn):
                if B >= 2:
                    g.wait_ge(sems["TT"], gv_tt[B - 2])
                srcv = wgvT_i.rearrange("(c p) j -> p c j", p=128)[:, :, B * 256:(B + 1) * 256]
                srcg = wgvT_i.rearrange("(c p) j -> p c j", p=128)[
                    :, :, HID + B * 256:HID + (B + 1) * 256]
                for q4 in range(4):
                    sl = slice(q4 * 4, (q4 + 1) * 4)
                    inc(g.dma_start(out=ws3[:, sl, 0:256], in_=srcv[:, sl, :]), spn, 16)
                    inc(g.dma_start(out=ws3[:, sl, 256:512], in_=srcg[:, sl, :]), spn, 16)
            G(g9)
            wln = (spn, n[spn])

            def wq1(v, ws3=ws3, wln=wln):
                v.wait_ge(sems[wln[0]], wln[1])
                inc(v.tensor_scalar(ws3[:, :, :], ws3[:, :, :], scw[:, 2:3], RB,
                                    ALU.mult, ALU.add), "VV")
            V(wq1)
            q1n = n["VV"]

            def wq2(g, ws3=ws3, q1n=q1n):
                g.wait_ge(sems["VV"], q1n)
                inc(g.tensor_scalar(ws3[:, :, :], ws3[:, :, :], RB + 1.0, RB - 1.0,
                                    ALU.min, ALU.max), "GG")
            G(wq2)
            q2n = n["GG"]

            def wq3(v, ws3=ws3, q2n=q2n):
                v.wait_ge(sems["GG"], q2n)
                inc(v.tensor_scalar(ws3[:, :, :], ws3[:, :, :], RB, 0.0, ALU.subtract,
                                    ALU.add), "VV")
            V(wq3)
            wqn = n["VV"]

            for ti in range(NT):
                bv, bg = ti % 2, 2 + ti % 2

                def t22(t, ti=ti, bv=bv, bg=bg, ws3=ws3, wln=wln, wqn=wqn):
                    t.wait_ge(sems["VV"], wqn)
                    for bnk in (bv, bg):
                        if bnk in pb_free:
                            t.wait_ge(sems["VV"], pb_free[bnk])
                        if (bnk, "a") in pb_free:
                            t.wait_ge(sems["AA"], pb_free[(bnk, "a")])
                    for kc in range(16):
                        t.matmul(bk(bv)[:, 0:256],
                                 actq[:, kc * SEG + ti * 128:kc * SEG + (ti + 1) * 128],
                                 ws3[:, kc, 0:256], start=(kc == 0), stop=(kc == 15))
                    for kc in range(16):
                        mm = t.matmul(bk(bg)[:, 0:256],
                                      actq[:, kc * SEG + ti * 128:kc * SEG + (ti + 1) * 128],
                                      ws3[:, kc, 256:512], start=(kc == 0), stop=(kc == 15))
                    inc(mm, "TT")
                T(t22)
                ttn = n["TT"]
                gv_tt[B] = ttn

                def a14(a, ti=ti, bv=bv, bg=bg, ttn=ttn):
                    a.wait_ge(sems["TT"], ttn)
                    cnt = inc(a.activation(sgst[:, :], bk(bg)[:, 0:256], AF.Silu,
                                           scale=svcol[:, ti:ti + 1]), "AA")
                    pb_free[(bg, "a")] = cnt
                A(a14)
                aan = n["AA"]

                def v38(v, ti=ti, B=B, aan=aan, bv=bv, bg=bg):
                    v.wait_ge(sems["AA"], aan)
                    if last_sp:
                        v.wait_ge(sems["SP"], last_sp["c"])
                    v.scalar_tensor_tensor(midst[:, 0:256], bk(bv)[:, 0:256],
                                           svcol[:, ti:ti + 1], sgst[:, :],
                                           ALU.mult, ALU.mult)
                    v.drain()
                    cnt = inc(v.tensor_reduce(ambuf[:, ti * 32 + B:ti * 32 + B + 1],
                                              midst[:, 0:256], AX.X, ALU.max,
                                              apply_absolute_value=True), "VV")
                    pb_free[bv] = cnt
                    pb_free[bg] = cnt
                V(v38)
                vvm = n["VV"]

                def a15(a, ti=ti, B=B, vvm=vvm):
                    a.wait_ge(sems["VV"], vvm)
                    last_sp["c"] = inc(a.dma_start(out=mid3[ti][:, B * 256:(B + 1) * 256],
                                                   in_=midst[:, 0:256]), "SP", 16)
                A(a15)
        sp_mid = n["SP"]
        bar()
        ckstop("gv", acc)

        def v39(v):
            for ti in range(NT):
                v.tensor_reduce(colb[:, 20:21], ambuf[:, ti * 32:(ti + 1) * 32], AX.X, ALU.max)
                v.drain()
                v.tensor_scalar(colb[:, 20:21], colb[:, 20:21], 1e-5, 0.0, ALU.max, ALU.add)
                v.drain()
                v.reciprocal(colb[:, 21:22], colb[:, 20:21])
                v.drain()
                v.tensor_scalar(qcols2[:, ti:ti + 1], colb[:, 21:22], 127.0, 0.0,
                                ALU.mult, ALU.add)
                v.tensor_scalar(icols2[:, ti:ti + 1], colb[:, 20:21], 1.0 / 127.0, 0.0,
                                ALU.mult, ALU.add)
                v.drain()
                v.tensor_scalar(icols2[:, ti:ti + 1], icols2[:, ti:ti + 1], scw[:, 1:2],
                                0.0, ALU.mult, ALU.add)
                v.drain()
            inc(v.nop(), "VV")
        V(v39)
        bar()

        wo2_tt = {}
        for SB in range(8):
            ws3 = wffn[:, 0:16384].rearrange("p (c j) -> p c j", j=2048)

            def g10(g, SB=SB, ws3=ws3):
                if SB >= 1:
                    g.wait_ge(sems["TT"], wo2_tt[SB - 1])
                src = woutT_i.rearrange("(c p) j -> p c j", p=128)[:, SB * 8:(SB + 1) * 8, :]
                inc(g.dma_start(out=ws3[:, 0:4, :], in_=src[:, 0:4, :]), "WF0", 16)
                inc(g.dma_start(out=ws3[:, 4:8, :], in_=src[:, 4:8, :]), "WF0", 16)
            G(g10)
            wln = n["WF0"]

            def wq1b(v, ws3=ws3, wln=wln):
                v.wait_ge(sems["WF0"], wln)
                inc(v.tensor_scalar(ws3[:, :, :], ws3[:, :, :], scw[:, 3:4], RB,
                                    ALU.mult, ALU.add), "VV")
            V(wq1b)
            q1n = n["VV"]

            def wq2b(g, ws3=ws3, q1n=q1n):
                g.wait_ge(sems["VV"], q1n)
                inc(g.tensor_scalar(ws3[:, :, :], ws3[:, :, :], RB + 1.0, RB - 1.0,
                                    ALU.min, ALU.max), "GG")
            G(wq2b)
            q2n = n["GG"]

            def wq3b(v, ws3=ws3, q2n=q2n):
                v.wait_ge(sems["GG"], q2n)
                inc(v.tensor_scalar(ws3[:, :, :], ws3[:, :, :], RB, 0.0, ALU.subtract,
                                    ALU.add), "VV")
            V(wq3b)
            wqnb = n["VV"]

            for ti in range(NT):
                def sy6(sy, SB=SB, ti=ti):
                    if (SB, ti) == (0, 0):
                        sy.wait_ge(sems["SP"], sp_mid)
                    sy.wait_ge(sems["VV"], n["VV"])
                    inc(sy.dma_start(out=midst[:, :],
                                     in_=mid3[ti][:, SB * 1024:(SB + 1) * 1024]), "MD", 16)
                SY(sy6)
                xlm = n["MD"]

                def v40(v, ti=ti, xlm=xlm):
                    v.wait_ge(sems["MD"], xlm)
                    inc(v.tensor_scalar(qmstage[:, :], midst[:, :], qcols2[:, ti:ti + 1],
                                        RBF, ALU.mult, ALU.add), "VV")
                V(v40)
                vq1 = n["VV"]

                def g11(g, vq1=vq1):
                    g.wait_ge(sems["VV"], vq1)
                    inc(g.tensor_scalar(qmstage[:, :], qmstage[:, :], RBF + 127.0,
                                        RBF - 127.0, ALU.min, ALU.max), "GG")
                G(g11)
                gq1 = n["GG"]

                def v41(v, gq1=gq1):
                    v.wait_ge(sems["GG"], gq1)
                    inc(v.tensor_scalar(qmidt[:, :], qmstage[:, :], RBF, 0.0,
                                        ALU.subtract, ALU.add), "VV")
                V(v41)
                qn = n["VV"]

                T(lambda t, qn=qn: t.wait_ge(sems["VV"], qn))
                qfn = tr_pipe(qmidt, lambda v, j, ps: v.tensor_copy(
                    qmidf[:, j * 128:(j + 1) * 128], ps), 8)

                def t24(t, ws3=ws3, wln=wln, qfn=qfn, wqnb=wqnb):
                    t.wait_ge(sems["VV"], wqnb)
                    t.wait_ge(sems["VV"], qfn)
                    for nn in range(4):
                        if nn in pb_free:
                            t.wait_ge(sems["VV"], pb_free[nn])
                        for hc in range(8):
                            mm = t.matmul(bk(nn)[:, :], qmidf[:, hc * 128:(hc + 1) * 128],
                                          ws3[:, hc, nn * 512:(nn + 1) * 512],
                                          start=(hc == 0), stop=(hc == 7))
                        inc(mm, "TT")
                T(t24)
                ttn = n["TT"]
                wo2_tt[SB] = ttn

                def v43(v, ti=ti, ttn=ttn):
                    for nn in range(4):
                        v.wait_ge(sems["TT"], ttn - 4 + nn + 1)
                        ac = acc[:, ti * D + nn * 512:ti * D + (nn + 1) * 512]
                        pb_free[nn] = inc(v.tensor_tensor(ac, ac, bk(nn)[:, :], ALU.add), "VV")
                V(v43)
        bar()

        xl_prev = {}
        for ti in range(NT):
            at = acc[:, ti * D:(ti + 1) * D]

            def sy7(sy, ti=ti):
                if ti >= 1:
                    sy.wait_ge(sems["VV"], xl_prev[ti - 1])
                inc(sy.dma_start(out=xf[:, :], in_=x2s3[ti]), "XL0", 16)
            SY(sy7)
            xln = n["XL0"]

            def v44(v, ti=ti, at=at):
                inc(v.tensor_scalar(at, at, icols2[:, ti:ti + 1], 0.0, ALU.mult,
                                    ALU.add), "VV")
            V(v44)
            vo1 = n["VV"]

            def g12(g, vo1=vo1, at=at):
                g.wait_ge(sems["VV"], vo1)
                inc(g.tensor_tensor(at, at, gbt[:, :], ALU.mult), "GG")
            G(g12)
            go2 = n["GG"]

            def v45(v, ti=ti, xln=xln, go2=go2, at=at):
                v.wait_ge(sems["XL0"], xln)
                v.wait_ge(sems["GG"], go2)
                xl_prev[ti] = inc(v.tensor_tensor(at, at, xf[:, :], ALU.add), "VV")
            V(v45)
            vfin = n["VV"]

            def sy8(sy, ti=ti, vfin=vfin, at=at):
                sy.wait_ge(sems["VV"], vfin)
                inc(sy.dma_start(out=y3[ti], in_=at), "ST", 16)
            SY(sy8)
        stf = n["ST"]
        SY(lambda sy: sy.wait_ge(sems["ST"], stf))
        esp7.close()

    return nc


_NC_CACHE = {}


def kernel(L0_attn_x, L0_attn_Wq, L0_attn_Wk, L0_attn_Wv, L0_attn_Wgk1,
           L0_attn_Wgk2, L0_attn_bgk2, L0_attn_Wg, L0_attn_gnorm_w, L0_attn_Wo,
           L0_ffn_norm_w, L0_ffn_w_gv, L0_ffn_w_out, L0_c, L0_wmod):
    f32 = np.float32
    ca = lambda a: np.ascontiguousarray(np.asarray(a, f32))
    x = ca(L0_attn_x)
    wq, wk, wv, wg, wo = ca(L0_attn_Wq), ca(L0_attn_Wk), ca(L0_attn_Wv), \
        ca(L0_attn_Wg), ca(L0_attn_Wo)
    wgk1, wgk2, bgk2 = ca(L0_attn_Wgk1), ca(L0_attn_Wgk2), ca(L0_attn_bgk2)
    gnw, nw = ca(L0_attn_gnorm_w), ca(L0_ffn_norm_w)
    wgv, wout = ca(L0_ffn_w_gv), ca(L0_ffn_w_out)
    cc, wmod = ca(L0_c), ca(L0_wmod)

    wgvT = np.ascontiguousarray(wgv.T)
    woutT = np.ascontiguousarray(wout.T)
    bgk2c = np.ascontiguousarray(bgk2.reshape(16, 128).T)
    gnw2 = np.ascontiguousarray(np.tile(gnw, 2)[None, :])
    nwr = np.ascontiguousarray(nw[None, :])
    scanm = np.ones((128, SEG), f32)
    scanm[:, 0::64] = 0.0
    umask = np.ascontiguousarray(np.triu(np.ones((64, 64), f32)))
    ident = np.eye(128, dtype=f32)

    if "nc" not in _NC_CACHE:
        try:
            _NC_CACHE["nc"] = build_nc()
        except StopBuild:
            _NC_CACHE["nc"] = _LAST["nc"]
    nc = _NC_CACHE["nc"]

    in_maps = []
    for c in range(8):
        b, s = c // 4, c % 4
        smask = np.zeros((128, 8), f32)
        for j in range(8):
            if j // 4 == b and j % 4 < s:
                smask[:, j] = 1.0
        bsel = np.zeros((8, 8), f32)
        for r in range(8):
            if r // 4 == b:
                bsel[r, r % 4] = 1.0
            bsel[r, 4] = 1.0
        in_maps.append({
            "x_seg": np.ascontiguousarray(x[b, s * SEG:(s + 1) * SEG, :]),
            "cbc": np.ascontiguousarray(cc[b].reshape(16, 128).T),
            "wmod_q": np.ascontiguousarray(wmod[:, s * 3072:(s + 1) * 3072]),
            "wq": wq, "wk": wk, "wv": wv, "wg": wg, "wo": wo,
            "wgk1": wgk1, "wgk2": wgk2, "bgk2c": bgk2c, "gnw2": gnw2,
            "nw_row": nwr, "wgvT": wgvT, "woutT": woutT,
            "wgv_sh": np.ascontiguousarray(wgv[c * 2048:(c + 1) * 2048, :]),
            "wout_sh": np.ascontiguousarray(wout[c * 256:(c + 1) * 256, :]),
            "scanm": scanm, "umask": umask, "ident": ident,
            "smask": smask, "bsel": bsel,
        })
    res = run_bass_kernel_spmd(nc, in_maps, core_ids=list(range(8)))
    out = np.empty((2, 4096, D), f32)
    for c in range(8):
        b, s = c // 4, c % 4
        out[b, s * SEG:(s + 1) * SEG, :] = res.results[c]["y"]
    return out
